# version 09
# baseline (speedup 1.0000x reference)
import numpy as np
import ml_dtypes
from contextlib import ExitStack

import concourse.bass as bass
import concourse.mybir as mybir
from concourse.bass_utils import run_bass_kernel_spmd

F32 = mybir.dt.float32
BF16 = mybir.dt.bfloat16
I32 = mybir.dt.int32
AF = mybir.ActivationFunctionType
ALU = mybir.AluOpType
AX = mybir.AxisListType
BF = ml_dtypes.bfloat16

B, T, C = 4, 2048, 2048
TL = 1024
H, D, M = 16, 128, 16
ROPE, THETA, WIN, CAP = 64, 1e4, 512, 50.0
G, QC, INTER, F2 = 4, 512, 2048, 8192
NCH = C // 128
VIRT = 2560
NVT = VIRT // 128
LV0 = 1536
KV0 = 1024
KVN = VIRT - KV0
NKV = KVN // 128
NBV = VIRT // M
QT = TL // 128
NF = F2 // 128
SCALE = 1.0 / float(np.sqrt(D))
LN_EPS, RMS_EPS, QEPS = 1e-5, 1e-6, 1e-5
NEG = -1e30
TWO_PI = float(2 * np.pi)

JORDER = []
for _i in range(16):
    JORDER += [_i, 16 + _i]


def _host_weights(i):
    w = {}
    w["wqc"] = np.ascontiguousarray(i["L0_attn_Wq_c"], np.float32)
    w["wqup"] = np.ascontiguousarray(i["L0_attn_Wq_up"], np.float32)
    w["wk"] = np.ascontiguousarray(i["L0_attn_Wk"], np.float32)
    w["wv"] = np.ascontiguousarray(i["L0_attn_Wv"], np.float32)
    wc = np.asarray(i["L0_attn_W_comp"], np.float32)
    w["wcomp"] = np.ascontiguousarray(
        wc.reshape(256, 128, 128).transpose(1, 0, 2).reshape(128, -1))
    w["wg"] = np.ascontiguousarray(
        np.asarray(i["L0_attn_Wg"], np.float32).reshape(C, C))
    wo = np.asarray(i["L0_attn_Wo"], np.float32)
    w["wo"] = np.ascontiguousarray(
        wo.reshape(64, 128, 16, 128).transpose(1, 2, 0, 3).reshape(128, -1))
    wgu = np.asarray(i["L0_ffn_w_gate_up"], np.float32)
    t = wgu.reshape(32, 512, 16, 128).transpose(3, 0, 2, 1)
    w["wgu"] = np.ascontiguousarray(t[:, JORDER].reshape(128, -1))
    wd = np.asarray(i["L0_ffn_w_down"], np.float32)
    w["wd"] = np.ascontiguousarray(
        wd.reshape(16, 128, 64, 128).transpose(3, 0, 2, 1).reshape(128, -1))
    w["qnw"] = np.ascontiguousarray(
        np.asarray(i["L0_attn_q_norm_w"], np.float32).reshape(128, 1))
    w["knw"] = np.ascontiguousarray(
        np.asarray(i["L0_attn_k_norm_w"], np.float32).reshape(128, 1))
    w["sink"] = np.ascontiguousarray(
        np.asarray(i["L0_attn_sink"], np.float32).reshape(1, H))
    w["ident"] = np.eye(128, dtype=BF)
    w["identf"] = np.eye(128, dtype=np.float32)
    return w


def _host_core(x, core):
    b, half = core // 2, core % 2
    t0 = half * TL
    pr = LV0 - t0
    d = {}
    xtm = np.zeros((VIRT, C), np.float32)
    n = t0 + TL
    xtm[pr:pr + n] = x[b][0:n]
    d["x_tm"] = xtm
    d["x_fm"] = np.ascontiguousarray(xtm.T)
    d["pos"] = np.arange(t0 - 512, t0 - 512 + KVN,
                         dtype=np.float32).reshape(1, -1)
    msw = np.zeros((128, 2 * 8 * 512), BF)
    for ch in range(2):
        for e in range(8):
            s_g = (t0 - 512 + (4 * ch + e) * 128) + np.arange(128)[:, None]
            t_g = (t0 + 512 * ch) + np.arange(512)[None, :]
            ok = (s_g >= 0) & (s_g <= t_g) & (t_g - s_g < WIN)
            msw[:, (ch * 8 + e) * 512:(ch * 8 + e + 1) * 512] = ok.astype(BF)
    d["msw"] = msw
    mcm = np.empty((128, QT * NBV), np.float32)
    for j in range(QT):
        t_g = (t0 + 128 * j) + np.arange(128)[:, None]
        n = np.arange(NBV)[None, :] - (LV0 - t0) // M
        ok = (n >= 0) & (M * n + (M - 1) < t_g)
        mcm[:, j * NBV:(j + 1) * NBV] = np.where(ok, 0.0, NEG)
    d["mc"] = mcm
    return d


ENGS = ("sync", "tensor", "vector", "scalar", "gpsimd")


class Sem:
    __slots__ = ("h", "v", "name")

    def __init__(self, h, name):
        self.h, self.v, self.name = h, 0, name


class Bank:
    __slots__ = ("ps", "prod", "free")

    def __init__(self, ps):
        self.ps = ps
        self.prod = 0
        self.free = None


def _sb(es, nc, name, shape, dt=F32, side=None):
    return es.enter_context(nc.sbuf_tensor(name, shape, dt, side=side))


def build_nc(upto=9, taps=()):
    nc = bass.Bass("TRN2", target_bir_lowering=False, debug=False)
    dram = {}
    for name, shape, dt in [
        ("x_tm", [VIRT, C], F32), ("x_fm", [C, VIRT], F32),
        ("wqc", [C, QC], F32), ("wqup", [QC, C], F32),
        ("wk", [C, D], F32), ("wv", [C, D], F32),
        ("wcomp", [128, 256 * 128], F32), ("wg", [C, C], F32),
        ("wo", [128, 16 * 64 * 128], F32),
        ("wgu", [128, 32 * 16 * 512], F32), ("wd", [128, 16 * 64 * 128], F32),
        ("qnw", [128, 1], F32), ("knw", [128, 1], F32), ("sink", [1, H], F32),
        ("ident", [128, 128], BF16), ("identf", [128, 128], F32),
        ("pos", [1, KVN], F32),
        ("msw", [128, 2 * 8 * 512], BF16), ("mc", [128, QT * NBV], F32),
    ]:
        dram[name] = nc.dram_tensor(name, shape, dt, kind="ExternalInput").ap()
    out = nc.dram_tensor("out", [C, TL], F32, kind="ExternalOutput").ap()
    tap_map = {}
    for tname, tshape, tdt in taps:
        tap_map[tname] = nc.dram_tensor("tap_" + tname, tshape, tdt,
                                        kind="ExternalOutput").ap()
    scr = {
        "k1": nc.dram_tensor("k1_spill", [C, TL], BF16, kind="Internal").ap(),
        "att": nc.dram_tensor("att_spill", [C, TL], BF16, kind="Internal").ap(),
        "h": nc.dram_tensor("h_spill", [8 * 128, F2], F32, kind="Internal").ap(),
    }
    es = ExitStack()
    with es:
        steps = _program(nc, es, dram, out, scr, tap_map, upto)
        block = es.enter_context(nc.Block())
        for eng in ENGS:
            def body(e, _eng=eng):
                for fn in steps[_eng]:
                    fn(e)
            getattr(block, eng)(body)
    return nc


def _program(nc, es, dram, out, scr, tap_map, upto):
    steps = {e: [] for e in ENGS}
    bg = []

    SEMS = {}

    def mksem(name):
        sm = Sem(es.enter_context(nc.semaphore(name)), name)
        SEMS[name] = sm
        return sm

    s_ld = mksem("s_ld")
    s_x = mksem("s_x")
    s_v = mksem("s_v")
    s_a = mksem("s_a")
    s_t = mksem("s_t")
    s_gp = mksem("s_gp")
    s_pp = mksem("s_pp")
    s_pf = mksem("s_pf")
    s_sp = mksem("s_sp")
    s_out = mksem("s_out")
    SEMOF = {"vector": s_v, "scalar": s_a, "tensor": s_t, "gpsimd": s_gp}

    def add(eng, fn):
        steps[eng].append(fn)

    def wait(eng, sem, val):
        if val > 0:
            add(eng, lambda e, s=sem.h, vv=val: e.wait_ge(s, vv))

    def tok(eng, fn):
        sm = SEMOF[eng]
        sm.v += 1
        add(eng, lambda e, f=fn, s=sm.h: f(e).then_inc(s, 1))
        return sm.v

    def v(fn):
        tok("vector", fn)

    def a(fn):
        tok("scalar", fn)

    def t(fn):
        add("tensor", fn)

    def vtok(fn):
        return tok("vector", fn)

    def atok(fn):
        return tok("scalar", fn)

    def ttok(fn):
        return tok("tensor", fn)

    def vbar():
        wait("vector", s_v, s_v.v)

    def abar():
        wait("scalar", s_a, s_a.v)

    def gdma(dst, src, sem=s_ld):
        sem.v += 16
        add("gpsimd", lambda e, d=dst, s=src, h=sem.h:
            e.dma_start(out=d, in_=s).then_inc(h, 16))
        return sem.v

    def sdma(dst, src, sem=s_x):
        sem.v += 16
        add("sync", lambda e, d=dst, s=src, h=sem.h:
            e.dma_start(out=d, in_=s).then_inc(h, 16))
        return sem.v

    def edma(eng, dst, src, sem):
        sem.v += 16
        add(eng, lambda e, d=dst, s=src, h=sem.h:
            e.dma_start(out=d, in_=s).then_inc(h, 16))
        return sem.v

    def drain_bg(n=1):
        for _ in range(n):
            if bg:
                eng, fn, bump = bg.pop(0)
                add(eng, fn)
                if bump is not None:
                    bump[0].v += bump[1]

    def drain_all_bg():
        while bg:
            drain_bg(1)

    def pe_group(bank, emits):
        if bank.free is not None:
            wait("tensor", bank.free[0], bank.free[1])
            bank.free = None
        n = len(emits)
        for idx, f in enumerate(emits):
            st = idx == 0
            if idx == n - 1:
                bank.prod = ttok(lambda e, ff=f, st=st: ff(e, st, True))
            else:
                t(lambda e, ff=f, st=st: ff(e, st, False))
        return bank.prod

    def consume(bank, eng, fn):
        wait(eng, s_t, bank.prod)
        tk = tok(eng, fn)
        bank.free = (SEMOF[eng], tk)
        return tk

    def fence():
        for eng in ENGS:
            for sm in list(SEMS.values()):
                wait(eng, sm, sm.v)

    def tap(name, ap_fn, sem, tk):
        if name in tap_map:
            wait("sync", sem, tk)
            s_out.v += 16
            add("sync", lambda e, d=tap_map[name], f=ap_fn:
                e.dma_start(out=d, in_=f()).then_inc(s_out.h, 16))

    consts = es.enter_context(ExitStack())
    onesb = _sb(consts, nc, "onesb", [128, 1], BF16)
    onesf_r = _sb(consts, nc, "onesf_r", [1, 128], F32)
    onesf_c = _sb(consts, nc, "onesf_c", [128, 1], F32)
    identb = _sb(consts, nc, "identb", [128, 128], BF16)
    identf = _sb(consts, nc, "identf_s", [128, 128], F32)
    qnw = _sb(consts, nc, "qnw_in", [128, 1], F32)
    knw = _sb(consts, nc, "knw_in", [128, 1], F32)
    qnw_s = _sb(consts, nc, "qnw_s", [128, 1], F32)
    sinkrep = _sb(consts, nc, "sinkrep", [128, H], F32)
    expsink = _sb(consts, nc, "expsink", [1, H], F32)
    sx_g = _sb(consts, nc, "sx_g", [128, QT], F32)
    mxh_g = _sb(consts, nc, "mxh_g", [128, QT], F32)
    r1col = _sb(consts, nc, "r1col", [128, QT], F32)
    sx2col = _sb(consts, nc, "sx2col", [128, QT], F32)
    ppcol = _sb(consts, nc, "ppcol", [128, 192], F32)
    swv = _sb(consts, nc, "swv", [1, 8], F32)
    swcol = _sb(consts, nc, "swcol", [128, 8], F32)
    ppfin = _sb(consts, nc, "ppfin", [128, 8], F32)
    ppfin_i = _sb(consts, nc, "ppfin_i", [128, 2], I32)
    ppsums = _sb(consts, nc, "ppsums", [1, 8], F32)

    ropel = es.enter_context(ExitStack())
    cosq = _sb(ropel, nc, "cosq", [32, KVN], F32)
    sinq = _sb(ropel, nc, "sinq", [32, KVN], F32)
    posr_early = _sb(ropel, nc, "posr_e", [1, KVN], F32)
    glive = es.enter_context(ExitStack())
    xn = [_sb(glive, nc, f"xn{c}", [128, VIRT], BF16) for c in range(NCH)]

    pb = [Bank(es.enter_context(nc.psum_tensor(f"pb{k}", [128, 512], F32)))
          for k in range(7)]
    pbt = Bank(es.enter_context(nc.psum_tensor("pbt", [128, 1024], BF16)))

    s_aux = mksem("s_aux")
    gdma(identb[:], dram["ident"], sem=s_aux)
    gdma(identf[:], dram["identf"], sem=s_aux)
    gdma(qnw[:], dram["qnw"], sem=s_aux)
    gdma(knw[:], dram["knw"], sem=s_aux)
    gdma(sinkrep[0:1, 0:H], dram["sink"], sem=s_aux)
    tok_aux = gdma(posr_early[0:1, :], dram["pos"], sem=s_aux)

    wait("vector", s_aux, tok_aux)
    v(lambda e: e.memset(onesb[:], 1.0))
    v(lambda e: e.memset(onesf_r[:], 1.0))
    v(lambda e: e.memset(onesf_c[:], 1.0))
    tok_vc = vtok(lambda e: e.tensor_scalar(qnw_s[:], qnw[:], SCALE, None, ALU.mult))

    wait("tensor", s_v, tok_vc)
    pe_group(pb[6], [lambda e, st, sp: e.matmul(
        pb[6].ps[:, 0:H], onesf_r[:], sinkrep[0:1, 0:H], start=st, stop=sp)])
    tok_srep2 = consume(pb[6], "vector",
                        lambda e: e.tensor_copy(sinkrep[:], pb[6].ps[:, 0:H]))
    wait("scalar", s_v, tok_srep2)
    atok(lambda e: e.activation(expsink[:], sinkrep[0:1, 0:H], AF.Exp))

    ropes = es.enter_context(ExitStack())
    iot = _sb(ropes, nc, "iot", [32, 1], F32)
    posr = posr_early
    angt = _sb(ropes, nc, "angt", [32, KVN], F32)
    ang2 = _sb(ropes, nc, "ang2", [32, KVN], F32)
    angi = _sb(ropes, nc, "angi", [32, KVN], I32)
    angw = _sb(ropes, nc, "angw", [32, KVN], F32)
    invc = _sb(ropes, nc, "invc", [32, 1], F32)
    tok_iot = tok("gpsimd", lambda e: e.iota(
        iot[:], [[0, 1]], base=0, channel_multiplier=1,
        allow_small_or_imprecise_dtypes=True))
    wait("scalar", s_gp, tok_iot)
    LTH = float(np.log(THETA) / 32.0)
    tok_inv = atok(lambda e: e.activation(invc[:], iot[:], AF.Exp, scale=-LTH))
    wait("tensor", s_aux, tok_aux)
    wait("vector", s_a, tok_inv)
    for off in range(0, KVN, 512):
        pe_group(pb[6], [lambda e, st, sp, o=off: e.matmul(
            pb[6].ps[0:32, 0:512], onesf_r[0:1, 0:32], posr[0:1, o:o + 512],
            start=st, stop=sp)])
        consume(pb[6], "vector", lambda e, o=off: e.tensor_scalar(
            angt[:, o:o + 512], pb[6].ps[0:32, 0:512], invc[:, 0:1], None, ALU.mult))

    def range_reduce_sin(dst, shift):
        vbar()
        off = 0.25 if shift else 0.0
        v(lambda e, o=off: e.tensor_scalar(ang2[:], angt[:], 1.0 / TWO_PI, o,
                                           ALU.mult, ALU.add))
        vbar()
        v(lambda e: e.tensor_copy(angi[:], ang2[:]))
        vbar()
        v(lambda e: e.tensor_copy(angw[:], angi[:]))
        vbar()
        v(lambda e: e.tensor_tensor(ang2[:], ang2[:], angw[:], op=ALU.subtract))
        vbar()
        v(lambda e: e.tensor_scalar(angw[:], ang2[:], 0.0, 0.5,
                                    ALU.add, ALU.is_gt))
        vbar()
        v(lambda e: e.tensor_tensor(ang2[:], ang2[:], angw[:], op=ALU.subtract))
        vbar()
        v(lambda e: e.tensor_scalar(angw[:], ang2[:], -1.0, 0.5,
                                    ALU.mult, ALU.is_gt))
        vbar()
        tk = vtok(lambda e: e.tensor_tensor(ang2[:], ang2[:], angw[:],
                                            op=ALU.add))
        wait("scalar", s_v, tk)
        tk2 = atok(lambda e, d=dst: e.activation(d, ang2[:], AF.Sin,
                                                 scale=TWO_PI))
        wait("vector", s_a, tk2)

    range_reduce_sin(sinq[:], False)
    range_reduce_sin(cosq[:], True)
    ropes.close()
    fence()

    ppes = es.enter_context(ExitStack())
    s_ppv = [mksem("s_pp0"), mksem("s_pp1"), mksem("s_pp2")]
    pp_buf = [_sb(ppes, nc, f"pp{i}", [128, 2048], F32, side="right") for i in range(3)]
    pp_specs = [("wgu", k) for k in range(128)] + [("wd", k) for k in range(64)]
    pp_cnt = [0, 0, 0]
    for idx, (wname, k) in enumerate(pp_specs):
        bi = idx % 3
        if idx >= 3:
            bg.append(("gpsimd", lambda e, n=(idx - 2): e.wait_ge(s_pf.h, n),
                       None))
        pp_cnt[bi] += 16
        tk = pp_cnt[bi]
        bg.append(("gpsimd", lambda e, w=wname, kk=k, b=bi:
                   e.dma_start(out=pp_buf[b][:],
                               in_=dram[w][:, kk * 2048:(kk + 1) * 2048])
                   .then_inc(s_ppv[b].h, 16), (s_ppv[bi], 16)))
        bg.append(("vector", lambda e, tt_=tk, b=bi: e.wait_ge(s_ppv[b].h, tt_),
                   None))
        bg.append(("vector", lambda e, b=bi, cix=idx: e.reduce_sum(
            ppcol[:, cix:cix + 1], pp_buf[b][:], axis=AX.X,
            apply_absolute_value=True).then_inc(s_pf.h, 1), (s_pf, 1)))

    ph1 = es.enter_context(ExitStack())
    xt_buf = [_sb(ph1, nc, f"xt{i}", [128, C]) for i in range(2)]
    sum_g = _sb(ph1, nc, "sum_g", [128, NVT])
    sq_g = _sb(ph1, nc, "sq_g", [128, NVT])
    m_g = _sb(ph1, nc, "m_g", [128, NVT])
    nm_g = _sb(ph1, nc, "nm_g", [128, NVT])
    rs_g = _sb(ph1, nc, "rs_g", [128, NVT])
    var_g = _sb(ph1, nc, "var_g", [128, NVT])
    mx_g = _sb(ph1, nc, "mx_g", [128, QT])
    dump = _sb(ph1, nc, "dump", [128, C])
    dump_tok = [0]

    v(lambda e: e.memset(sum_g[:], 0.0))
    tok_z = vtok(lambda e: e.memset(sq_g[:], 0.0))
    wait("scalar", s_v, tok_z)
    s_xt = [mksem("s_xt0"), mksem("s_xt1")]
    buf_free = [0, 0]
    for i in range(NVT):
        bi = i % 2
        wait("sync", s_a, buf_free[bi])
        sdma(xt_buf[bi][:], dram["x_tm"][128 * i:128 * (i + 1), :], sem=s_xt[bi])
        wait("scalar", s_xt[bi], s_xt[bi].v)
        wait("scalar", s_a, dump_tok[0])
        tok_sm = atok(lambda e, b=bi, i=i: e.activation(
            dump[:], xt_buf[b][:], AF.Copy, accum_out=sum_g[:, i:i + 1]))
        dump_tok[0] = tok_sm
        wait("vector", s_a, tok_sm)
        v(lambda e, i=i: e.tensor_scalar(m_g[:, i:i + 1], sum_g[:, i:i + 1],
                                         1.0 / C, None, ALU.mult))
        vbar()
        tok_nm = vtok(lambda e, i=i: e.tensor_scalar(
            nm_g[:, i:i + 1], m_g[:, i:i + 1], -1.0, None, ALU.mult))
        wait("scalar", s_v, tok_nm)
        wait("scalar", s_a, dump_tok[0])
        tok_stats = atok(lambda e, b=bi, i=i: e.activation(
            dump[:], xt_buf[b][:], AF.Square, bias=nm_g[:, i:i + 1], scale=1.0,
            accum_out=sq_g[:, i:i + 1]))
        dump_tok[0] = tok_stats
        buf_free[bi] = tok_stats
        drain_bg(2)

    wait("vector", s_a, tok_stats)
    v(lambda e: e.tensor_scalar(var_g[:], sq_g[:], 1.0 / C, None, ALU.mult))
    vbar()
    tok_var = vtok(lambda e: e.tensor_scalar(var_g[:], var_g[:], LN_EPS,
                                             None, ALU.add))
    wait("scalar", s_v, tok_var)
    tok_sqr = atok(lambda e: e.activation(rs_g[:], var_g[:], AF.Sqrt))
    wait("vector", s_a, tok_sqr)
    tok_rs = vtok(lambda e: e.reciprocal(rs_g[:], rs_g[:]))

    wait("scalar", s_v, tok_rs)
    for j in range(QT):
        i = NVT - QT + j
        bi = j % 2
        wait("sync", s_a, buf_free[bi])
        sdma(xt_buf[bi][:], dram["x_tm"][128 * i:128 * (i + 1), :], sem=s_xt[bi])
        wait("scalar", s_xt[bi], s_xt[bi].v)
        wait("scalar", s_a, dump_tok[0])
        tok_abs = atok(lambda e, b=bi, i=i: e.activation(
            dump[:], xt_buf[b][:], AF.Abs, bias=nm_g[:, i:i + 1], scale=1.0))
        dump_tok[0] = tok_abs
        buf_free[bi] = tok_abs
        wait("vector", s_a, tok_abs)
        tok_mx = vtok(lambda e, j=j: e.reduce_max(mx_g[:, j:j + 1], dump[:],
                                                  axis=AX.X))
        wait("scalar", s_v, tok_mx)
        drain_bg(2)
    vbar()
    v(lambda e: e.tensor_tensor(sx_g[:], mx_g[:], rs_g[:, NVT - QT:NVT],
                                op=ALU.mult))
    vbar()
    v(lambda e: e.tensor_scalar(sx_g[:], sx_g[:], QEPS, None, ALU.max))
    vbar()
    v(lambda e: e.reciprocal(sx_g[:], sx_g[:]))
    vbar()
    tok_sx = vtok(lambda e: e.tensor_scalar(sx_g[:], sx_g[:], 127.0, None,
                                            ALU.mult))

    sxl = es.enter_context(ExitStack())
    sxrow = _sb(sxl, nc, "sxrow", [1, TL], F32, side="right")
    reps = es.enter_context(ExitStack())
    m_rep = _sb(reps, nc, "m_rep", [128, VIRT], F32, side="right")
    rs_rep = _sb(reps, nc, "rs_rep", [128, VIRT], F32, side="right")
    packs = es.enter_context(ExitStack())
    pk_sb = _sb(packs, nc, "pk_sb", [NVT, 128], F32, side="right")
    mrow = _sb(packs, nc, "mrow", [1, VIRT], F32, side="right")
    rsrow = _sb(packs, nc, "rsrow", [1, VIRT], F32, side="right")

    s_pk = mksem("s_pk")
    pack_state = {"last_dma": 0}

    def pack_row(grid_ap, n, row_dst, gate_tok, gate_sem):
        wait("tensor", gate_sem, gate_tok)
        wait("scalar", s_pk, pack_state["last_dma"])
        pe_group(pb[6], [lambda e, st, sp, gap=grid_ap, n=n: e.transpose(
            pb[6].ps[0:n, 0:128], gap, identf[:])])
        tok_ev = consume(pb[6], "scalar",
                         lambda e, n=n: e.copy(pk_sb[0:n, :], pb[6].ps[0:n, 0:128]))
        wait("sync", s_a, tok_ev)
        pack_state["last_dma"] = sdma(row_dst, pk_sb[0:n, :], sem=s_pk)
        return pack_state["last_dma"]

    tap("sumg", lambda: sum_g[:], s_a, tok_stats)
    tap("sqg", lambda: sq_g[:], s_v, tok_rs)
    tap("mg", lambda: m_g[:], s_v, tok_rs)
    tap("rsg", lambda: rs_g[:], s_v, tok_rs)
    tok_mrow = pack_row(m_g[:], NVT, mrow[:], tok_rs, s_v)
    tok_rsrow = pack_row(rs_g[:], NVT, rsrow[:], tok_rs, s_v)
    tok_sxrow = pack_row(sx_g[:], QT, sxrow[:], tok_sx, s_v)
    ph1.close()

    tap("mrow", lambda: mrow[:], s_pk, tok_mrow)
    tap("rsrow", lambda: rsrow[:], s_pk, tok_rsrow)
    tap("sxrow", lambda: sxrow[:], s_pk, tok_sxrow)

    tok_rep_ev = 0
    for row_sb, rep_sb, gate in ((mrow, m_rep, tok_mrow), (rsrow, rs_rep, tok_rsrow)):
        wait("tensor", s_pk, gate)
        for off in range(0, VIRT, 512):
            pe_group(pb[6], [lambda e, st, sp, r=row_sb, o=off: e.matmul(
                pb[6].ps[:, 0:512], onesf_r[:], r[0:1, o:o + 512],
                start=st, stop=sp)])
            tok_rep_ev = consume(pb[6], "scalar", lambda e, d=rep_sb, o=off:
                                 e.copy(d[:, o:o + 512], pb[6].ps[:, 0:512]))

    packs.close()

    fence()
    ph2 = es.enter_context(ExitStack())
    xf_buf = [_sb(ph2, nc, f"xf{i}", [128, VIRT]) for i in range(2)]
    xn_tmp = _sb(ph2, nc, "xn_tmp", [128, VIRT])
    xnf_loc = _sb(ph2, nc, "xnf_loc", [128, TL])
    sx_rep = _sb(ph2, nc, "sx_rep", [128, TL])
    k1q = _sb(ph2, nc, "k1q", [128, TL], I32)
    k1 = [_sb(ph2, nc, f"k1_{c}", [128, TL], BF16) for c in range(2)]
    wait("tensor", s_pk, tok_sxrow)
    for off in range(0, TL, 512):
        pe_group(pb[6], [lambda e, st, sp, o=off: e.matmul(
            pb[6].ps[:, 0:512], onesf_r[:], sxrow[0:1, o:o + 512],
            start=st, stop=sp)])
        tok_sxr = consume(pb[6], "scalar", lambda e, o=off: e.copy(
            sx_rep[:, o:o + 512], pb[6].ps[:, 0:512]))
    wait("vector", s_a, tok_rep_ev)
    wait("vector", s_a, tok_sxr)
    s_xf = [mksem("s_xf0"), mksem("s_xf1")]
    s_sk = [mksem("s_sk0"), mksem("s_sk1")]
    tok_xn = [0] * NCH
    tok_k1 = [0] * NCH
    k1_spill_tok = [0] * NCH
    for cdx in range(NCH):
        bi = cdx % 2
        if cdx >= 2:
            wait("sync", s_v, tok_k1[cdx - 2])
        sdma(xf_buf[bi][:], dram["x_fm"][128 * cdx:128 * (cdx + 1), :], sem=s_xf[bi])
        wait("vector", s_xf[bi], s_xf[bi].v)
        v(lambda e, b=bi: e.tensor_tensor(xn_tmp[:], xf_buf[b][:], m_rep[:],
                                          op=ALU.subtract))
        vbar()
        tok_xn[cdx] = vtok(lambda e, c=cdx: e.tensor_tensor(
            xn[c][:], xn_tmp[:], rs_rep[:], op=ALU.mult))
        v(lambda e: e.tensor_tensor(xnf_loc[:], xn_tmp[:, LV0:VIRT],
                                    rs_rep[:, LV0:VIRT], op=ALU.mult))
        vbar()
        v(lambda e: e.tensor_tensor(k1q[:], xnf_loc[:], sx_rep[:], op=ALU.mult))
        vbar()
        if cdx >= 2:
            wait("vector", s_sk[bi], k1_spill_tok[cdx - 2])
        tok_k1[cdx] = vtok(lambda e, c=cdx % 2: e.tensor_scalar(
            k1[c][:], k1q[:], -128.0, 127.0, ALU.max, ALU.min))
        wait("gpsimd", s_v, tok_k1[cdx])
        k1_spill_tok[cdx] = gdma(scr["k1"][128 * cdx:128 * (cdx + 1), :],
                                 k1[cdx % 2][:], sem=s_sk[bi])
        drain_bg(4)
    tok_xn_all = tok_xn[-1]
    tok_k1spill = (s_sk[0].v, s_sk[1].v)
    tap("xn0", lambda: xn[0][:], s_v, tok_xn_all)
    tap("k10", lambda: k1[0][:], s_v, tok_k1[0])

    ph2.close()
    reps.close()
    sxl.close()

    def finish():
        drain_all_bg()
        for sname, sm in list(SEMS.items()):
            if sm.v > 0 and sname.startswith(("s_ob", "s_sk", "s_sat", "s_sh",
                                              "s_out", "s_pp")):
                wait("sync", sm, sm.v)

    if upto <= 1:
        finish()
        return steps

    fence()
    qkv = es.enter_context(ExitStack())
    q_bf = [_sb(qkv, nc, f"q{h}", [128, TL], BF16, side="right") for h in range(H)]
    kvl = es.enter_context(ExitStack())
    k_bf = _sb(kvl, nc, "k_bf", [128, KVN], BF16, side="right")
    v_tm = [_sb(kvl, nc, f"vtm{k}", [128, 128], BF16, side="right") for k in range(NKV)]
    ckv = _sb(kvl, nc, "ckv", [128, NBV], BF16, side="right")
    ckt1 = _sb(kvl, nc, "ckt1", [128, 128], BF16, side="right")
    ckt2 = _sb(kvl, nc, "ckt2", [32, 128], BF16, side="right")

    ph4s = es.enter_context(ExitStack())
    qc_sb = [_sb(ph4s, nc, f"qcs{c}", [128, TL], BF16) for c in range(4)]
    pst = _sb(ph4s, nc, "pst", [128, 512])
    psq = _sb(ph4s, nc, "psq", [128, 512])
    rt = [_sb(ph4s, nc, f"rt{i}", [32, 512]) for i in range(5)]
    rrow_sb = _sb(ph4s, nc, "rrow_sb", [1, 512])

    def proj_span_norm(bk_mm, ems, dst_ap, nw_ap, cos_ap, sin_ap, do_rope):
        wait("scalar", s_v, s_v.v)
        pe_group(bk_mm, ems)
        tok_ev = consume(bk_mm, "scalar", lambda e: e.copy(pst[:], bk_mm.ps[:, 0:512]))
        wait("vector", s_a, tok_ev)
        if do_rope:
            v(lambda e: e.tensor_copy(rt[4][:], pst[32:64, :]))
            vbar()
            v(lambda e: e.tensor_tensor(rt[0][:], pst[0:32, :], cos_ap, op=ALU.mult))
            v(lambda e: e.tensor_tensor(rt[1][:], rt[4][:], sin_ap, op=ALU.mult))
            v(lambda e: e.tensor_tensor(rt[2][:], pst[0:32, :], sin_ap, op=ALU.mult))
            v(lambda e: e.tensor_tensor(rt[3][:], rt[4][:], cos_ap, op=ALU.mult))
            vbar()
            v(lambda e: e.tensor_tensor(pst[0:32, :], rt[0][:], rt[1][:],
                                        op=ALU.subtract))
            v(lambda e: e.tensor_tensor(rt[4][:], rt[2][:], rt[3][:],
                                        op=ALU.add))
            vbar()
            tok_pre = vtok(lambda e: e.tensor_copy(pst[32:64, :], rt[4][:]))
            wait("scalar", s_v, tok_pre)
        tok_sq = atok(lambda e: e.activation(psq[:], pst[:], AF.Square))
        wait("tensor", s_a, tok_sq)
        bkr = pb[4]
        if bkr.free is not None:
            wait("tensor", bkr.free[0], bkr.free[1])
            bkr.free = None
        pe_group(bkr, [lambda e, st, sp: e.matmul(
            bkr.ps[0:1, 0:512], onesf_c[:], psq[:], start=st, stop=sp)])
        tok_rr = consume(bkr, "vector", lambda e: e.tensor_scalar(
            rrow_sb[:], bkr.ps[0:1, 0:512], 1.0 / D, RMS_EPS, ALU.mult, ALU.add))
        wait("scalar", s_v, tok_rr)
        tok_rsq = atok(lambda e: e.activation(rrow_sb[:], rrow_sb[:], AF.Sqrt))
        wait("vector", s_a, tok_rsq)
        tok_rrc = vtok(lambda e: e.reciprocal(rrow_sb[:], rrow_sb[:]))
        wait("tensor", s_v, tok_rrc)
        bkp = pb[5]
        if bkp.free is not None:
            wait("tensor", bkp.free[0], bkp.free[1])
            bkp.free = None
        pe_group(bkp, [lambda e, st, sp: e.matmul(
            bkp.ps[:, 0:512], onesf_r[:], rrow_sb[:], start=st, stop=sp)])
        return consume(bkp, "vector", lambda e, d=dst_ap: e.scalar_tensor_tensor(
            d, pst[:], nw_ap, bkp.ps[:, 0:512], ALU.mult, ALU.mult))

    ph4a = es.enter_context(ExitStack())
    wqc_sb = [_sb(ph4a, nc, f"wqc{c}", [128, QC], BF16) for c in range(NCH)]
    s_wqc = mksem("s_wqc")
    for c in range(NCH):
        gdma(wqc_sb[c][:], dram["wqc"][128 * c:128 * (c + 1), :], sem=s_wqc)
    tok_wqc = s_wqc.v
    wait("tensor", s_wqc, tok_wqc)
    wait("tensor", s_v, tok_xn_all)
    for oc in range(4):
        for th in range(2):
            bk = pb[(2 * oc + th) % 2]
            ems = []
            for c in range(NCH):
                ems.append(lambda e, st, sp, bkk=bk, c=c, oc=oc, th=th: e.matmul(
                    bkk.ps[:, 0:512], wqc_sb[c][:, 128 * oc:128 * (oc + 1)],
                    xn[c][:, LV0 + 512 * th:LV0 + 512 * (th + 1)],
                    start=st, stop=sp))
            pe_group(bk, ems)
            consume(bk, "scalar", lambda e, bkk=bk, oc=oc, th=th: e.copy(
                qc_sb[oc][:, 512 * th:512 * (th + 1)], bkk.ps[:, 0:512]))
        drain_bg(2)
    ph4a.close()
    fence()

    ph4b = es.enter_context(ExitStack())
    wqup_sb = [_sb(ph4b, nc, f"wqup{c}", [128, C], BF16) for c in range(4)]
    s_wqup = mksem("s_wqup")
    for c in range(4):
        gdma(wqup_sb[c][:], dram["wqup"][128 * c:128 * (c + 1), :], sem=s_wqup)
    tok_wqup = s_wqup.v
    wait("tensor", s_wqup, tok_wqup)
    COSQ0 = 512
    tok_q = [0] * H
    for h in range(H):
        for th in range(2):
            bk = pb[th]
            ems = []
            for c4 in range(4):
                ems.append(lambda e, st, sp, bkk=bk, c4=c4, h=h, th=th: e.matmul(
                    bkk.ps[:, 0:512], wqup_sb[c4][:, 128 * h:128 * (h + 1)],
                    qc_sb[c4][:, 512 * th:512 * (th + 1)], start=st, stop=sp))
            o0 = COSQ0 + 512 * th
            tok_q[h] = proj_span_norm(
                bk, ems, q_bf[h][:, 512 * th:512 * (th + 1)], qnw_s[:, 0:1],
                cosq[:, o0:o0 + 512], sinq[:, o0:o0 + 512], True)
        drain_bg(2)
    tap("q0", lambda: q_bf[0][:], s_v, tok_q[0])
    ph4b.close()
    fence()

    ph4c = es.enter_context(ExitStack())
    wk_sb = [_sb(ph4c, nc, f"wks{c}", [128, D], BF16) for c in range(NCH)]
    s_wk = mksem("s_wk")
    for c in range(NCH):
        gdma(wk_sb[c][:], dram["wk"][128 * c:128 * (c + 1), :], sem=s_wk)
    tok_wk = s_wk.v
    wait("tensor", s_wk, tok_wk)
    tok_kbf = 0
    for th in range(3):
        bk = pb[th % 2]
        ems = []
        for c in range(NCH):
            ems.append(lambda e, st, sp, bkk=bk, c=c, th=th: e.matmul(
                bkk.ps[:, 0:512], wk_sb[c][:],
                xn[c][:, KV0 + 512 * th:KV0 + 512 * (th + 1)], start=st, stop=sp))
        o0 = 512 * th
        tok_kbf = proj_span_norm(
            bk, ems, k_bf[:, 512 * th:512 * (th + 1)], knw[:, 0:1],
            cosq[:, o0:o0 + 512], sinq[:, o0:o0 + 512], True)
        drain_bg(2)
    tap("kbf", lambda: k_bf[:], s_v, tok_kbf)
    ph4c.close()
    fence()

    ph4d = es.enter_context(ExitStack())
    wv_sb = [_sb(ph4d, nc, f"wvs{c}", [128, D], BF16) for c in range(NCH)]
    v_fm = _sb(ph4d, nc, "v_fm", [128, KVN], BF16)
    s_wv = mksem("s_wv")
    for c in range(NCH):
        gdma(wv_sb[c][:], dram["wv"][128 * c:128 * (c + 1), :], sem=s_wv)
    tok_wv = s_wv.v
    wait("tensor", s_wv, tok_wv)
    for th in range(3):
        bk = pb[th % 2]
        ems = []
        for c in range(NCH):
            ems.append(lambda e, st, sp, bkk=bk, c=c, th=th: e.matmul(
                bkk.ps[:, 0:512], wv_sb[c][:],
                xn[c][:, KV0 + 512 * th:KV0 + 512 * (th + 1)], start=st, stop=sp))
        pe_group(bk, ems)
        consume(bk, "scalar", lambda e, bkk=bk, th=th: e.copy(
            v_fm[:, 512 * th:512 * (th + 1)], bkk.ps[:, 0:512]))
    wait("tensor", s_a, s_a.v)
    tok_vtm = [0] * NKV
    for kt in range(NKV):
        half = (kt % 2) * 128
        if pbt.free is not None:
            wait("tensor", pbt.free[0], pbt.free[1])
            pbt.free = None
        pbt.prod = ttok(lambda e, kt=kt, hf=half: e.transpose(
            pbt.ps[:, hf:hf + 128], v_fm[:, 128 * kt:128 * (kt + 1)], identb[:]))
        tok_vtm[kt] = consume(pbt, "scalar", lambda e, kt=kt, hf=half: e.copy(
            v_tm[kt][:], pbt.ps[:, hf:hf + 128]))
        drain_bg(1)
    ph4d.close()
    fence()

    ph4e = es.enter_context(ExitStack())
    wcp_buf = [_sb(ph4e, nc, f"wcp{i}", [128, 4096], BF16) for i in range(2)]
    ckst = _sb(ph4e, nc, "ckst", [128, NBV])
    cksq = _sb(ph4e, nc, "cksq", [128, NBV])
    ckrow = _sb(ph4e, nc, "ckrow", [1, NBV])
    bk = pb[6]
    if bk.free is not None:
        wait("tensor", bk.free[0], bk.free[1])
        bk.free = None
    s_wcp = [mksem("s_wcp0"), mksem("s_wcp1")]
    tok_wcp = [0] * 8
    mm_wcp = [0] * 8
    tok_wcp[0] = gdma(wcp_buf[0][:], dram["wcomp"][:, 0:4096], sem=s_wcp[0])
    for kk in range(8):
        if kk < 7:
            if kk >= 1:
                wait("gpsimd", s_t, mm_wcp[kk - 1])
            tok_wcp[kk + 1] = gdma(wcp_buf[(kk + 1) % 2][:],
                                   dram["wcomp"][:, 4096 * (kk + 1):4096 * (kk + 2)],
                                   sem=s_wcp[(kk + 1) % 2])
        wait("tensor", s_wcp[kk % 2], tok_wcp[kk])
        for fl in range(32):
            f = 32 * kk + fl
            m_, cdx = f // 16, f % 16
            xn_str = xn[cdx][:, m_:VIRT:M]
            if fl == 31:
                mm_wcp[kk] = ttok(lambda e, b=kk % 2, fl=fl, xs=xn_str,
                                  st=(f == 0), sp=(f == 255): e.matmul(
                    bk.ps[:, 0:NBV], wcp_buf[b][:, 128 * fl:128 * (fl + 1)],
                    xs, start=st, stop=sp))
            else:
                t(lambda e, b=kk % 2, fl=fl, xs=xn_str, st=(f == 0): e.matmul(
                    bk.ps[:, 0:NBV], wcp_buf[b][:, 128 * fl:128 * (fl + 1)],
                    xs, start=st, stop=False))
        drain_bg(1)
    bk.prod = mm_wcp[7]
    tok_ck0 = consume(bk, "scalar", lambda e: e.copy(ckst[:], bk.ps[:, 0:NBV]))
    wait("scalar", s_a, tok_ck0)
    tok_cksq = atok(lambda e: e.activation(cksq[:], ckst[:], AF.Square))
    wait("tensor", s_a, tok_cksq)
    bkr = pb[4]
    if bkr.free is not None:
        wait("tensor", bkr.free[0], bkr.free[1])
        bkr.free = None
    pe_group(bkr, [lambda e, st, sp: e.matmul(
        bkr.ps[0:1, 0:NBV], onesf_c[:], cksq[:], start=st, stop=sp)])
    tok_rr = consume(bkr, "vector", lambda e: e.tensor_scalar(
        ckrow[:], bkr.ps[0:1, 0:NBV], 1.0 / D, RMS_EPS, ALU.mult, ALU.add))
    wait("scalar", s_v, tok_rr)
    tok_rsq = atok(lambda e: e.activation(ckrow[:], ckrow[:], AF.Sqrt))
    wait("vector", s_a, tok_rsq)
    tok_rrc = vtok(lambda e: e.reciprocal(ckrow[:], ckrow[:]))
    wait("tensor", s_v, tok_rrc)
    bkp = pb[5]
    if bkp.free is not None:
        wait("tensor", bkp.free[0], bkp.free[1])
        bkp.free = None
    pe_group(bkp, [lambda e, st, sp: e.matmul(
        bkp.ps[:, 0:NBV], onesf_r[:], ckrow[:], start=st, stop=sp)])
    tok_ckv = consume(bkp, "vector", lambda e: e.scalar_tensor_tensor(
        ckv[:], ckst[:], knw[:, 0:1], bkp.ps[:, 0:NBV], ALU.mult, ALU.mult))
    wait("tensor", s_v, tok_ckv)
    if pbt.free is not None:
        wait("tensor", pbt.free[0], pbt.free[1])
        pbt.free = None
    t(lambda e: e.transpose(pbt.ps[:, 0:128], ckv[:, 0:128], identb[:]))
    pbt.prod = ttok(lambda e: e.transpose(pbt.ps[0:32, 128:256], ckv[:, 128:160],
                                          identb[:]))
    wait("scalar", s_t, pbt.prod)
    a(lambda e: e.copy(ckt1[:], pbt.ps[:, 0:128]))
    tok_ckt2 = atok(lambda e: e.copy(ckt2[:], pbt.ps[0:32, 128:256]))
    pbt.free = (s_a, tok_ckt2)
    tap("ckv", lambda: ckv[:], s_v, tok_ckv)
    ph4e.close()
    ph4s.close()
    glive.close()
    ropel.close()
    if upto <= 2:
        finish()
        return steps

    fence()
    attcons = es.enter_context(ExitStack())
    s_msk = mksem("s_msk")
    msw = _sb(attcons, nc, "msw_s", [128, 2 * 8 * 512], BF16)
    mc = _sb(attcons, nc, "mc_s", [128, QT * NBV], F32)
    gdma(msw[:], dram["msw"], sem=s_msk)
    tok_masks = gdma(mc[:], dram["mc"], sem=s_msk)
    wait("vector", s_msk, tok_masks)

    ao = q_bf
    ph5 = es.enter_context(ExitStack())
    csc = [_sb(ph5, nc, f"csc{i}", [128, NBV]) for i in range(2)]
    pexp = [_sb(ph5, nc, f"pexp{i}", [128, NBV]) for i in range(2)]
    pn_sb = [_sb(ph5, nc, f"pn{i}", [128, NBV], BF16) for i in range(2)]
    ptA = [_sb(ph5, nc, f"ptA{i}", [128, 128], BF16) for i in range(2)]
    ptB = [_sb(ph5, nc, f"ptB{i}", [32, 128], BF16) for i in range(2)]
    mxc = _sb(ph5, nc, "mxc", [128, 4])
    u_sb = [_sb(ph5, nc, f"usb{i}", [128, 512]) for i in range(2)]
    ex_sb = [_sb(ph5, nc, f"exsb{i}", [128, 512], BF16) for i in range(2)]
    drow = _sb(ph5, nc, "drow", [1, 512])
    rrep = _sb(ph5, nc, "rrep", [128, 512])
    swtmp = _sb(ph5, nc, "swtmp", [128, 512])
    ao_c = [_sb(ph5, nc, f"aoc{i}", [128, 512]) for i in range(2)]

    wait("tensor", s_a, s_a.v)
    comp_gate = [0, 0]
    aoc_free = [0, 0]
    for h in range(H):
        for ch in range(2):
            for jj in range(4):
                j = 4 * ch + jj
                par = jj % 2
                bk2 = pb[2]
                pe_group(bk2, [lambda e, st, sp, h=h, j=j: e.matmul(
                    bk2.ps[:, 0:NBV], q_bf[h][:, 128 * j:128 * (j + 1)],
                    ckv[:], start=st, stop=sp)])
                wait("vector", s_t, comp_gate[par])
                tok_csc = consume(bk2, "vector", lambda e, p=par, j=j: e.tensor_tensor(
                    csc[p][:], bk2.ps[:, 0:NBV], mc[:, NBV * j:NBV * (j + 1)],
                    op=ALU.add))
                vbar()
                v(lambda e, p=par: e.reduce_max(mxc[:, 0:1], csc[p][:], axis=AX.X))
                vbar()
                v(lambda e, h=h: e.tensor_tensor(mxc[:, 0:1], mxc[:, 0:1],
                                                 sinkrep[:, h:h + 1], op=ALU.max))
                vbar()
                tok_mx = vtok(lambda e: e.tensor_scalar(mxc[:, 1:2], mxc[:, 0:1],
                                                        -1.0, None, ALU.mult))
                wait("scalar", s_v, tok_mx)
                tok_pexp = atok(lambda e, p=par: e.activation(
                    pexp[p][:], csc[p][:], AF.Exp, bias=mxc[:, 1:2]))
                a(lambda e, h=h: e.activation(mxc[:, 2:3], mxc[:, 0:1], AF.Exp,
                                              bias=sinkrep[:, h:h + 1], scale=-1.0))
                wait("vector", s_a, s_a.v)
                v(lambda e, p=par: e.reduce_sum(mxc[:, 3:4], pexp[p][:], axis=AX.X))
                vbar()
                v(lambda e: e.tensor_tensor(mxc[:, 3:4], mxc[:, 3:4], mxc[:, 2:3],
                                            op=ALU.add))
                vbar()
                v(lambda e: e.reciprocal(mxc[:, 3:4], mxc[:, 3:4]))
                vbar()
                tok_pn = vtok(lambda e, p=par: e.tensor_scalar(
                    pn_sb[p][:], pexp[p][:], mxc[:, 3:4], None, ALU.mult))
                wait("tensor", s_v, tok_pn)
                if pbt.free is not None:
                    wait("tensor", pbt.free[0], pbt.free[1])
                    pbt.free = None
                t(lambda e, p=par: e.transpose(pbt.ps[:, 0:128], pn_sb[p][:, 0:128],
                                               identb[:]))
                pbt.prod = ttok(lambda e, p=par: e.transpose(
                    pbt.ps[0:32, 128:256], pn_sb[p][:, 128:160], identb[:]))
                wait("scalar", s_t, pbt.prod)
                a(lambda e, p=par: e.copy(ptA[p][:], pbt.ps[:, 0:128]))
                tok_pt = atok(lambda e, p=par: e.copy(ptB[p][:], pbt.ps[0:32, 128:256]))
                pbt.free = (s_a, tok_pt)
                bk3 = pb[3]
                if jj == 0 and bk3.free is not None:
                    wait("tensor", bk3.free[0], bk3.free[1])
                    bk3.free = None
                wait("tensor", s_a, tok_pt)
                t(lambda e, p=par, jj=jj: e.matmul(
                    bk3.ps[:, 128 * jj:128 * (jj + 1)], ckt1[:], ptA[p][:],
                    start=True, stop=False))
                comp_gate[par] = ttok(lambda e, p=par, jj=jj: e.matmul(
                    bk3.ps[:, 128 * jj:128 * (jj + 1)], ckt2[:], ptB[p][:],
                    start=False, stop=True))
                bk3.prod = comp_gate[par]
            cpar = (2 * h + ch) % 2
            wait("scalar", s_t, bk3.prod)
            wait("scalar", s_v, aoc_free[cpar])
            tok_evc = atok(lambda e, cp=cpar: e.copy(ao_c[cp][:],
                                                     bk3.ps[:, 0:512]))
            bk3.free = (s_a, tok_evc)
            bk4, bk5 = pb[4], pb[5]
            if bk4.free is not None:
                wait("tensor", bk4.free[0], bk4.free[1])
                bk4.free = None
            if bk5.free is not None:
                wait("tensor", bk5.free[0], bk5.free[1])
                bk5.free = None
            tanh_tok = [0] * 8
            mask_tok = [0] * 8
            for e_i in range(8):
                kt = 4 * ch + e_i
                par = e_i % 2
                bks = pb[par]
                if e_i >= 2:
                    wait("tensor", s_a, tanh_tok[e_i - 2])
                elif bks.free is not None:
                    wait("tensor", bks.free[0], bks.free[1])
                    bks.free = None
                bks.prod = ttok(lambda e, h=h, ch=ch, kt=kt, p=par: e.matmul(
                    pb[p].ps[:, 0:512], k_bf[:, 128 * kt:128 * (kt + 1)],
                    q_bf[h][:, 512 * ch:512 * (ch + 1)], start=True, stop=True))
                wait("scalar", s_t, bks.prod)
                if e_i >= 2:
                    wait("scalar", s_t, s_t.v)
                tanh_tok[e_i] = atok(lambda e, p=par: e.activation(
                    u_sb[p][:], pb[p].ps[:, 0:512], AF.Tanh, scale=1.0 / CAP))
                abar()
                tok_ex = atok(lambda e, p=par: e.activation(
                    ex_sb[p][:], u_sb[p][:], AF.Exp, scale=CAP))
                wait("vector", s_a, tok_ex)
                mask_tok[e_i] = vtok(lambda e, p=par, ch=ch, e_i=e_i: e.tensor_tensor(
                    ex_sb[p][:], ex_sb[p][:],
                    msw[:, (ch * 8 + e_i) * 512:(ch * 8 + e_i + 1) * 512],
                    op=ALU.mult))
                wait("tensor", s_v, mask_tok[e_i])
                t(lambda e, p=par, e_i=e_i: e.matmul(
                    bk4.ps[0:1, 0:512], onesb[:], ex_sb[p][:],
                    start=(e_i == 0), stop=(e_i == 7)))
                last_pv = ttok(lambda e, p=par, e_i=e_i, kt=kt: e.matmul(
                    bk5.ps[:, 0:512], v_tm[kt][:], ex_sb[p][:],
                    start=(e_i == 0), stop=(e_i == 7)))
            bk4.prod = last_pv
            bk5.prod = last_pv
            pb[0].free = (s_a, tanh_tok[6])
            pb[1].free = (s_a, tanh_tok[7])
            wait("vector", s_t, last_pv)
            v(lambda e, h=h: e.tensor_scalar(drow[:], bk4.ps[0:1, 0:512],
                                             expsink[0:1, h:h + 1], None, ALU.add))
            vbar()
            tok_dr = vtok(lambda e: e.reciprocal(drow[:], drow[:]))
            bk4.free = (s_v, tok_dr)
            wait("tensor", s_v, tok_dr)
            bk6 = pb[6]
            if bk6.free is not None:
                wait("tensor", bk6.free[0], bk6.free[1])
                bk6.free = None
            pe_group(bk6, [lambda e, st, sp: e.matmul(
                bk6.ps[:, 0:512], onesf_r[:], drow[:], start=st, stop=sp)])
            tok_rrep = consume(bk6, "scalar", lambda e: e.copy(rrep[:],
                                                               bk6.ps[:, 0:512]))
            wait("vector", s_a, tok_rrep)
            tok_sw = vtok(lambda e: e.tensor_tensor(swtmp[:], bk5.ps[:, 0:512],
                                                    rrep[:], op=ALU.mult))
            bk5.free = (s_v, tok_sw)
            vbar()
            wait("vector", s_a, tok_evc)
            tok_ao = vtok(lambda e, h=h, ch=ch, cp=cpar: e.tensor_tensor(
                ao[h][:, 512 * ch:512 * (ch + 1)], swtmp[:], ao_c[cp][:],
                op=ALU.add))
            aoc_free[cpar] = tok_ao
            drain_bg(10)
    tok_ao_all = tok_ao
    tap("ao0", lambda: ao[0][:], s_v, tok_ao_all)

    ph5.close()
    attcons.close()
    kvl.close()

    if upto <= 3:
        finish()
        return steps

    fence()
    ph6l = es.enter_context(ExitStack())
    inter = [_sb(ph6l, nc, f"it{k}", [128, TL], BF16) for k in range(64)]
    ph6 = es.enter_context(ExitStack())
    s_wg = mksem("s_wg")
    wg_sb = [_sb(ph6, nc, f"wgsb{i}", [128, C], BF16) for i in range(4)]
    wg_mm_done = 0
    for g_ in range(4):
        wait("gpsimd", s_t, wg_mm_done)
        tok_wg = 0
        for ic in range(4):
            tok_wg = gdma(wg_sb[ic][:],
                          dram["wg"][128 * (4 * g_ + ic):128 * (4 * g_ + ic + 1), :],
                          sem=s_wg)
        wait("tensor", s_wg, tok_wg)
        for oc in range(16):
            for th in range(2):
                bk = pb[(2 * oc + th) % 2]
                ems = []
                for ic in range(4):
                    ems.append(lambda e, st, sp, ic=ic, g_=g_, oc=oc, th=th,
                               bkk=bk: e.matmul(
                        bkk.ps[:, 0:512], wg_sb[ic][:, 128 * oc:128 * (oc + 1)],
                        ao[4 * g_ + ic][:, 512 * th:512 * (th + 1)], start=st, stop=sp))
                pe_group(bk, ems)
                consume(bk, "scalar", lambda e, bkk=bk, g_=g_, oc=oc, th=th: e.copy(
                    inter[16 * g_ + oc][:, 512 * th:512 * (th + 1)], bkk.ps[:, 0:512]))
            drain_bg(4)
        wg_mm_done = s_t.v
    tap("inter0", lambda: inter[0][:], s_a, s_a.v)

    ph6.close()
    qkv.close()

    fence()
    ph7 = es.enter_context(ExitStack())
    wo_sb = [_sb(ph7, nc, f"wosb{i}", [128, 8192], BF16) for i in range(2)]
    att_st = [_sb(ph7, nc, f"attst{i}", [128, 512], BF16) for i in range(2)]
    s_wo = [mksem("s_wo0"), mksem("s_wo1")]
    s_sat = [mksem("s_sat0"), mksem("s_sat1")]
    tok_wo = [0] * 16
    tok_wo[0] = gdma(wo_sb[0][:], dram["wo"][:, 0:8192], sem=s_wo[0])
    tok_wo[1] = gdma(wo_sb[1][:], dram["wo"][:, 8192:16384], sem=s_wo[1])
    att_st_free = [0, 0]
    wo_mm_done = [0] * 16
    for cc in range(16):
        slot = cc % 2
        wait("tensor", s_wo[slot], tok_wo[cc])
        for th in range(2):
            bk = pb[2 + th]
            ems = []
            for k in range(64):
                ems.append(lambda e, st, sp, s_=slot, k=k, th=th, bkk=bk: e.matmul(
                    bkk.ps[:, 0:512], wo_sb[s_][:, 128 * k:128 * (k + 1)],
                    inter[k][:, 512 * th:512 * (th + 1)], start=st, stop=sp))
            pe_group(bk, ems)
            stp = (2 * cc + th) % 2
            wait("scalar", s_sat[stp], att_st_free[stp])
            tok_ev = consume(bk, "scalar", lambda e, bkk=bk, p=stp: e.copy(
                att_st[p][:], bkk.ps[:, 0:512]))
            wait("gpsimd", s_a, tok_ev)
            att_st_free[stp] = gdma(
                scr["att"][128 * cc:128 * (cc + 1), 512 * th:512 * (th + 1)],
                att_st[stp][:], sem=s_sat[stp])
        wo_mm_done[cc] = s_t.v
        if cc + 2 < 16:
            wait("gpsimd", s_t, wo_mm_done[cc])
            tok_wo[cc + 2] = gdma(wo_sb[slot][:],
                                  dram["wo"][:, 8192 * (cc + 2):8192 * (cc + 3)],
                                  sem=s_wo[slot])
        drain_bg(6)
    tok_att_sp = (s_sat[0].v, s_sat[1].v)
    if "att0" in tap_map:
        wait("sync", s_sat[0], tok_att_sp[0])
        wait("sync", s_sat[1], tok_att_sp[1])
        s_out.v += 16
        add("sync", lambda e: e.dma_start(out=tap_map["att0"],
                                          in_=scr["att"][0:128, :])
            .then_inc(s_out.h, 16))

    ph7.close()
    ph6l.close()

    if upto <= 4:
        finish()
        return steps

    drain_all_bg()
    wait("vector", s_pf, s_pf.v)
    v(lambda e: e.reduce_sum(swcol[:, 0:1], ppcol[:, 0:128], axis=AX.X))
    tok_ws = vtok(lambda e: e.reduce_sum(swcol[:, 1:2], ppcol[:, 128:192], axis=AX.X))
    vbar()
    wait("tensor", s_v, tok_ws)
    v(lambda e: e.reduce_sum(ppfin[:, 0:1], ppcol[:, 0:128], axis=AX.X))
    tok_cs = vtok(lambda e: e.reduce_sum(ppfin[:, 1:2], ppcol[:, 128:192],
                                         axis=AX.X))
    vbar()
    v(lambda e: e.tensor_scalar(ppfin_i[:, 0:2], ppfin[:, 0:2], 1.0 / 64.0,
                                None, ALU.mult))
    vbar()
    v(lambda e: e.tensor_copy(ppfin[:, 2:4], ppfin_i[:, 0:2]))
    vbar()
    tok_hilo = vtok(lambda e: e.scalar_tensor_tensor(
        ppfin[:, 4:6], ppfin[:, 2:4], -64.0, ppfin[:, 0:2], ALU.mult, ALU.add))
    wait("tensor", s_v, tok_hilo)
    bk6 = pb[6]
    pe_group(bk6, [
        lambda e, st, sp: e.matmul(bk6.ps[0:1, 0:2], onesf_c[:],
                                   ppfin[:, 2:4], start=True, stop=True),
        lambda e, st, sp: e.matmul(bk6.ps[0:1, 2:4], onesf_c[:],
                                   ppfin[:, 4:6], start=True, stop=True),
    ])
    tok_sums = consume(bk6, "vector", lambda e: e.tensor_copy(
        ppsums[0:1, 0:4], bk6.ps[0:1, 0:4]))
    vbar()
    v(lambda e: e.scalar_tensor_tensor(ppsums[0:1, 4:6], ppsums[0:1, 0:2],
                                       64.0, ppsums[0:1, 2:4], ALU.mult, ALU.add))
    vbar()
    v(lambda e: e.memset(swv[:], 0.0))
    vbar()
    v(lambda e: e.tensor_scalar(swv[0:1, 0:1], ppsums[0:1, 4:5],
                                1.0 / (2.0 * F2 * C), None, ALU.mult))
    v(lambda e: e.tensor_scalar(swv[0:1, 2:3], ppsums[0:1, 5:6],
                                1.0 / (C * F2), None, ALU.mult))
    vbar()
    v(lambda e: e.tensor_scalar(swv[0:1, 0:1], swv[0:1, 0:1], QEPS, None, ALU.max))
    v(lambda e: e.tensor_scalar(swv[0:1, 2:3], swv[0:1, 2:3], QEPS, None, ALU.max))
    vbar()
    v(lambda e: e.reciprocal(swv[0:1, 1:2], swv[0:1, 0:1]))
    v(lambda e: e.tensor_scalar(swv[0:1, 3:4], swv[0:1, 2:3], 0.5,
                                float(1.0 + 2.0 ** -24), ALU.mult, ALU.mult))
    vbar()
    tok_swv = vtok(lambda e: e.tensor_scalar(swv[0:1, 4:5], swv[0:1, 3:4],
                                             -1.0, None, ALU.mult))
    wait("tensor", s_v, tok_swv)
    pe_group(bk6, [lambda e, st, sp: e.matmul(
        bk6.ps[:, 0:8], onesf_r[:], swv[0:1, 0:8], start=st, stop=sp)])
    tok_swc = consume(bk6, "vector", lambda e: e.tensor_copy(swcol[:, 0:8],
                                                             bk6.ps[:, 0:8]))
    vbar()
    v(lambda e: e.reciprocal(r1col[:], sx_g[:]))
    vbar()
    tok_r1 = vtok(lambda e: e.tensor_scalar(r1col[:], r1col[:], swcol[:, 0:1],
                                            None, ALU.mult))
    tap("swv", lambda: swv[:], s_v, tok_swv)
    tap("ppfin", lambda: ppfin[:], s_v, tok_swv)
    ppes.close()

    fence()
    ph8 = es.enter_context(ExitStack())
    k1r = [_sb(ph8, nc, f"k1r{c}", [128, TL], BF16) for c in range(NCH)]
    wgu_f32 = [_sb(ph8, nc, f"wguf{i}", [128, 2048]) for i in range(4)]
    wgu_i = _sb(ph8, nc, "wgu_i", [128, 2048], I32)
    wgu_t = [_sb(ph8, nc, f"wgut{i}", [128, 8192], BF16) for i in range(2)]
    a_st = [_sb(ph8, nc, f"ast{i}", [128, 512], F32) for i in range(QT)]
    h_st = [_sb(ph8, nc, f"hst{i}", [128, 512], F32) for i in range(2)]
    hmx = _sb(ph8, nc, "hmx", [128, QT])

    s_k1r = mksem("s_k1r")
    s_wgf = [mksem(f"s_wgf{i}") for i in range(4)]
    s_sh = [mksem("s_sh0"), mksem("s_sh1")]
    wait("gpsimd", s_sk[0], tok_k1spill[0])
    wait("gpsimd", s_sk[1], tok_k1spill[1])
    tok_k1r = [0] * NCH
    for c in range(NCH):
        tok_k1r[c] = gdma(k1r[c][:], scr["k1"][128 * c:128 * (c + 1), :],
                          sem=s_k1r)
    v(lambda e: e.memset(mxh_g[:], 0.0))
    wait("tensor", s_k1r, tok_k1r[-1])
    wait("tensor", s_v, tok_r1)

    GROUPS = [([0, 1, 2, 3], [0, 1, 2, 3]),
              ([4, 5, 6], [4, 5, 6]),
              ([7], [0])]
    tern_tok = [0, 0]
    mm_done_tok = [0, 0]
    h_st_free = [0, 0]
    f32_free = [0, 0, 0, 0]
    silu_tok = [0] * QT
    a_st_free = [0] * QT
    for jj in range(32):
        par = jj % 2
        is_gate = jj % 2 == 0
        pair = jj // 2
        wait("vector", s_t, mm_done_tok[par])
        for q4 in range(4):
            fb = q4
            qdma, qeng = (sdma, "sync") if fb % 2 == 0 else (gdma, "gpsimd")
            wait(qeng, s_v, f32_free[fb])
            qdma(wgu_f32[fb][:], dram["wgu"][:, jj * 8192 + 2048 * q4:
                                             jj * 8192 + 2048 * (q4 + 1)],
                 sem=s_wgf[fb])
            wait("vector", s_wgf[fb], s_wgf[fb].v)
            if jj == 0 and q4 == 0:
                wait("vector", s_v, tok_swc)
            vbar()
            f32_free[fb] = vtok(lambda e, fb=fb: e.tensor_scalar(
                wgu_i[:], wgu_f32[fb][:], swcol[:, 1:2], None, ALU.mult))
            vbar()
            tern_tok[par] = vtok(lambda e, p=par, q4=q4: e.tensor_scalar(
                wgu_t[p][:, 2048 * q4:2048 * (q4 + 1)], wgu_i[:],
                -1.0, 1.0, ALU.max, ALU.min))
        wait("tensor", s_v, tern_tok[par])
        for tts, bankidx in GROUPS:
            bset = [pb[bx] for bx in bankidx]
            for bx in bset:
                if bx.free is not None:
                    wait("tensor", bx.free[0], bx.free[1])
                    bx.free = None
            for cc in range(NCH):
                for ti, tt in enumerate(tts):
                    bkx = bset[ti]
                    last = (cc == NCH - 1) and (ti == len(tts) - 1)
                    emitfn = (lambda e, bkk=bkx, c=cc, tt=tt, p=par,
                              st=(cc == 0), sp=(cc == NCH - 1): e.matmul(
                        bkk.ps[:, 0:512], k1r[c][:, 128 * tt:128 * (tt + 1)],
                        wgu_t[p][:, 512 * c:512 * (c + 1)], start=st, stop=sp))
                    if last:
                        mm_done_tok[par] = ttok(emitfn)
                        for bx in bset:
                            bx.prod = mm_done_tok[par]
                    else:
                        t(emitfn)
            for ti, tt in enumerate(tts):
                bkx = bset[ti]
                if is_gate:
                    wait("scalar", s_v, a_st_free[tt])
                    silu_tok[tt] = consume(bkx, "scalar", lambda e, bkk=bkx, tt=tt:
                                           e.activation(a_st[tt][:],
                                                        bkk.ps[:, 0:512], AF.Silu,
                                                        scale=r1col[:, tt:tt + 1]))
                else:
                    sp_ = tt % 2
                    wait("vector", s_a, silu_tok[tt])
                    wait("vector", s_sh[sp_], h_st_free[sp_])
                    a_st_free[tt] = consume(bkx, "vector",
                                            lambda e, bkk=bkx, tt=tt, p=sp_:
                                            e.scalar_tensor_tensor(
                        h_st[p][:], bkk.ps[:, 0:512], r1col[:, tt:tt + 1],
                        a_st[tt][:], ALU.mult, ALU.mult))
                    vbar()
                    v(lambda e, tt=tt, p=sp_: e.reduce_max(
                        hmx[:, tt:tt + 1], h_st[p][:], axis=AX.X,
                        apply_absolute_value=True))
                    vbar()
                    tok_hmx = vtok(lambda e, tt=tt: e.tensor_tensor(
                        mxh_g[:, tt:tt + 1], mxh_g[:, tt:tt + 1],
                        hmx[:, tt:tt + 1], op=ALU.max))
                    wait("scalar", s_v, tok_hmx)
                    h_st_free[sp_] = edma(
                        "scalar",
                        scr["h"][128 * tt:128 * (tt + 1),
                                 512 * pair:512 * (pair + 1)],
                        h_st[sp_][:], sem=s_sh[sp_])
    tok_h_sp = (s_sh[0].v, s_sh[1].v)

    ph8.close()

    if upto <= 5:
        finish()
        return steps

    fence()
    ph9l = es.enter_context(ExitStack())
    sx2row = _sb(ph9l, nc, "sx2row", [1, TL], F32, side="right")
    r2row = _sb(ph9l, nc, "r2row", [1, TL], F32, side="right")
    r2rep = _sb(ph9l, nc, "r2rep", [128, TL], F32, side="right")
    k2b = [_sb(ph9l, nc, f"k2b{f}", [128, 512], BF16, side="right") for f in range(NF)]

    vbar()
    v(lambda e: e.tensor_scalar(sx2col[:], mxh_g[:], QEPS, None, ALU.max))
    vbar()
    v(lambda e: e.reciprocal(sx2col[:], sx2col[:]))
    vbar()
    tok_sx2 = vtok(lambda e: e.tensor_scalar(sx2col[:], sx2col[:], 127.0,
                                             None, ALU.mult))
    pk2 = es.enter_context(ExitStack())
    pk2_sb = _sb(pk2, nc, "pk2_sb", [QT, 128], F32, side="right")
    wait("tensor", s_v, tok_sx2)
    bk6 = pb[6]
    if bk6.free is not None:
        wait("tensor", bk6.free[0], bk6.free[1])
        bk6.free = None
    pe_group(bk6, [lambda e, st, sp: e.transpose(bk6.ps[0:QT, 0:128], sx2col[:],
                                                 identf[:])])
    tok_pk2 = consume(bk6, "scalar", lambda e: e.copy(pk2_sb[:],
                                                      bk6.ps[0:QT, 0:128]))
    wait("sync", s_a, tok_pk2)
    tok_sx2row = sdma(sx2row[:], pk2_sb[:], sem=s_pk)
    wait("vector", s_pk, tok_sx2row)
    v(lambda e: e.reciprocal(r2row[:], sx2row[:]))
    vbar()
    tok_r2 = vtok(lambda e: e.tensor_scalar(r2row[:], r2row[:], swv[0:1, 2:3],
                                            None, ALU.mult))
    wait("tensor", s_v, tok_r2)
    tok_r2rep = 0
    for off in range(0, TL, 512):
        pe_group(bk6, [lambda e, st, sp, o=off: e.matmul(
            bk6.ps[:, 0:512], onesf_r[:], r2row[0:1, o:o + 512],
            start=st, stop=sp)])
        tok_r2rep = consume(bk6, "scalar", lambda e, o=off: e.copy(
            r2rep[:, o:o + 512], bk6.ps[:, 0:512]))
    pk2.close()

    k2a = [_sb(ph9l, nc, f"k2a{f}", [128, 512], BF16, side="right") for f in range(NF)]
    ph9q = es.enter_context(ExitStack())
    h_q = [_sb(ph9q, nc, f"h_q{i}", [128, 2048]) for i in range(2)]
    hq_i = _sb(ph9q, nc, "hq_i", [128, 2048], I32)
    k2tm = _sb(ph9q, nc, "k2tm", [128, 2048], BF16)

    s_hq = [mksem("s_hq0"), mksem("s_hq1")]
    wait("gpsimd", s_sh[0], tok_h_sp[0])
    wait("gpsimd", s_sh[1], tok_h_sp[1])
    hq_free = [0, 0]
    for tt in range(8):
        dst_list = k2a if tt < 4 else k2b
        dcol = 128 * (tt % 4)
        for q4 in range(4):
            hb = q4 % 2
            qdma, qeng = (gdma, "gpsimd") if hb == 0 else (sdma, "sync")
            wait(qeng, s_v, hq_free[hb])
            tok_hr = qdma(h_q[hb][:], scr["h"][128 * tt:128 * (tt + 1),
                                               2048 * q4:2048 * (q4 + 1)],
                          sem=s_hq[hb])
            wait("vector", s_hq[hb], tok_hr)
            hq_free[hb] = vtok(lambda e, hb=hb, tt=tt: e.tensor_scalar(
                hq_i[:], h_q[hb][:], sx2col[:, tt:tt + 1], None, ALU.mult))
            vbar()
            tok_q2 = vtok(lambda e: e.tensor_scalar(
                k2tm[:], hq_i[:], -128.0, 127.0, ALU.max, ALU.min))
            wait("tensor", s_v, tok_q2)
            for fl in range(16):
                f = 16 * q4 + fl
                hf = (fl % 2) * 128
                if pbt.free is not None:
                    wait("tensor", pbt.free[0], pbt.free[1])
                    pbt.free = None
                pbt.prod = ttok(lambda e, fl=fl, hf=hf: e.transpose(
                    pbt.ps[:, hf:hf + 128], k2tm[:, 128 * fl:128 * (fl + 1)],
                    identb[:]))
                consume(pbt, "scalar", lambda e, f=f, hf=hf, dl=dst_list,
                        dc=dcol: e.copy(dl[f][:, dc:dc + 128],
                                        pbt.ps[:, hf:hf + 128]))
            wait("vector", s_a, s_a.v)
    ph9q.close()

    fence()
    ph9d = es.enter_context(ExitStack())
    wd_f32 = [_sb(ph9d, nc, f"wdf{i}", [128, 2048]) for i in range(3)]
    wd_g1 = _sb(ph9d, nc, "wd_g1", [128, 2048])
    wd_g2 = _sb(ph9d, nc, "wd_g2", [128, 2048])
    wd_t = [_sb(ph9d, nc, f"wdt{i}", [128, 2048], BF16) for i in range(2)]
    xres = [_sb(ph9d, nc, f"xres{i}", [128, 512]) for i in range(2)]
    atres = [_sb(ph9d, nc, f"atres{i}", [128, 512], BF16) for i in range(2)]
    ores = [_sb(ph9d, nc, f"ores{i}", [128, 512]) for i in range(2)]
    otmp = _sb(ph9d, nc, "otmp", [128, 512])

    s_wdf = [mksem(f"s_wdf{i}") for i in range(3)]
    s_xr = [mksem("s_xr0"), mksem("s_xr1")]
    s_ar = [mksem("s_ar0"), mksem("s_ar1")]
    s_ob = [mksem("s_ob0"), mksem("s_ob1")]
    wait("tensor", s_a, s_a.v)
    wait("gpsimd", s_sat[0], tok_att_sp[0])
    wait("gpsimd", s_sat[1], tok_att_sp[1])
    wdf_free = [0, 0, 0]
    wdt_free = [0, 0]
    ores_free = [0, 0]
    res_free = [0, 0]
    for cc in range(16):
        xres_tok = [0, 0]
        atres_tok = [0, 0]
        for th in range(2):
            wait("sync", s_v, res_free[th])
            xres_tok[th] = sdma(
                xres[th][:], dram["x_fm"][128 * cc:128 * (cc + 1),
                                          LV0 + 512 * th:LV0 + 512 * (th + 1)],
                sem=s_xr[th])
            wait("gpsimd", s_v, res_free[th])
            atres_tok[th] = gdma(
                atres[th][:], scr["att"][128 * cc:128 * (cc + 1),
                                         512 * th:512 * (th + 1)], sem=s_ar[th])
        bkth = [pb[2], pb[3]]
        for bx in bkth:
            if bx.free is not None:
                wait("tensor", bx.free[0], bx.free[1])
                bx.free = None
        for q4 in range(4):
            fb = (cc * 4 + q4) % 3
            tb = q4 % 2
            qdma, qeng = (sdma, "sync") if fb % 2 == 0 else (gdma, "gpsimd")
            wait(qeng, s_v, wdf_free[fb])
            qdma(wd_f32[fb][:], dram["wd"][:, cc * 8192 + 2048 * q4:
                                           cc * 8192 + 2048 * (q4 + 1)],
                 sem=s_wdf[fb])
            wait("vector", s_wdf[fb], s_wdf[fb].v)
            wait("vector", s_t, wdt_free[tb])
            vbar()
            v(lambda e, fb=fb: e.tensor_scalar(
                wd_g1[:], wd_f32[fb][:], 0.0, swcol[:, 3:4], ALU.add, ALU.is_gt))
            wdf_free[fb] = vtok(lambda e, fb=fb: e.tensor_scalar(
                wd_g2[:], wd_f32[fb][:], -1.0, swcol[:, 3:4], ALU.mult, ALU.is_gt))
            vbar()
            tok_tern = vtok(lambda e, tb=tb: e.tensor_tensor(
                wd_t[tb][:], wd_g1[:], wd_g2[:], op=ALU.subtract))
            wait("tensor", s_v, tok_tern)
            for fl in range(16):
                f = 16 * q4 + fl
                for th in range(2):
                    src = k2a[f] if th == 0 else k2b[f]
                    st = f == 0
                    sp = f == NF - 1
                    emitfn = (lambda e, fb=tb, fl=fl, th=th, s=src, st=st, sp=sp:
                              e.matmul(bkth[th].ps[:, 0:512],
                                       wd_t[fb][:, 128 * fl:128 * (fl + 1)],
                                       s[:], start=st, stop=sp))
                    if fl == 15 and th == 1:
                        wdt_free[tb] = ttok(emitfn)
                        if sp:
                            bkth[1].prod = wdt_free[tb]
                    elif sp and th == 0:
                        bkth[0].prod = ttok(emitfn)
                    else:
                        t(emitfn)
        for th in range(2):
            wait("vector", s_xr[th], xres_tok[th])
            wait("vector", s_ar[th], atres_tok[th])
            wait("vector", s_ob[th], ores_free[th])
            wait("vector", s_a, tok_r2rep)
            wait("vector", s_t, bkth[th].prod)
            vbar()
            tok1 = vtok(lambda e, th=th: e.tensor_tensor(
                otmp[:], bkth[th].ps[:, 0:512],
                r2rep[:, 512 * th:512 * (th + 1)], op=ALU.mult))
            bkth[th].free = (s_v, tok1)
            vbar()
            v(lambda e, th=th: e.tensor_tensor(otmp[:], otmp[:], xres[th][:],
                                               op=ALU.add))
            vbar()
            tok2 = vtok(lambda e, th=th: e.tensor_tensor(
                ores[th][:], otmp[:], atres[th][:], op=ALU.add))
            res_free[th] = tok2
            wait("sync", s_v, tok2)
            ores_free[th] = sdma(
                out[128 * cc:128 * (cc + 1), 512 * th:512 * (th + 1)],
                ores[th][:], sem=s_ob[th])

    finish()
    return steps


def kernel(**inputs):
    w = _host_weights(inputs)
    x = np.asarray(inputs["L0_attn_h"], np.float32)
    nc = build_nc()
    in_maps = []
    for core in range(8):
        m = dict(w)
        m.update(_host_core(x, core))
        in_maps.append(m)
    res = run_bass_kernel_spmd(nc, in_maps, core_ids=list(range(8)))
    outp = np.empty((B, T, C), np.float32)
    for core in range(8):
        b, half = core // 2, core % 2
        outp[b, half * TL:(half + 1) * TL, :] = res.results[core]["out"].T
    return outp


# version 10
# speedup vs baseline: 1.0247x; 1.0247x over previous
import numpy as np
import ml_dtypes
from contextlib import ExitStack

import concourse.bass as bass
import concourse.mybir as mybir
from concourse.bass_utils import run_bass_kernel_spmd

F32 = mybir.dt.float32
BF16 = mybir.dt.bfloat16
I32 = mybir.dt.int32
AF = mybir.ActivationFunctionType
ALU = mybir.AluOpType
AX = mybir.AxisListType
BF = ml_dtypes.bfloat16

B, T, C = 4, 2048, 2048
TL = 1024
H, D, M = 16, 128, 16
ROPE, THETA, WIN, CAP = 64, 1e4, 512, 50.0
G, QC, INTER, F2 = 4, 512, 2048, 8192
NCH = C // 128
VIRT = 2560
NVT = VIRT // 128
LV0 = 1536
KV0 = 1024
KVN = VIRT - KV0
NKV = KVN // 128
NBV = VIRT // M
QT = TL // 128
NF = F2 // 128
SCALE = 1.0 / float(np.sqrt(D))
LN_EPS, RMS_EPS, QEPS = 1e-5, 1e-6, 1e-5
NEG = -1e30
TWO_PI = float(2 * np.pi)

JORDER = []
for _i in range(16):
    JORDER += [_i, 16 + _i]


def _host_weights(i):
    w = {}
    w["wqc"] = np.ascontiguousarray(i["L0_attn_Wq_c"], np.float32)
    w["wqup"] = np.ascontiguousarray(i["L0_attn_Wq_up"], np.float32)
    w["wk"] = np.ascontiguousarray(i["L0_attn_Wk"], np.float32)
    w["wv"] = np.ascontiguousarray(i["L0_attn_Wv"], np.float32)
    wc = np.asarray(i["L0_attn_W_comp"], np.float32)
    w["wcomp"] = np.ascontiguousarray(
        wc.reshape(256, 128, 128).transpose(1, 0, 2).reshape(128, -1))
    w["wg"] = np.ascontiguousarray(
        np.asarray(i["L0_attn_Wg"], np.float32).reshape(C, C))
    wo = np.asarray(i["L0_attn_Wo"], np.float32)
    w["wo"] = np.ascontiguousarray(
        wo.reshape(64, 128, 16, 128).transpose(1, 2, 0, 3).reshape(128, -1))
    wgu = np.asarray(i["L0_ffn_w_gate_up"], np.float32)
    t = wgu.reshape(32, 512, 16, 128).transpose(3, 0, 2, 1)
    w["wgu"] = np.ascontiguousarray(t[:, JORDER].reshape(128, -1))
    wd = np.asarray(i["L0_ffn_w_down"], np.float32)
    w["wd"] = np.ascontiguousarray(
        wd.reshape(16, 128, 64, 128).transpose(3, 0, 2, 1).reshape(128, -1))
    w["qnw"] = np.ascontiguousarray(
        np.asarray(i["L0_attn_q_norm_w"], np.float32).reshape(128, 1))
    w["knw"] = np.ascontiguousarray(
        np.asarray(i["L0_attn_k_norm_w"], np.float32).reshape(128, 1))
    w["sink"] = np.ascontiguousarray(
        np.asarray(i["L0_attn_sink"], np.float32).reshape(1, H))
    w["ident"] = np.eye(128, dtype=BF)
    w["identf"] = np.eye(128, dtype=np.float32)
    return w


def _host_core(x, core):
    b, half = core // 2, core % 2
    t0 = half * TL
    pr = LV0 - t0
    d = {}
    xtm = np.zeros((VIRT, C), np.float32)
    n = t0 + TL
    xtm[pr:pr + n] = x[b][0:n]
    d["x_tm"] = xtm
    d["x_fm"] = np.ascontiguousarray(xtm.T)
    d["pos"] = np.arange(t0 - 512, t0 - 512 + KVN,
                         dtype=np.float32).reshape(1, -1)
    msw = np.zeros((128, 2 * 8 * 512), BF)
    for ch in range(2):
        for e in range(8):
            s_g = (t0 - 512 + (4 * ch + e) * 128) + np.arange(128)[:, None]
            t_g = (t0 + 512 * ch) + np.arange(512)[None, :]
            ok = (s_g >= 0) & (s_g <= t_g) & (t_g - s_g < WIN)
            msw[:, (ch * 8 + e) * 512:(ch * 8 + e + 1) * 512] = ok.astype(BF)
    d["msw"] = msw
    mcm = np.empty((128, QT * NBV), np.float32)
    for j in range(QT):
        t_g = (t0 + 128 * j) + np.arange(128)[:, None]
        n = np.arange(NBV)[None, :] - (LV0 - t0) // M
        ok = (n >= 0) & (M * n + (M - 1) < t_g)
        mcm[:, j * NBV:(j + 1) * NBV] = np.where(ok, 0.0, NEG)
    d["mc"] = mcm
    return d


ENGS = ("sync", "tensor", "vector", "scalar", "gpsimd")


class Sem:
    __slots__ = ("h", "v", "name")

    def __init__(self, h, name):
        self.h, self.v, self.name = h, 0, name


class Bank:
    __slots__ = ("ps", "prod", "free")

    def __init__(self, ps):
        self.ps = ps
        self.prod = 0
        self.free = None


def _sb(es, nc, name, shape, dt=F32, side=None):
    return es.enter_context(nc.sbuf_tensor(name, shape, dt, side=side))


def build_nc(upto=9, taps=()):
    nc = bass.Bass("TRN2", target_bir_lowering=False, debug=False)
    dram = {}
    for name, shape, dt in [
        ("x_tm", [VIRT, C], F32), ("x_fm", [C, VIRT], F32),
        ("wqc", [C, QC], F32), ("wqup", [QC, C], F32),
        ("wk", [C, D], F32), ("wv", [C, D], F32),
        ("wcomp", [128, 256 * 128], F32), ("wg", [C, C], F32),
        ("wo", [128, 16 * 64 * 128], F32),
        ("wgu", [128, 32 * 16 * 512], F32), ("wd", [128, 16 * 64 * 128], F32),
        ("qnw", [128, 1], F32), ("knw", [128, 1], F32), ("sink", [1, H], F32),
        ("ident", [128, 128], BF16), ("identf", [128, 128], F32),
        ("pos", [1, KVN], F32),
        ("msw", [128, 2 * 8 * 512], BF16), ("mc", [128, QT * NBV], F32),
    ]:
        dram[name] = nc.dram_tensor(name, shape, dt, kind="ExternalInput").ap()
    out = nc.dram_tensor("out", [C, TL], F32, kind="ExternalOutput").ap()
    tap_map = {}
    for tname, tshape, tdt in taps:
        tap_map[tname] = nc.dram_tensor("tap_" + tname, tshape, tdt,
                                        kind="ExternalOutput").ap()
    scr = {
        "k1": nc.dram_tensor("k1_spill", [C, TL], BF16, kind="Internal").ap(),
        "att": nc.dram_tensor("att_spill", [C, TL], BF16, kind="Internal").ap(),
        "h": nc.dram_tensor("h_spill", [8 * 128, F2], F32, kind="Internal").ap(),
    }
    es = ExitStack()
    with es:
        steps = _program(nc, es, dram, out, scr, tap_map, upto)
        block = es.enter_context(nc.Block())
        for eng in ENGS:
            def body(e, _eng=eng):
                for fn in steps[_eng]:
                    fn(e)
            getattr(block, eng)(body)
    return nc


def _program(nc, es, dram, out, scr, tap_map, upto):
    steps = {e: [] for e in ENGS}
    bg = []

    SEMS = {}

    def mksem(name):
        sm = Sem(es.enter_context(nc.semaphore(name)), name)
        SEMS[name] = sm
        return sm

    s_ld = mksem("s_ld")
    s_x = mksem("s_x")
    s_v = mksem("s_v")
    s_a = mksem("s_a")
    s_t = mksem("s_t")
    s_gp = mksem("s_gp")
    s_pp = mksem("s_pp")
    s_pf = mksem("s_pf")
    s_sp = mksem("s_sp")
    s_out = mksem("s_out")
    SEMOF = {"vector": s_v, "scalar": s_a, "tensor": s_t, "gpsimd": s_gp}

    def add(eng, fn):
        steps[eng].append(fn)

    def wait(eng, sem, val):
        if val > 0:
            add(eng, lambda e, s=sem.h, vv=val: e.wait_ge(s, vv))

    def tok(eng, fn):
        sm = SEMOF[eng]
        sm.v += 1
        add(eng, lambda e, f=fn, s=sm.h: f(e).then_inc(s, 1))
        return sm.v

    def v(fn):
        tok("vector", fn)

    def a(fn):
        tok("scalar", fn)

    def t(fn):
        add("tensor", fn)

    def vtok(fn):
        return tok("vector", fn)

    def atok(fn):
        return tok("scalar", fn)

    def ttok(fn):
        return tok("tensor", fn)

    def vbar():
        wait("vector", s_v, s_v.v)

    def abar():
        wait("scalar", s_a, s_a.v)

    def gdma(dst, src, sem=s_ld):
        sem.v += 16
        add("gpsimd", lambda e, d=dst, s=src, h=sem.h:
            e.dma_start(out=d, in_=s).then_inc(h, 16))
        return sem.v

    def sdma(dst, src, sem=s_x):
        sem.v += 16
        add("sync", lambda e, d=dst, s=src, h=sem.h:
            e.dma_start(out=d, in_=s).then_inc(h, 16))
        return sem.v

    def edma(eng, dst, src, sem):
        sem.v += 16
        add(eng, lambda e, d=dst, s=src, h=sem.h:
            e.dma_start(out=d, in_=s).then_inc(h, 16))
        return sem.v

    def drain_bg(n=1):
        for _ in range(n):
            if bg:
                eng, fn, bump = bg.pop(0)
                add(eng, fn)
                if bump is not None:
                    bump[0].v += bump[1]

    def drain_all_bg():
        while bg:
            drain_bg(1)

    def pe_group(bank, emits):
        if bank.free is not None:
            wait("tensor", bank.free[0], bank.free[1])
            bank.free = None
        n = len(emits)
        for idx, f in enumerate(emits):
            st = idx == 0
            if idx == n - 1:
                bank.prod = ttok(lambda e, ff=f, st=st: ff(e, st, True))
            else:
                t(lambda e, ff=f, st=st: ff(e, st, False))
        return bank.prod

    def consume(bank, eng, fn):
        wait(eng, s_t, bank.prod)
        tk = tok(eng, fn)
        bank.free = (SEMOF[eng], tk)
        return tk

    def fence():
        for eng in ENGS:
            for sm in list(SEMS.values()):
                wait(eng, sm, sm.v)

    def tap(name, ap_fn, sem, tk):
        if name in tap_map:
            wait("sync", sem, tk)
            s_out.v += 16
            add("sync", lambda e, d=tap_map[name], f=ap_fn:
                e.dma_start(out=d, in_=f()).then_inc(s_out.h, 16))

    consts = es.enter_context(ExitStack())
    onesb = _sb(consts, nc, "onesb", [128, 1], BF16)
    onesf_r = _sb(consts, nc, "onesf_r", [1, 128], F32)
    onesf_c = _sb(consts, nc, "onesf_c", [128, 1], F32)
    identb = _sb(consts, nc, "identb", [128, 128], BF16)
    identf = _sb(consts, nc, "identf_s", [128, 128], F32)
    qnw = _sb(consts, nc, "qnw_in", [128, 1], F32)
    knw = _sb(consts, nc, "knw_in", [128, 1], F32)
    qnw_s = _sb(consts, nc, "qnw_s", [128, 1], F32)
    sinkrep = _sb(consts, nc, "sinkrep", [128, H], F32)
    expsink = _sb(consts, nc, "expsink", [1, H], F32)
    sx_g = _sb(consts, nc, "sx_g", [128, QT], F32)
    mxh_g = _sb(consts, nc, "mxh_g", [128, QT], F32)
    r1col = _sb(consts, nc, "r1col", [128, QT], F32)
    sx2col = _sb(consts, nc, "sx2col", [128, QT], F32)
    ppcol = _sb(consts, nc, "ppcol", [128, 192], F32)
    swv = _sb(consts, nc, "swv", [1, 8], F32)
    swcol = _sb(consts, nc, "swcol", [128, 8], F32)
    ppfin = _sb(consts, nc, "ppfin", [128, 8], F32)
    ppfin_i = _sb(consts, nc, "ppfin_i", [128, 2], I32)
    ppsums = _sb(consts, nc, "ppsums", [1, 8], F32)

    ropel = es.enter_context(ExitStack())
    cosq = _sb(ropel, nc, "cosq", [32, KVN], F32)
    sinq = _sb(ropel, nc, "sinq", [32, KVN], F32)
    posr_early = _sb(ropel, nc, "posr_e", [1, KVN], F32)
    glive = es.enter_context(ExitStack())
    xn = [_sb(glive, nc, f"xn{c}", [128, VIRT], BF16) for c in range(NCH)]

    pb = [Bank(es.enter_context(nc.psum_tensor(f"pb{k}", [128, 512], F32)))
          for k in range(7)]
    pbt = Bank(es.enter_context(nc.psum_tensor("pbt", [128, 1024], BF16)))

    s_aux = mksem("s_aux")
    gdma(identb[:], dram["ident"], sem=s_aux)
    gdma(identf[:], dram["identf"], sem=s_aux)
    gdma(qnw[:], dram["qnw"], sem=s_aux)
    gdma(knw[:], dram["knw"], sem=s_aux)
    gdma(sinkrep[0:1, 0:H], dram["sink"], sem=s_aux)
    tok_aux = gdma(posr_early[0:1, :], dram["pos"], sem=s_aux)

    wait("vector", s_aux, tok_aux)
    v(lambda e: e.memset(onesb[:], 1.0))
    v(lambda e: e.memset(onesf_r[:], 1.0))
    v(lambda e: e.memset(onesf_c[:], 1.0))
    tok_vc = vtok(lambda e: e.tensor_scalar(qnw_s[:], qnw[:], SCALE, None, ALU.mult))

    wait("tensor", s_v, tok_vc)
    pe_group(pb[6], [lambda e, st, sp: e.matmul(
        pb[6].ps[:, 0:H], onesf_r[:], sinkrep[0:1, 0:H], start=st, stop=sp)])
    tok_srep2 = consume(pb[6], "vector",
                        lambda e: e.tensor_copy(sinkrep[:], pb[6].ps[:, 0:H]))
    wait("scalar", s_v, tok_srep2)
    atok(lambda e: e.activation(expsink[:], sinkrep[0:1, 0:H], AF.Exp))

    ropes = es.enter_context(ExitStack())
    iot = _sb(ropes, nc, "iot", [32, 1], F32)
    posr = posr_early
    angt = _sb(ropes, nc, "angt", [32, KVN], F32)
    ang2 = _sb(ropes, nc, "ang2", [32, KVN], F32)
    angi = _sb(ropes, nc, "angi", [32, KVN], I32)
    angw = _sb(ropes, nc, "angw", [32, KVN], F32)
    invc = _sb(ropes, nc, "invc", [32, 1], F32)
    tok_iot = tok("gpsimd", lambda e: e.iota(
        iot[:], [[0, 1]], base=0, channel_multiplier=1,
        allow_small_or_imprecise_dtypes=True))
    wait("scalar", s_gp, tok_iot)
    LTH = float(np.log(THETA) / 32.0)
    tok_inv = atok(lambda e: e.activation(invc[:], iot[:], AF.Exp, scale=-LTH))
    wait("tensor", s_aux, tok_aux)
    wait("vector", s_a, tok_inv)
    for off in range(0, KVN, 512):
        pe_group(pb[6], [lambda e, st, sp, o=off: e.matmul(
            pb[6].ps[0:32, 0:512], onesf_r[0:1, 0:32], posr[0:1, o:o + 512],
            start=st, stop=sp)])
        consume(pb[6], "vector", lambda e, o=off: e.tensor_scalar(
            angt[:, o:o + 512], pb[6].ps[0:32, 0:512], invc[:, 0:1], None, ALU.mult))

    def range_reduce_sin(dst, shift):
        vbar()
        off = 0.25 if shift else 0.0
        v(lambda e, o=off: e.tensor_scalar(ang2[:], angt[:], 1.0 / TWO_PI, o,
                                           ALU.mult, ALU.add))
        vbar()
        v(lambda e: e.tensor_copy(angi[:], ang2[:]))
        vbar()
        v(lambda e: e.tensor_copy(angw[:], angi[:]))
        vbar()
        v(lambda e: e.tensor_tensor(ang2[:], ang2[:], angw[:], op=ALU.subtract))
        vbar()
        v(lambda e: e.tensor_scalar(angw[:], ang2[:], 0.0, 0.5,
                                    ALU.add, ALU.is_gt))
        vbar()
        v(lambda e: e.tensor_tensor(ang2[:], ang2[:], angw[:], op=ALU.subtract))
        vbar()
        v(lambda e: e.tensor_scalar(angw[:], ang2[:], -1.0, 0.5,
                                    ALU.mult, ALU.is_gt))
        vbar()
        tk = vtok(lambda e: e.tensor_tensor(ang2[:], ang2[:], angw[:],
                                            op=ALU.add))
        wait("scalar", s_v, tk)
        tk2 = atok(lambda e, d=dst: e.activation(d, ang2[:], AF.Sin,
                                                 scale=TWO_PI))
        wait("vector", s_a, tk2)

    range_reduce_sin(sinq[:], False)
    range_reduce_sin(cosq[:], True)
    ropes.close()
    fence()

    ppes = es.enter_context(ExitStack())
    s_ppv = [mksem("s_pp0"), mksem("s_pp1"), mksem("s_pp2")]
    pp_buf = [_sb(ppes, nc, f"pp{i}", [128, 2048], F32, side="right") for i in range(3)]
    pp_specs = [("wgu", k) for k in range(128)] + [("wd", k) for k in range(64)]
    pp_cnt = [0, 0, 0]
    for idx, (wname, k) in enumerate(pp_specs):
        bi = idx % 3
        if idx >= 3:
            bg.append(("gpsimd", lambda e, n=(idx - 2): e.wait_ge(s_pf.h, n),
                       None))
        pp_cnt[bi] += 16
        tk = pp_cnt[bi]
        bg.append(("gpsimd", lambda e, w=wname, kk=k, b=bi:
                   e.dma_start(out=pp_buf[b][:],
                               in_=dram[w][:, kk * 2048:(kk + 1) * 2048])
                   .then_inc(s_ppv[b].h, 16), (s_ppv[bi], 16)))
        bg.append(("vector", lambda e, tt_=tk, b=bi: e.wait_ge(s_ppv[b].h, tt_),
                   None))
        bg.append(("vector", lambda e, b=bi, cix=idx: e.reduce_sum(
            ppcol[:, cix:cix + 1], pp_buf[b][:], axis=AX.X,
            apply_absolute_value=True).then_inc(s_pf.h, 1), (s_pf, 1)))

    ph1 = es.enter_context(ExitStack())
    xt_buf = [_sb(ph1, nc, f"xt{i}", [128, C]) for i in range(3)]
    sum_g = _sb(ph1, nc, "sum_g", [128, NVT])
    sq_g = _sb(ph1, nc, "sq_g", [128, NVT])
    m_g = _sb(ph1, nc, "m_g", [128, NVT])
    nm_g = _sb(ph1, nc, "nm_g", [128, NVT])
    rs_g = _sb(ph1, nc, "rs_g", [128, NVT])
    var_g = _sb(ph1, nc, "var_g", [128, NVT])
    mx_g = _sb(ph1, nc, "mx_g", [128, QT])
    dump = _sb(ph1, nc, "dump", [128, C])
    dump_tok = [0]

    v(lambda e: e.memset(sum_g[:], 0.0))
    tok_z = vtok(lambda e: e.memset(sq_g[:], 0.0))
    wait("scalar", s_v, tok_z)
    s_xt = [mksem(f"s_xt{i}") for i in range(3)]
    buf_free = [0, 0, 0]
    for i in range(NVT):
        bi = i % 3
        wait("sync", s_a, buf_free[bi])
        sdma(xt_buf[bi][:], dram["x_tm"][128 * i:128 * (i + 1), :], sem=s_xt[bi])
        wait("scalar", s_xt[bi], s_xt[bi].v)
        wait("scalar", s_a, dump_tok[0])
        tok_sm = atok(lambda e, b=bi, i=i: e.activation(
            dump[:], xt_buf[b][:], AF.Copy, accum_out=sum_g[:, i:i + 1]))
        dump_tok[0] = tok_sm
        wait("vector", s_a, tok_sm)
        v(lambda e, i=i: e.tensor_scalar(m_g[:, i:i + 1], sum_g[:, i:i + 1],
                                         1.0 / C, None, ALU.mult))
        vbar()
        tok_nm = vtok(lambda e, i=i: e.tensor_scalar(
            nm_g[:, i:i + 1], m_g[:, i:i + 1], -1.0, None, ALU.mult))
        wait("scalar", s_v, tok_nm)
        wait("scalar", s_a, dump_tok[0])
        tok_stats = atok(lambda e, b=bi, i=i: e.activation(
            dump[:], xt_buf[b][:], AF.Square, bias=nm_g[:, i:i + 1], scale=1.0,
            accum_out=sq_g[:, i:i + 1]))
        dump_tok[0] = tok_stats
        buf_free[bi] = tok_stats
        drain_bg(2)

    wait("vector", s_a, tok_stats)
    v(lambda e: e.tensor_scalar(var_g[:], sq_g[:], 1.0 / C, None, ALU.mult))
    vbar()
    tok_var = vtok(lambda e: e.tensor_scalar(var_g[:], var_g[:], LN_EPS,
                                             None, ALU.add))
    wait("scalar", s_v, tok_var)
    tok_sqr = atok(lambda e: e.activation(rs_g[:], var_g[:], AF.Sqrt))
    wait("vector", s_a, tok_sqr)
    tok_rs = vtok(lambda e: e.reciprocal(rs_g[:], rs_g[:]))

    wait("scalar", s_v, tok_rs)
    for j in range(QT):
        i = NVT - QT + j
        bi = j % 3
        wait("sync", s_a, buf_free[bi])
        sdma(xt_buf[bi][:], dram["x_tm"][128 * i:128 * (i + 1), :], sem=s_xt[bi])
        wait("scalar", s_xt[bi], s_xt[bi].v)
        wait("scalar", s_a, dump_tok[0])
        tok_abs = atok(lambda e, b=bi, i=i: e.activation(
            dump[:], xt_buf[b][:], AF.Abs, bias=nm_g[:, i:i + 1], scale=1.0))
        dump_tok[0] = tok_abs
        buf_free[bi] = tok_abs
        wait("vector", s_a, tok_abs)
        tok_mx = vtok(lambda e, j=j: e.reduce_max(mx_g[:, j:j + 1], dump[:],
                                                  axis=AX.X))
        wait("scalar", s_v, tok_mx)
        drain_bg(2)
    vbar()
    v(lambda e: e.tensor_tensor(sx_g[:], mx_g[:], rs_g[:, NVT - QT:NVT],
                                op=ALU.mult))
    vbar()
    v(lambda e: e.tensor_scalar(sx_g[:], sx_g[:], QEPS, None, ALU.max))
    vbar()
    v(lambda e: e.reciprocal(sx_g[:], sx_g[:]))
    vbar()
    tok_sx = vtok(lambda e: e.tensor_scalar(sx_g[:], sx_g[:], 127.0, None,
                                            ALU.mult))

    sxl = es.enter_context(ExitStack())
    sxrow = _sb(sxl, nc, "sxrow", [1, TL], F32, side="right")
    reps = es.enter_context(ExitStack())
    m_rep = _sb(reps, nc, "m_rep", [128, VIRT], F32, side="right")
    rs_rep = _sb(reps, nc, "rs_rep", [128, VIRT], F32, side="right")
    packs = es.enter_context(ExitStack())
    pk_sb = _sb(packs, nc, "pk_sb", [NVT, 128], F32, side="right")
    mrow = _sb(packs, nc, "mrow", [1, VIRT], F32, side="right")
    rsrow = _sb(packs, nc, "rsrow", [1, VIRT], F32, side="right")

    s_pk = mksem("s_pk")
    pack_state = {"last_dma": 0}

    def pack_row(grid_ap, n, row_dst, gate_tok, gate_sem):
        wait("tensor", gate_sem, gate_tok)
        wait("scalar", s_pk, pack_state["last_dma"])
        pe_group(pb[6], [lambda e, st, sp, gap=grid_ap, n=n: e.transpose(
            pb[6].ps[0:n, 0:128], gap, identf[:])])
        tok_ev = consume(pb[6], "scalar",
                         lambda e, n=n: e.copy(pk_sb[0:n, :], pb[6].ps[0:n, 0:128]))
        wait("sync", s_a, tok_ev)
        pack_state["last_dma"] = sdma(row_dst, pk_sb[0:n, :], sem=s_pk)
        return pack_state["last_dma"]

    tap("sumg", lambda: sum_g[:], s_a, tok_stats)
    tap("sqg", lambda: sq_g[:], s_v, tok_rs)
    tap("mg", lambda: m_g[:], s_v, tok_rs)
    tap("rsg", lambda: rs_g[:], s_v, tok_rs)
    tok_mrow = pack_row(m_g[:], NVT, mrow[:], tok_rs, s_v)
    tok_rsrow = pack_row(rs_g[:], NVT, rsrow[:], tok_rs, s_v)
    tok_sxrow = pack_row(sx_g[:], QT, sxrow[:], tok_sx, s_v)
    ph1.close()

    tap("mrow", lambda: mrow[:], s_pk, tok_mrow)
    tap("rsrow", lambda: rsrow[:], s_pk, tok_rsrow)
    tap("sxrow", lambda: sxrow[:], s_pk, tok_sxrow)

    tok_rep_ev = 0
    for row_sb, rep_sb, gate in ((mrow, m_rep, tok_mrow), (rsrow, rs_rep, tok_rsrow)):
        wait("tensor", s_pk, gate)
        for off in range(0, VIRT, 512):
            pe_group(pb[6], [lambda e, st, sp, r=row_sb, o=off: e.matmul(
                pb[6].ps[:, 0:512], onesf_r[:], r[0:1, o:o + 512],
                start=st, stop=sp)])
            tok_rep_ev = consume(pb[6], "scalar", lambda e, d=rep_sb, o=off:
                                 e.copy(d[:, o:o + 512], pb[6].ps[:, 0:512]))

    packs.close()

    fence()
    ph2 = es.enter_context(ExitStack())
    xf_buf = [_sb(ph2, nc, f"xf{i}", [128, VIRT]) for i in range(3)]
    xn_tmp = _sb(ph2, nc, "xn_tmp", [128, VIRT])
    xnf_loc = _sb(ph2, nc, "xnf_loc", [128, TL])
    sx_rep = _sb(ph2, nc, "sx_rep", [128, TL])
    k1q = _sb(ph2, nc, "k1q", [128, TL], I32)
    k1 = [_sb(ph2, nc, f"k1_{c}", [128, TL], BF16) for c in range(2)]
    wait("tensor", s_pk, tok_sxrow)
    for off in range(0, TL, 512):
        pe_group(pb[6], [lambda e, st, sp, o=off: e.matmul(
            pb[6].ps[:, 0:512], onesf_r[:], sxrow[0:1, o:o + 512],
            start=st, stop=sp)])
        tok_sxr = consume(pb[6], "scalar", lambda e, o=off: e.copy(
            sx_rep[:, o:o + 512], pb[6].ps[:, 0:512]))
    wait("vector", s_a, tok_rep_ev)
    wait("vector", s_a, tok_sxr)
    s_xf = [mksem(f"s_xf{i}") for i in range(3)]
    s_sk = [mksem("s_sk0"), mksem("s_sk1")]
    tok_xn = [0] * NCH
    tok_k1 = [0] * NCH
    k1_spill_tok = [0] * NCH
    for cdx in range(NCH):
        bi = cdx % 3
        qdma, qeng = (sdma, "sync") if bi != 1 else (gdma, "gpsimd")
        if cdx >= 3:
            wait(qeng, s_v, tok_xn[cdx - 3])
        qdma(xf_buf[bi][:], dram["x_fm"][128 * cdx:128 * (cdx + 1), :], sem=s_xf[bi])
        wait("vector", s_xf[bi], s_xf[bi].v)
        v(lambda e, b=bi: e.tensor_tensor(xn_tmp[:], xf_buf[b][:], m_rep[:],
                                          op=ALU.subtract))
        vbar()
        tok_xn[cdx] = vtok(lambda e, c=cdx: e.tensor_tensor(
            xn[c][:], xn_tmp[:], rs_rep[:], op=ALU.mult))
        v(lambda e: e.tensor_tensor(xnf_loc[:], xn_tmp[:, LV0:VIRT],
                                    rs_rep[:, LV0:VIRT], op=ALU.mult))
        vbar()
        v(lambda e: e.tensor_tensor(k1q[:], xnf_loc[:], sx_rep[:], op=ALU.mult))
        vbar()
        sk = cdx % 2
        if cdx >= 2:
            wait("vector", s_sk[sk], k1_spill_tok[cdx - 2])
        tok_k1[cdx] = vtok(lambda e, c=sk: e.tensor_scalar(
            k1[c][:], k1q[:], -128.0, 127.0, ALU.max, ALU.min))
        wait("gpsimd", s_v, tok_k1[cdx])
        k1_spill_tok[cdx] = gdma(scr["k1"][128 * cdx:128 * (cdx + 1), :],
                                 k1[sk][:], sem=s_sk[sk])
        drain_bg(4)
    tok_xn_all = tok_xn[-1]
    tok_k1spill = (s_sk[0].v, s_sk[1].v)
    tap("xn0", lambda: xn[0][:], s_v, tok_xn_all)
    tap("k10", lambda: k1[0][:], s_v, tok_k1[0])

    ph2.close()
    reps.close()
    sxl.close()

    def finish():
        drain_all_bg()
        for sname, sm in list(SEMS.items()):
            if sm.v > 0 and sname.startswith(("s_ob", "s_sk", "s_sat", "s_sh",
                                              "s_out", "s_pp")):
                wait("sync", sm, sm.v)

    if upto <= 1:
        finish()
        return steps

    fence()
    qkv = es.enter_context(ExitStack())
    q_bf = [_sb(qkv, nc, f"q{h}", [128, TL], BF16, side="right") for h in range(H)]
    kvl = es.enter_context(ExitStack())
    k_bf = _sb(kvl, nc, "k_bf", [128, KVN], BF16, side="right")
    v_tm = [_sb(kvl, nc, f"vtm{k}", [128, 128], BF16, side="right") for k in range(NKV)]
    ckv = _sb(kvl, nc, "ckv", [128, NBV], BF16, side="right")
    ckt1 = _sb(kvl, nc, "ckt1", [128, 128], BF16, side="right")
    ckt2 = _sb(kvl, nc, "ckt2", [32, 128], BF16, side="right")

    ph4s = es.enter_context(ExitStack())
    qc_sb = [_sb(ph4s, nc, f"qcs{c}", [128, TL], BF16) for c in range(4)]
    pst = _sb(ph4s, nc, "pst", [128, 512])
    psq = _sb(ph4s, nc, "psq", [128, 512])
    rt = [_sb(ph4s, nc, f"rt{i}", [32, 512]) for i in range(5)]
    rrow_sb = _sb(ph4s, nc, "rrow_sb", [1, 512])

    def proj_span_norm(bk_mm, ems, dst_ap, nw_ap, cos_ap, sin_ap, do_rope):
        wait("scalar", s_v, s_v.v)
        pe_group(bk_mm, ems)
        tok_ev = consume(bk_mm, "scalar", lambda e: e.copy(pst[:], bk_mm.ps[:, 0:512]))
        wait("vector", s_a, tok_ev)
        if do_rope:
            v(lambda e: e.tensor_copy(rt[4][:], pst[32:64, :]))
            vbar()
            v(lambda e: e.tensor_tensor(rt[0][:], pst[0:32, :], cos_ap, op=ALU.mult))
            v(lambda e: e.tensor_tensor(rt[1][:], rt[4][:], sin_ap, op=ALU.mult))
            v(lambda e: e.tensor_tensor(rt[2][:], pst[0:32, :], sin_ap, op=ALU.mult))
            v(lambda e: e.tensor_tensor(rt[3][:], rt[4][:], cos_ap, op=ALU.mult))
            vbar()
            v(lambda e: e.tensor_tensor(pst[0:32, :], rt[0][:], rt[1][:],
                                        op=ALU.subtract))
            v(lambda e: e.tensor_tensor(rt[4][:], rt[2][:], rt[3][:],
                                        op=ALU.add))
            vbar()
            tok_pre = vtok(lambda e: e.tensor_copy(pst[32:64, :], rt[4][:]))
            wait("scalar", s_v, tok_pre)
        tok_sq = atok(lambda e: e.activation(psq[:], pst[:], AF.Square))
        wait("tensor", s_a, tok_sq)
        bkr = pb[4]
        if bkr.free is not None:
            wait("tensor", bkr.free[0], bkr.free[1])
            bkr.free = None
        pe_group(bkr, [lambda e, st, sp: e.matmul(
            bkr.ps[0:1, 0:512], onesf_c[:], psq[:], start=st, stop=sp)])
        tok_rr = consume(bkr, "vector", lambda e: e.tensor_scalar(
            rrow_sb[:], bkr.ps[0:1, 0:512], 1.0 / D, RMS_EPS, ALU.mult, ALU.add))
        wait("scalar", s_v, tok_rr)
        tok_rsq = atok(lambda e: e.activation(rrow_sb[:], rrow_sb[:], AF.Sqrt))
        wait("vector", s_a, tok_rsq)
        tok_rrc = vtok(lambda e: e.reciprocal(rrow_sb[:], rrow_sb[:]))
        wait("tensor", s_v, tok_rrc)
        bkp = pb[5]
        if bkp.free is not None:
            wait("tensor", bkp.free[0], bkp.free[1])
            bkp.free = None
        pe_group(bkp, [lambda e, st, sp: e.matmul(
            bkp.ps[:, 0:512], onesf_r[:], rrow_sb[:], start=st, stop=sp)])
        return consume(bkp, "vector", lambda e, d=dst_ap: e.scalar_tensor_tensor(
            d, pst[:], nw_ap, bkp.ps[:, 0:512], ALU.mult, ALU.mult))

    ph4a = es.enter_context(ExitStack())
    wqc_sb = [_sb(ph4a, nc, f"wqc{c}", [128, QC], BF16) for c in range(NCH)]
    s_wqc = mksem("s_wqc")
    for c in range(NCH):
        gdma(wqc_sb[c][:], dram["wqc"][128 * c:128 * (c + 1), :], sem=s_wqc)
    tok_wqc = s_wqc.v
    wait("tensor", s_wqc, tok_wqc)
    wait("tensor", s_v, tok_xn_all)
    for oc in range(4):
        for th in range(2):
            bk = pb[(2 * oc + th) % 2]
            ems = []
            for c in range(NCH):
                ems.append(lambda e, st, sp, bkk=bk, c=c, oc=oc, th=th: e.matmul(
                    bkk.ps[:, 0:512], wqc_sb[c][:, 128 * oc:128 * (oc + 1)],
                    xn[c][:, LV0 + 512 * th:LV0 + 512 * (th + 1)],
                    start=st, stop=sp))
            pe_group(bk, ems)
            consume(bk, "scalar", lambda e, bkk=bk, oc=oc, th=th: e.copy(
                qc_sb[oc][:, 512 * th:512 * (th + 1)], bkk.ps[:, 0:512]))
        drain_bg(2)
    ph4a.close()
    fence()

    ph4b = es.enter_context(ExitStack())
    wqup_sb = [_sb(ph4b, nc, f"wqup{c}", [128, C], BF16) for c in range(4)]
    s_wqup = mksem("s_wqup")
    for c in range(4):
        gdma(wqup_sb[c][:], dram["wqup"][128 * c:128 * (c + 1), :], sem=s_wqup)
    tok_wqup = s_wqup.v
    wait("tensor", s_wqup, tok_wqup)
    COSQ0 = 512
    tok_q = [0] * H
    for h in range(H):
        for th in range(2):
            bk = pb[th]
            ems = []
            for c4 in range(4):
                ems.append(lambda e, st, sp, bkk=bk, c4=c4, h=h, th=th: e.matmul(
                    bkk.ps[:, 0:512], wqup_sb[c4][:, 128 * h:128 * (h + 1)],
                    qc_sb[c4][:, 512 * th:512 * (th + 1)], start=st, stop=sp))
            o0 = COSQ0 + 512 * th
            tok_q[h] = proj_span_norm(
                bk, ems, q_bf[h][:, 512 * th:512 * (th + 1)], qnw_s[:, 0:1],
                cosq[:, o0:o0 + 512], sinq[:, o0:o0 + 512], True)
        drain_bg(2)
    tap("q0", lambda: q_bf[0][:], s_v, tok_q[0])
    ph4b.close()
    fence()

    ph4c = es.enter_context(ExitStack())
    wk_sb = [_sb(ph4c, nc, f"wks{c}", [128, D], BF16) for c in range(NCH)]
    s_wk = mksem("s_wk")
    for c in range(NCH):
        gdma(wk_sb[c][:], dram["wk"][128 * c:128 * (c + 1), :], sem=s_wk)
    tok_wk = s_wk.v
    wait("tensor", s_wk, tok_wk)
    tok_kbf = 0
    for th in range(3):
        bk = pb[th % 2]
        ems = []
        for c in range(NCH):
            ems.append(lambda e, st, sp, bkk=bk, c=c, th=th: e.matmul(
                bkk.ps[:, 0:512], wk_sb[c][:],
                xn[c][:, KV0 + 512 * th:KV0 + 512 * (th + 1)], start=st, stop=sp))
        o0 = 512 * th
        tok_kbf = proj_span_norm(
            bk, ems, k_bf[:, 512 * th:512 * (th + 1)], knw[:, 0:1],
            cosq[:, o0:o0 + 512], sinq[:, o0:o0 + 512], True)
        drain_bg(2)
    tap("kbf", lambda: k_bf[:], s_v, tok_kbf)
    ph4c.close()
    fence()

    ph4d = es.enter_context(ExitStack())
    wv_sb = [_sb(ph4d, nc, f"wvs{c}", [128, D], BF16) for c in range(NCH)]
    v_fm = _sb(ph4d, nc, "v_fm", [128, KVN], BF16)
    s_wv = mksem("s_wv")
    for c in range(NCH):
        gdma(wv_sb[c][:], dram["wv"][128 * c:128 * (c + 1), :], sem=s_wv)
    tok_wv = s_wv.v
    wait("tensor", s_wv, tok_wv)
    for th in range(3):
        bk = pb[th % 2]
        ems = []
        for c in range(NCH):
            ems.append(lambda e, st, sp, bkk=bk, c=c, th=th: e.matmul(
                bkk.ps[:, 0:512], wv_sb[c][:],
                xn[c][:, KV0 + 512 * th:KV0 + 512 * (th + 1)], start=st, stop=sp))
        pe_group(bk, ems)
        consume(bk, "scalar", lambda e, bkk=bk, th=th: e.copy(
            v_fm[:, 512 * th:512 * (th + 1)], bkk.ps[:, 0:512]))
    wait("tensor", s_a, s_a.v)
    tok_vtm = [0] * NKV
    for kt in range(NKV):
        half = (kt % 2) * 128
        if pbt.free is not None:
            wait("tensor", pbt.free[0], pbt.free[1])
            pbt.free = None
        pbt.prod = ttok(lambda e, kt=kt, hf=half: e.transpose(
            pbt.ps[:, hf:hf + 128], v_fm[:, 128 * kt:128 * (kt + 1)], identb[:]))
        tok_vtm[kt] = consume(pbt, "scalar", lambda e, kt=kt, hf=half: e.copy(
            v_tm[kt][:], pbt.ps[:, hf:hf + 128]))
        drain_bg(1)
    ph4d.close()
    fence()

    ph4e = es.enter_context(ExitStack())
    wcp_buf = [_sb(ph4e, nc, f"wcp{i}", [128, 4096], BF16) for i in range(2)]
    ckst = _sb(ph4e, nc, "ckst", [128, NBV])
    cksq = _sb(ph4e, nc, "cksq", [128, NBV])
    ckrow = _sb(ph4e, nc, "ckrow", [1, NBV])
    bk = pb[6]
    if bk.free is not None:
        wait("tensor", bk.free[0], bk.free[1])
        bk.free = None
    s_wcp = [mksem("s_wcp0"), mksem("s_wcp1")]
    tok_wcp = [0] * 8
    mm_wcp = [0] * 8
    tok_wcp[0] = gdma(wcp_buf[0][:], dram["wcomp"][:, 0:4096], sem=s_wcp[0])
    for kk in range(8):
        if kk < 7:
            if kk >= 1:
                wait("gpsimd", s_t, mm_wcp[kk - 1])
            tok_wcp[kk + 1] = gdma(wcp_buf[(kk + 1) % 2][:],
                                   dram["wcomp"][:, 4096 * (kk + 1):4096 * (kk + 2)],
                                   sem=s_wcp[(kk + 1) % 2])
        wait("tensor", s_wcp[kk % 2], tok_wcp[kk])
        for fl in range(32):
            f = 32 * kk + fl
            m_, cdx = f // 16, f % 16
            xn_str = xn[cdx][:, m_:VIRT:M]
            if fl == 31:
                mm_wcp[kk] = ttok(lambda e, b=kk % 2, fl=fl, xs=xn_str,
                                  st=(f == 0), sp=(f == 255): e.matmul(
                    bk.ps[:, 0:NBV], wcp_buf[b][:, 128 * fl:128 * (fl + 1)],
                    xs, start=st, stop=sp))
            else:
                t(lambda e, b=kk % 2, fl=fl, xs=xn_str, st=(f == 0): e.matmul(
                    bk.ps[:, 0:NBV], wcp_buf[b][:, 128 * fl:128 * (fl + 1)],
                    xs, start=st, stop=False))
        drain_bg(1)
    bk.prod = mm_wcp[7]
    tok_ck0 = consume(bk, "scalar", lambda e: e.copy(ckst[:], bk.ps[:, 0:NBV]))
    wait("scalar", s_a, tok_ck0)
    tok_cksq = atok(lambda e: e.activation(cksq[:], ckst[:], AF.Square))
    wait("tensor", s_a, tok_cksq)
    bkr = pb[4]
    if bkr.free is not None:
        wait("tensor", bkr.free[0], bkr.free[1])
        bkr.free = None
    pe_group(bkr, [lambda e, st, sp: e.matmul(
        bkr.ps[0:1, 0:NBV], onesf_c[:], cksq[:], start=st, stop=sp)])
    tok_rr = consume(bkr, "vector", lambda e: e.tensor_scalar(
        ckrow[:], bkr.ps[0:1, 0:NBV], 1.0 / D, RMS_EPS, ALU.mult, ALU.add))
    wait("scalar", s_v, tok_rr)
    tok_rsq = atok(lambda e: e.activation(ckrow[:], ckrow[:], AF.Sqrt))
    wait("vector", s_a, tok_rsq)
    tok_rrc = vtok(lambda e: e.reciprocal(ckrow[:], ckrow[:]))
    wait("tensor", s_v, tok_rrc)
    bkp = pb[5]
    if bkp.free is not None:
        wait("tensor", bkp.free[0], bkp.free[1])
        bkp.free = None
    pe_group(bkp, [lambda e, st, sp: e.matmul(
        bkp.ps[:, 0:NBV], onesf_r[:], ckrow[:], start=st, stop=sp)])
    tok_ckv = consume(bkp, "vector", lambda e: e.scalar_tensor_tensor(
        ckv[:], ckst[:], knw[:, 0:1], bkp.ps[:, 0:NBV], ALU.mult, ALU.mult))
    wait("tensor", s_v, tok_ckv)
    if pbt.free is not None:
        wait("tensor", pbt.free[0], pbt.free[1])
        pbt.free = None
    t(lambda e: e.transpose(pbt.ps[:, 0:128], ckv[:, 0:128], identb[:]))
    pbt.prod = ttok(lambda e: e.transpose(pbt.ps[0:32, 128:256], ckv[:, 128:160],
                                          identb[:]))
    wait("scalar", s_t, pbt.prod)
    a(lambda e: e.copy(ckt1[:], pbt.ps[:, 0:128]))
    tok_ckt2 = atok(lambda e: e.copy(ckt2[:], pbt.ps[0:32, 128:256]))
    pbt.free = (s_a, tok_ckt2)
    tap("ckv", lambda: ckv[:], s_v, tok_ckv)
    ph4e.close()
    ph4s.close()
    glive.close()
    ropel.close()
    if upto <= 2:
        finish()
        return steps

    fence()
    attcons = es.enter_context(ExitStack())
    s_msk = mksem("s_msk")
    msw = _sb(attcons, nc, "msw_s", [128, 2 * 8 * 512], BF16)
    mc = _sb(attcons, nc, "mc_s", [128, QT * NBV], F32)
    gdma(msw[:], dram["msw"], sem=s_msk)
    tok_masks = gdma(mc[:], dram["mc"], sem=s_msk)
    wait("vector", s_msk, tok_masks)

    ao = q_bf
    ph5 = es.enter_context(ExitStack())
    csc = [_sb(ph5, nc, f"csc{i}", [128, NBV]) for i in range(2)]
    pexp = [_sb(ph5, nc, f"pexp{i}", [128, NBV]) for i in range(2)]
    pn_sb = [_sb(ph5, nc, f"pn{i}", [128, NBV], BF16) for i in range(2)]
    ptA = [_sb(ph5, nc, f"ptA{i}", [128, 128], BF16) for i in range(2)]
    ptB = [_sb(ph5, nc, f"ptB{i}", [32, 128], BF16) for i in range(2)]
    mxc = _sb(ph5, nc, "mxc", [128, 4])
    u_sb = [_sb(ph5, nc, f"usb{i}", [128, 512]) for i in range(2)]
    ex_sb = [_sb(ph5, nc, f"exsb{i}", [128, 512], BF16) for i in range(2)]
    drow = _sb(ph5, nc, "drow", [1, 512])
    rrep = _sb(ph5, nc, "rrep", [128, 512])
    swtmp = _sb(ph5, nc, "swtmp", [128, 512])
    ao_c = [_sb(ph5, nc, f"aoc{i}", [128, 512]) for i in range(2)]

    wait("tensor", s_a, s_a.v)
    comp_gate = [0, 0]
    aoc_free = [0, 0]
    for h in range(H):
        for ch in range(2):
            for jj in range(4):
                j = 4 * ch + jj
                par = jj % 2
                bk2 = pb[2]
                pe_group(bk2, [lambda e, st, sp, h=h, j=j: e.matmul(
                    bk2.ps[:, 0:NBV], q_bf[h][:, 128 * j:128 * (j + 1)],
                    ckv[:], start=st, stop=sp)])
                wait("vector", s_t, comp_gate[par])
                tok_csc = consume(bk2, "vector", lambda e, p=par, j=j: e.tensor_tensor(
                    csc[p][:], bk2.ps[:, 0:NBV], mc[:, NBV * j:NBV * (j + 1)],
                    op=ALU.add))
                vbar()
                v(lambda e, p=par: e.reduce_max(mxc[:, 0:1], csc[p][:], axis=AX.X))
                vbar()
                v(lambda e, h=h: e.tensor_tensor(mxc[:, 0:1], mxc[:, 0:1],
                                                 sinkrep[:, h:h + 1], op=ALU.max))
                vbar()
                tok_mx = vtok(lambda e: e.tensor_scalar(mxc[:, 1:2], mxc[:, 0:1],
                                                        -1.0, None, ALU.mult))
                wait("scalar", s_v, tok_mx)
                tok_pexp = atok(lambda e, p=par: e.activation(
                    pexp[p][:], csc[p][:], AF.Exp, bias=mxc[:, 1:2]))
                a(lambda e, h=h: e.activation(mxc[:, 2:3], mxc[:, 0:1], AF.Exp,
                                              bias=sinkrep[:, h:h + 1], scale=-1.0))
                wait("vector", s_a, s_a.v)
                v(lambda e, p=par: e.reduce_sum(mxc[:, 3:4], pexp[p][:], axis=AX.X))
                vbar()
                v(lambda e: e.tensor_tensor(mxc[:, 3:4], mxc[:, 3:4], mxc[:, 2:3],
                                            op=ALU.add))
                vbar()
                v(lambda e: e.reciprocal(mxc[:, 3:4], mxc[:, 3:4]))
                vbar()
                tok_pn = vtok(lambda e, p=par: e.tensor_scalar(
                    pn_sb[p][:], pexp[p][:], mxc[:, 3:4], None, ALU.mult))
                wait("tensor", s_v, tok_pn)
                if pbt.free is not None:
                    wait("tensor", pbt.free[0], pbt.free[1])
                    pbt.free = None
                t(lambda e, p=par: e.transpose(pbt.ps[:, 0:128], pn_sb[p][:, 0:128],
                                               identb[:]))
                pbt.prod = ttok(lambda e, p=par: e.transpose(
                    pbt.ps[0:32, 128:256], pn_sb[p][:, 128:160], identb[:]))
                wait("scalar", s_t, pbt.prod)
                a(lambda e, p=par: e.copy(ptA[p][:], pbt.ps[:, 0:128]))
                tok_pt = atok(lambda e, p=par: e.copy(ptB[p][:], pbt.ps[0:32, 128:256]))
                pbt.free = (s_a, tok_pt)
                bk3 = pb[3]
                if jj == 0 and bk3.free is not None:
                    wait("tensor", bk3.free[0], bk3.free[1])
                    bk3.free = None
                wait("tensor", s_a, tok_pt)
                t(lambda e, p=par, jj=jj: e.matmul(
                    bk3.ps[:, 128 * jj:128 * (jj + 1)], ckt1[:], ptA[p][:],
                    start=True, stop=False))
                comp_gate[par] = ttok(lambda e, p=par, jj=jj: e.matmul(
                    bk3.ps[:, 128 * jj:128 * (jj + 1)], ckt2[:], ptB[p][:],
                    start=False, stop=True))
                bk3.prod = comp_gate[par]
            cpar = (2 * h + ch) % 2
            wait("scalar", s_t, bk3.prod)
            wait("scalar", s_v, aoc_free[cpar])
            tok_evc = atok(lambda e, cp=cpar: e.copy(ao_c[cp][:],
                                                     bk3.ps[:, 0:512]))
            bk3.free = (s_a, tok_evc)
            bk4, bk5 = pb[4], pb[5]
            if bk4.free is not None:
                wait("tensor", bk4.free[0], bk4.free[1])
                bk4.free = None
            if bk5.free is not None:
                wait("tensor", bk5.free[0], bk5.free[1])
                bk5.free = None
            tanh_tok = [0] * 8
            mask_tok = [0] * 8
            for e_i in range(8):
                kt = 4 * ch + e_i
                par = e_i % 2
                bks = pb[par]
                if e_i >= 2:
                    wait("tensor", s_a, tanh_tok[e_i - 2])
                elif bks.free is not None:
                    wait("tensor", bks.free[0], bks.free[1])
                    bks.free = None
                bks.prod = ttok(lambda e, h=h, ch=ch, kt=kt, p=par: e.matmul(
                    pb[p].ps[:, 0:512], k_bf[:, 128 * kt:128 * (kt + 1)],
                    q_bf[h][:, 512 * ch:512 * (ch + 1)], start=True, stop=True))
                wait("scalar", s_t, bks.prod)
                if e_i >= 2:
                    wait("scalar", s_t, s_t.v)
                tanh_tok[e_i] = atok(lambda e, p=par: e.activation(
                    u_sb[p][:], pb[p].ps[:, 0:512], AF.Tanh, scale=1.0 / CAP))
                abar()
                tok_ex = atok(lambda e, p=par: e.activation(
                    ex_sb[p][:], u_sb[p][:], AF.Exp, scale=CAP))
                wait("vector", s_a, tok_ex)
                mask_tok[e_i] = vtok(lambda e, p=par, ch=ch, e_i=e_i: e.tensor_tensor(
                    ex_sb[p][:], ex_sb[p][:],
                    msw[:, (ch * 8 + e_i) * 512:(ch * 8 + e_i + 1) * 512],
                    op=ALU.mult))
                wait("tensor", s_v, mask_tok[e_i])
                t(lambda e, p=par, e_i=e_i: e.matmul(
                    bk4.ps[0:1, 0:512], onesb[:], ex_sb[p][:],
                    start=(e_i == 0), stop=(e_i == 7)))
                last_pv = ttok(lambda e, p=par, e_i=e_i, kt=kt: e.matmul(
                    bk5.ps[:, 0:512], v_tm[kt][:], ex_sb[p][:],
                    start=(e_i == 0), stop=(e_i == 7)))
            bk4.prod = last_pv
            bk5.prod = last_pv
            pb[0].free = (s_a, tanh_tok[6])
            pb[1].free = (s_a, tanh_tok[7])
            wait("vector", s_t, last_pv)
            v(lambda e, h=h: e.tensor_scalar(drow[:], bk4.ps[0:1, 0:512],
                                             expsink[0:1, h:h + 1], None, ALU.add))
            vbar()
            tok_dr = vtok(lambda e: e.reciprocal(drow[:], drow[:]))
            bk4.free = (s_v, tok_dr)
            wait("tensor", s_v, tok_dr)
            bk6 = pb[6]
            if bk6.free is not None:
                wait("tensor", bk6.free[0], bk6.free[1])
                bk6.free = None
            pe_group(bk6, [lambda e, st, sp: e.matmul(
                bk6.ps[:, 0:512], onesf_r[:], drow[:], start=st, stop=sp)])
            tok_rrep = consume(bk6, "scalar", lambda e: e.copy(rrep[:],
                                                               bk6.ps[:, 0:512]))
            wait("vector", s_a, tok_rrep)
            tok_sw = vtok(lambda e: e.tensor_tensor(swtmp[:], bk5.ps[:, 0:512],
                                                    rrep[:], op=ALU.mult))
            bk5.free = (s_v, tok_sw)
            vbar()
            wait("vector", s_a, tok_evc)
            tok_ao = vtok(lambda e, h=h, ch=ch, cp=cpar: e.tensor_tensor(
                ao[h][:, 512 * ch:512 * (ch + 1)], swtmp[:], ao_c[cp][:],
                op=ALU.add))
            aoc_free[cpar] = tok_ao
            drain_bg(10)
    tok_ao_all = tok_ao
    tap("ao0", lambda: ao[0][:], s_v, tok_ao_all)

    ph5.close()
    attcons.close()
    kvl.close()

    if upto <= 3:
        finish()
        return steps

    fence()
    ph6l = es.enter_context(ExitStack())
    inter = [_sb(ph6l, nc, f"it{k}", [128, TL], BF16) for k in range(64)]
    ph6 = es.enter_context(ExitStack())
    s_wg = mksem("s_wg")
    wg_sb = [_sb(ph6, nc, f"wgsb{i}", [128, C], BF16) for i in range(4)]
    wg_mm_done = 0
    for g_ in range(4):
        wait("gpsimd", s_t, wg_mm_done)
        tok_wg = 0
        for ic in range(4):
            tok_wg = gdma(wg_sb[ic][:],
                          dram["wg"][128 * (4 * g_ + ic):128 * (4 * g_ + ic + 1), :],
                          sem=s_wg)
        wait("tensor", s_wg, tok_wg)
        for oc in range(16):
            for th in range(2):
                bk = pb[(2 * oc + th) % 2]
                ems = []
                for ic in range(4):
                    ems.append(lambda e, st, sp, ic=ic, g_=g_, oc=oc, th=th,
                               bkk=bk: e.matmul(
                        bkk.ps[:, 0:512], wg_sb[ic][:, 128 * oc:128 * (oc + 1)],
                        ao[4 * g_ + ic][:, 512 * th:512 * (th + 1)], start=st, stop=sp))
                pe_group(bk, ems)
                consume(bk, "scalar", lambda e, bkk=bk, g_=g_, oc=oc, th=th: e.copy(
                    inter[16 * g_ + oc][:, 512 * th:512 * (th + 1)], bkk.ps[:, 0:512]))
            drain_bg(4)
        wg_mm_done = s_t.v
    tap("inter0", lambda: inter[0][:], s_a, s_a.v)

    ph6.close()
    qkv.close()

    fence()
    ph7 = es.enter_context(ExitStack())
    wo_sb = [_sb(ph7, nc, f"wosb{i}", [128, 8192], BF16) for i in range(2)]
    att_st = [_sb(ph7, nc, f"attst{i}", [128, 512], BF16) for i in range(2)]
    s_wo = [mksem("s_wo0"), mksem("s_wo1")]
    s_sat = [mksem("s_sat0"), mksem("s_sat1")]
    tok_wo = [0] * 16
    tok_wo[0] = gdma(wo_sb[0][:], dram["wo"][:, 0:8192], sem=s_wo[0])
    tok_wo[1] = gdma(wo_sb[1][:], dram["wo"][:, 8192:16384], sem=s_wo[1])
    att_st_free = [0, 0]
    wo_mm_done = [0] * 16
    for cc in range(16):
        slot = cc % 2
        wait("tensor", s_wo[slot], tok_wo[cc])
        for th in range(2):
            bk = pb[2 + th]
            ems = []
            for k in range(64):
                ems.append(lambda e, st, sp, s_=slot, k=k, th=th, bkk=bk: e.matmul(
                    bkk.ps[:, 0:512], wo_sb[s_][:, 128 * k:128 * (k + 1)],
                    inter[k][:, 512 * th:512 * (th + 1)], start=st, stop=sp))
            pe_group(bk, ems)
            stp = (2 * cc + th) % 2
            wait("scalar", s_sat[stp], att_st_free[stp])
            tok_ev = consume(bk, "scalar", lambda e, bkk=bk, p=stp: e.copy(
                att_st[p][:], bkk.ps[:, 0:512]))
            wait("gpsimd", s_a, tok_ev)
            att_st_free[stp] = gdma(
                scr["att"][128 * cc:128 * (cc + 1), 512 * th:512 * (th + 1)],
                att_st[stp][:], sem=s_sat[stp])
        wo_mm_done[cc] = s_t.v
        if cc + 2 < 16:
            wait("gpsimd", s_t, wo_mm_done[cc])
            tok_wo[cc + 2] = gdma(wo_sb[slot][:],
                                  dram["wo"][:, 8192 * (cc + 2):8192 * (cc + 3)],
                                  sem=s_wo[slot])
        drain_bg(6)
    tok_att_sp = (s_sat[0].v, s_sat[1].v)
    if "att0" in tap_map:
        wait("sync", s_sat[0], tok_att_sp[0])
        wait("sync", s_sat[1], tok_att_sp[1])
        s_out.v += 16
        add("sync", lambda e: e.dma_start(out=tap_map["att0"],
                                          in_=scr["att"][0:128, :])
            .then_inc(s_out.h, 16))

    ph7.close()
    ph6l.close()

    if upto <= 4:
        finish()
        return steps

    drain_all_bg()
    wait("vector", s_pf, s_pf.v)
    v(lambda e: e.reduce_sum(swcol[:, 0:1], ppcol[:, 0:128], axis=AX.X))
    tok_ws = vtok(lambda e: e.reduce_sum(swcol[:, 1:2], ppcol[:, 128:192], axis=AX.X))
    vbar()
    wait("tensor", s_v, tok_ws)
    v(lambda e: e.reduce_sum(ppfin[:, 0:1], ppcol[:, 0:128], axis=AX.X))
    tok_cs = vtok(lambda e: e.reduce_sum(ppfin[:, 1:2], ppcol[:, 128:192],
                                         axis=AX.X))
    vbar()
    v(lambda e: e.tensor_scalar(ppfin_i[:, 0:2], ppfin[:, 0:2], 1.0 / 64.0,
                                None, ALU.mult))
    vbar()
    v(lambda e: e.tensor_copy(ppfin[:, 2:4], ppfin_i[:, 0:2]))
    vbar()
    tok_hilo = vtok(lambda e: e.scalar_tensor_tensor(
        ppfin[:, 4:6], ppfin[:, 2:4], -64.0, ppfin[:, 0:2], ALU.mult, ALU.add))
    wait("tensor", s_v, tok_hilo)
    bk6 = pb[6]
    pe_group(bk6, [
        lambda e, st, sp: e.matmul(bk6.ps[0:1, 0:2], onesf_c[:],
                                   ppfin[:, 2:4], start=True, stop=True),
        lambda e, st, sp: e.matmul(bk6.ps[0:1, 2:4], onesf_c[:],
                                   ppfin[:, 4:6], start=True, stop=True),
    ])
    tok_sums = consume(bk6, "vector", lambda e: e.tensor_copy(
        ppsums[0:1, 0:4], bk6.ps[0:1, 0:4]))
    vbar()
    v(lambda e: e.scalar_tensor_tensor(ppsums[0:1, 4:6], ppsums[0:1, 0:2],
                                       64.0, ppsums[0:1, 2:4], ALU.mult, ALU.add))
    vbar()
    v(lambda e: e.memset(swv[:], 0.0))
    vbar()
    v(lambda e: e.tensor_scalar(swv[0:1, 0:1], ppsums[0:1, 4:5],
                                1.0 / (2.0 * F2 * C), None, ALU.mult))
    v(lambda e: e.tensor_scalar(swv[0:1, 2:3], ppsums[0:1, 5:6],
                                1.0 / (C * F2), None, ALU.mult))
    vbar()
    v(lambda e: e.tensor_scalar(swv[0:1, 0:1], swv[0:1, 0:1], QEPS, None, ALU.max))
    v(lambda e: e.tensor_scalar(swv[0:1, 2:3], swv[0:1, 2:3], QEPS, None, ALU.max))
    vbar()
    v(lambda e: e.reciprocal(swv[0:1, 1:2], swv[0:1, 0:1]))
    v(lambda e: e.tensor_scalar(swv[0:1, 3:4], swv[0:1, 2:3], 0.5,
                                float(1.0 + 2.0 ** -24), ALU.mult, ALU.mult))
    vbar()
    tok_swv = vtok(lambda e: e.tensor_scalar(swv[0:1, 4:5], swv[0:1, 3:4],
                                             -1.0, None, ALU.mult))
    wait("tensor", s_v, tok_swv)
    pe_group(bk6, [lambda e, st, sp: e.matmul(
        bk6.ps[:, 0:8], onesf_r[:], swv[0:1, 0:8], start=st, stop=sp)])
    tok_swc = consume(bk6, "vector", lambda e: e.tensor_copy(swcol[:, 0:8],
                                                             bk6.ps[:, 0:8]))
    vbar()
    v(lambda e: e.reciprocal(r1col[:], sx_g[:]))
    vbar()
    tok_r1 = vtok(lambda e: e.tensor_scalar(r1col[:], r1col[:], swcol[:, 0:1],
                                            None, ALU.mult))
    tap("swv", lambda: swv[:], s_v, tok_swv)
    tap("ppfin", lambda: ppfin[:], s_v, tok_swv)
    ppes.close()

    fence()
    ph8 = es.enter_context(ExitStack())
    k1r = [_sb(ph8, nc, f"k1r{c}", [128, TL], BF16) for c in range(NCH)]
    wgu_f32 = [_sb(ph8, nc, f"wguf{i}", [128, 2048]) for i in range(4)]
    wgu_i = _sb(ph8, nc, "wgu_i", [128, 2048], I32)
    wgu_t = [_sb(ph8, nc, f"wgut{i}", [128, 8192], BF16) for i in range(2)]
    a_st = [_sb(ph8, nc, f"ast{i}", [128, 512], F32) for i in range(QT)]
    h_st = [_sb(ph8, nc, f"hst{i}", [128, 512], F32) for i in range(2)]
    hmx = _sb(ph8, nc, "hmx", [128, QT])

    s_k1r = mksem("s_k1r")
    s_wgf = [mksem(f"s_wgf{i}") for i in range(4)]
    s_sh = [mksem("s_sh0"), mksem("s_sh1")]
    wait("gpsimd", s_sk[0], tok_k1spill[0])
    wait("gpsimd", s_sk[1], tok_k1spill[1])
    tok_k1r = [0] * NCH
    for c in range(NCH):
        tok_k1r[c] = gdma(k1r[c][:], scr["k1"][128 * c:128 * (c + 1), :],
                          sem=s_k1r)
    v(lambda e: e.memset(mxh_g[:], 0.0))
    wait("tensor", s_k1r, tok_k1r[-1])
    wait("tensor", s_v, tok_r1)

    GROUPS = [([0, 1, 2, 3], [0, 1, 2, 3]),
              ([4, 5, 6], [4, 5, 6]),
              ([7], [0])]
    tern_tok = [0, 0]
    mm_done_tok = [0, 0]
    h_st_free = [0, 0]
    f32_free = [0, 0, 0, 0]
    silu_tok = [0] * QT
    a_st_free = [0] * QT
    for jj in range(32):
        par = jj % 2
        is_gate = jj % 2 == 0
        pair = jj // 2
        wait("vector", s_t, mm_done_tok[par])
        for q4 in range(4):
            fb = q4
            qdma, qeng = (sdma, "sync") if fb % 2 == 0 else (gdma, "gpsimd")
            wait(qeng, s_v, f32_free[fb])
            qdma(wgu_f32[fb][:], dram["wgu"][:, jj * 8192 + 2048 * q4:
                                             jj * 8192 + 2048 * (q4 + 1)],
                 sem=s_wgf[fb])
            wait("vector", s_wgf[fb], s_wgf[fb].v)
            if jj == 0 and q4 == 0:
                wait("vector", s_v, tok_swc)
            vbar()
            f32_free[fb] = vtok(lambda e, fb=fb: e.tensor_scalar(
                wgu_i[:], wgu_f32[fb][:], swcol[:, 1:2], None, ALU.mult))
            vbar()
            tern_tok[par] = vtok(lambda e, p=par, q4=q4: e.tensor_scalar(
                wgu_t[p][:, 2048 * q4:2048 * (q4 + 1)], wgu_i[:],
                -1.0, 1.0, ALU.max, ALU.min))
        wait("tensor", s_v, tern_tok[par])
        for tts, bankidx in GROUPS:
            bset = [pb[bx] for bx in bankidx]
            for bx in bset:
                if bx.free is not None:
                    wait("tensor", bx.free[0], bx.free[1])
                    bx.free = None
            for cc in range(NCH):
                for ti, tt in enumerate(tts):
                    bkx = bset[ti]
                    last = (cc == NCH - 1) and (ti == len(tts) - 1)
                    emitfn = (lambda e, bkk=bkx, c=cc, tt=tt, p=par,
                              st=(cc == 0), sp=(cc == NCH - 1): e.matmul(
                        bkk.ps[:, 0:512], k1r[c][:, 128 * tt:128 * (tt + 1)],
                        wgu_t[p][:, 512 * c:512 * (c + 1)], start=st, stop=sp))
                    if last:
                        mm_done_tok[par] = ttok(emitfn)
                        for bx in bset:
                            bx.prod = mm_done_tok[par]
                    else:
                        t(emitfn)
            for ti, tt in enumerate(tts):
                bkx = bset[ti]
                if is_gate:
                    wait("scalar", s_v, a_st_free[tt])
                    silu_tok[tt] = consume(bkx, "scalar", lambda e, bkk=bkx, tt=tt:
                                           e.activation(a_st[tt][:],
                                                        bkk.ps[:, 0:512], AF.Silu,
                                                        scale=r1col[:, tt:tt + 1]))
                else:
                    sp_ = tt % 2
                    wait("vector", s_a, silu_tok[tt])
                    wait("vector", s_sh[sp_], h_st_free[sp_])
                    a_st_free[tt] = consume(bkx, "vector",
                                            lambda e, bkk=bkx, tt=tt, p=sp_:
                                            e.scalar_tensor_tensor(
                        h_st[p][:], bkk.ps[:, 0:512], r1col[:, tt:tt + 1],
                        a_st[tt][:], ALU.mult, ALU.mult))
                    vbar()
                    v(lambda e, tt=tt, p=sp_: e.reduce_max(
                        hmx[:, tt:tt + 1], h_st[p][:], axis=AX.X,
                        apply_absolute_value=True))
                    vbar()
                    tok_hmx = vtok(lambda e, tt=tt: e.tensor_tensor(
                        mxh_g[:, tt:tt + 1], mxh_g[:, tt:tt + 1],
                        hmx[:, tt:tt + 1], op=ALU.max))
                    wait("scalar", s_v, tok_hmx)
                    h_st_free[sp_] = edma(
                        "scalar",
                        scr["h"][128 * tt:128 * (tt + 1),
                                 512 * pair:512 * (pair + 1)],
                        h_st[sp_][:], sem=s_sh[sp_])
    tok_h_sp = (s_sh[0].v, s_sh[1].v)

    ph8.close()

    if upto <= 5:
        finish()
        return steps

    fence()
    ph9l = es.enter_context(ExitStack())
    sx2row = _sb(ph9l, nc, "sx2row", [1, TL], F32, side="right")
    r2row = _sb(ph9l, nc, "r2row", [1, TL], F32, side="right")
    r2rep = _sb(ph9l, nc, "r2rep", [128, TL], F32, side="right")
    k2b = [_sb(ph9l, nc, f"k2b{f}", [128, 512], BF16, side="right") for f in range(NF)]

    vbar()
    v(lambda e: e.tensor_scalar(sx2col[:], mxh_g[:], QEPS, None, ALU.max))
    vbar()
    v(lambda e: e.reciprocal(sx2col[:], sx2col[:]))
    vbar()
    tok_sx2 = vtok(lambda e: e.tensor_scalar(sx2col[:], sx2col[:], 127.0,
                                             None, ALU.mult))
    pk2 = es.enter_context(ExitStack())
    pk2_sb = _sb(pk2, nc, "pk2_sb", [QT, 128], F32, side="right")
    wait("tensor", s_v, tok_sx2)
    bk6 = pb[6]
    if bk6.free is not None:
        wait("tensor", bk6.free[0], bk6.free[1])
        bk6.free = None
    pe_group(bk6, [lambda e, st, sp: e.transpose(bk6.ps[0:QT, 0:128], sx2col[:],
                                                 identf[:])])
    tok_pk2 = consume(bk6, "scalar", lambda e: e.copy(pk2_sb[:],
                                                      bk6.ps[0:QT, 0:128]))
    wait("sync", s_a, tok_pk2)
    tok_sx2row = sdma(sx2row[:], pk2_sb[:], sem=s_pk)
    wait("vector", s_pk, tok_sx2row)
    v(lambda e: e.reciprocal(r2row[:], sx2row[:]))
    vbar()
    tok_r2 = vtok(lambda e: e.tensor_scalar(r2row[:], r2row[:], swv[0:1, 2:3],
                                            None, ALU.mult))
    wait("tensor", s_v, tok_r2)
    tok_r2rep = 0
    for off in range(0, TL, 512):
        pe_group(bk6, [lambda e, st, sp, o=off: e.matmul(
            bk6.ps[:, 0:512], onesf_r[:], r2row[0:1, o:o + 512],
            start=st, stop=sp)])
        tok_r2rep = consume(bk6, "scalar", lambda e, o=off: e.copy(
            r2rep[:, o:o + 512], bk6.ps[:, 0:512]))
    pk2.close()

    k2a = [_sb(ph9l, nc, f"k2a{f}", [128, 512], BF16, side="right") for f in range(NF)]
    ph9q = es.enter_context(ExitStack())
    h_q = [_sb(ph9q, nc, f"h_q{i}", [128, 2048]) for i in range(2)]
    hq_i = _sb(ph9q, nc, "hq_i", [128, 2048], I32)
    k2tm = _sb(ph9q, nc, "k2tm", [128, 2048], BF16)

    s_hq = [mksem("s_hq0"), mksem("s_hq1")]
    wait("gpsimd", s_sh[0], tok_h_sp[0])
    wait("gpsimd", s_sh[1], tok_h_sp[1])
    hq_free = [0, 0]
    for tt in range(8):
        dst_list = k2a if tt < 4 else k2b
        dcol = 128 * (tt % 4)
        for q4 in range(4):
            hb = q4 % 2
            qdma, qeng = (gdma, "gpsimd") if hb == 0 else (sdma, "sync")
            wait(qeng, s_v, hq_free[hb])
            tok_hr = qdma(h_q[hb][:], scr["h"][128 * tt:128 * (tt + 1),
                                               2048 * q4:2048 * (q4 + 1)],
                          sem=s_hq[hb])
            wait("vector", s_hq[hb], tok_hr)
            hq_free[hb] = vtok(lambda e, hb=hb, tt=tt: e.tensor_scalar(
                hq_i[:], h_q[hb][:], sx2col[:, tt:tt + 1], None, ALU.mult))
            vbar()
            tok_q2 = vtok(lambda e: e.tensor_scalar(
                k2tm[:], hq_i[:], -128.0, 127.0, ALU.max, ALU.min))
            wait("tensor", s_v, tok_q2)
            for fl in range(16):
                f = 16 * q4 + fl
                hf = (fl % 2) * 128
                if pbt.free is not None:
                    wait("tensor", pbt.free[0], pbt.free[1])
                    pbt.free = None
                pbt.prod = ttok(lambda e, fl=fl, hf=hf: e.transpose(
                    pbt.ps[:, hf:hf + 128], k2tm[:, 128 * fl:128 * (fl + 1)],
                    identb[:]))
                consume(pbt, "scalar", lambda e, f=f, hf=hf, dl=dst_list,
                        dc=dcol: e.copy(dl[f][:, dc:dc + 128],
                                        pbt.ps[:, hf:hf + 128]))
            wait("vector", s_a, s_a.v)
    ph9q.close()

    fence()
    ph9d = es.enter_context(ExitStack())
    wd_f32 = [_sb(ph9d, nc, f"wdf{i}", [128, 2048]) for i in range(3)]
    wd_g1 = _sb(ph9d, nc, "wd_g1", [128, 2048])
    wd_g2 = _sb(ph9d, nc, "wd_g2", [128, 2048])
    wd_t = [_sb(ph9d, nc, f"wdt{i}", [128, 2048], BF16) for i in range(2)]
    xres = [_sb(ph9d, nc, f"xres{i}", [128, 512]) for i in range(2)]
    atres = [_sb(ph9d, nc, f"atres{i}", [128, 512], BF16) for i in range(2)]
    ores = [_sb(ph9d, nc, f"ores{i}", [128, 512]) for i in range(2)]
    otmp = _sb(ph9d, nc, "otmp", [128, 512])

    s_wdf = [mksem(f"s_wdf{i}") for i in range(3)]
    s_xr = [mksem("s_xr0"), mksem("s_xr1")]
    s_ar = [mksem("s_ar0"), mksem("s_ar1")]
    s_ob = [mksem("s_ob0"), mksem("s_ob1")]
    wait("tensor", s_a, s_a.v)
    wait("gpsimd", s_sat[0], tok_att_sp[0])
    wait("gpsimd", s_sat[1], tok_att_sp[1])
    wdf_free = [0, 0, 0]
    wdt_free = [0, 0]
    ores_free = [0, 0]
    res_free = [0, 0]
    for cc in range(16):
        xres_tok = [0, 0]
        atres_tok = [0, 0]
        for th in range(2):
            wait("sync", s_v, res_free[th])
            xres_tok[th] = sdma(
                xres[th][:], dram["x_fm"][128 * cc:128 * (cc + 1),
                                          LV0 + 512 * th:LV0 + 512 * (th + 1)],
                sem=s_xr[th])
            wait("gpsimd", s_v, res_free[th])
            atres_tok[th] = gdma(
                atres[th][:], scr["att"][128 * cc:128 * (cc + 1),
                                         512 * th:512 * (th + 1)], sem=s_ar[th])
        bkth = [pb[2], pb[3]]
        for bx in bkth:
            if bx.free is not None:
                wait("tensor", bx.free[0], bx.free[1])
                bx.free = None
        for q4 in range(4):
            fb = (cc * 4 + q4) % 3
            tb = q4 % 2
            qdma, qeng = (sdma, "sync") if fb % 2 == 0 else (gdma, "gpsimd")
            wait(qeng, s_v, wdf_free[fb])
            qdma(wd_f32[fb][:], dram["wd"][:, cc * 8192 + 2048 * q4:
                                           cc * 8192 + 2048 * (q4 + 1)],
                 sem=s_wdf[fb])
            wait("vector", s_wdf[fb], s_wdf[fb].v)
            wait("vector", s_t, wdt_free[tb])
            vbar()
            v(lambda e, fb=fb: e.tensor_scalar(
                wd_g1[:], wd_f32[fb][:], 0.0, swcol[:, 3:4], ALU.add, ALU.is_gt))
            wdf_free[fb] = vtok(lambda e, fb=fb: e.tensor_scalar(
                wd_g2[:], wd_f32[fb][:], -1.0, swcol[:, 3:4], ALU.mult, ALU.is_gt))
            vbar()
            tok_tern = vtok(lambda e, tb=tb: e.tensor_tensor(
                wd_t[tb][:], wd_g1[:], wd_g2[:], op=ALU.subtract))
            wait("tensor", s_v, tok_tern)
            for fl in range(16):
                f = 16 * q4 + fl
                for th in range(2):
                    src = k2a[f] if th == 0 else k2b[f]
                    st = f == 0
                    sp = f == NF - 1
                    emitfn = (lambda e, fb=tb, fl=fl, th=th, s=src, st=st, sp=sp:
                              e.matmul(bkth[th].ps[:, 0:512],
                                       wd_t[fb][:, 128 * fl:128 * (fl + 1)],
                                       s[:], start=st, stop=sp))
                    if fl == 15 and th == 1:
                        wdt_free[tb] = ttok(emitfn)
                        if sp:
                            bkth[1].prod = wdt_free[tb]
                    elif sp and th == 0:
                        bkth[0].prod = ttok(emitfn)
                    else:
                        t(emitfn)
        for th in range(2):
            wait("vector", s_xr[th], xres_tok[th])
            wait("vector", s_ar[th], atres_tok[th])
            wait("vector", s_ob[th], ores_free[th])
            wait("vector", s_a, tok_r2rep)
            wait("vector", s_t, bkth[th].prod)
            vbar()
            tok1 = vtok(lambda e, th=th: e.tensor_tensor(
                otmp[:], bkth[th].ps[:, 0:512],
                r2rep[:, 512 * th:512 * (th + 1)], op=ALU.mult))
            bkth[th].free = (s_v, tok1)
            vbar()
            v(lambda e, th=th: e.tensor_tensor(otmp[:], otmp[:], xres[th][:],
                                               op=ALU.add))
            vbar()
            tok2 = vtok(lambda e, th=th: e.tensor_tensor(
                ores[th][:], otmp[:], atres[th][:], op=ALU.add))
            res_free[th] = tok2
            wait("sync", s_v, tok2)
            ores_free[th] = sdma(
                out[128 * cc:128 * (cc + 1), 512 * th:512 * (th + 1)],
                ores[th][:], sem=s_ob[th])

    finish()
    return steps


def kernel(**inputs):
    w = _host_weights(inputs)
    x = np.asarray(inputs["L0_attn_h"], np.float32)
    nc = build_nc()
    in_maps = []
    for core in range(8):
        m = dict(w)
        m.update(_host_core(x, core))
        in_maps.append(m)
    res = run_bass_kernel_spmd(nc, in_maps, core_ids=list(range(8)))
    outp = np.empty((B, T, C), np.float32)
    for core in range(8):
        b, half = core // 2, core % 2
        outp[b, half * TL:(half + 1) * TL, :] = res.results[core]["out"].T
    return outp


# version 11 (final)
# speedup vs baseline: 1.0323x; 1.0074x over previous
import numpy as np
import ml_dtypes
from contextlib import ExitStack

import concourse.bass as bass
import concourse.mybir as mybir
from concourse.bass_utils import run_bass_kernel_spmd

F32 = mybir.dt.float32
BF16 = mybir.dt.bfloat16
I32 = mybir.dt.int32
AF = mybir.ActivationFunctionType
ALU = mybir.AluOpType
AX = mybir.AxisListType
BF = ml_dtypes.bfloat16

B, T, C = 4, 2048, 2048
TL = 1024
H, D, M = 16, 128, 16
ROPE, THETA, WIN, CAP = 64, 1e4, 512, 50.0
G, QC, INTER, F2 = 4, 512, 2048, 8192
NCH = C // 128
VIRT = 2560
NVT = VIRT // 128
LV0 = 1536
KV0 = 1024
KVN = VIRT - KV0
NKV = KVN // 128
NBV = VIRT // M
QT = TL // 128
NF = F2 // 128
SCALE = 1.0 / float(np.sqrt(D))
LN_EPS, RMS_EPS, QEPS = 1e-5, 1e-6, 1e-5
NEG = -1e30
TWO_PI = float(2 * np.pi)

JORDER = []
for _i in range(16):
    JORDER += [_i, 16 + _i]


def _host_weights(i):
    w = {}
    w["wqc"] = np.ascontiguousarray(i["L0_attn_Wq_c"], np.float32)
    w["wqup"] = np.ascontiguousarray(i["L0_attn_Wq_up"], np.float32)
    w["wk"] = np.ascontiguousarray(i["L0_attn_Wk"], np.float32)
    w["wv"] = np.ascontiguousarray(i["L0_attn_Wv"], np.float32)
    wc = np.asarray(i["L0_attn_W_comp"], np.float32)
    w["wcomp"] = np.ascontiguousarray(
        wc.reshape(256, 128, 128).transpose(1, 0, 2).reshape(128, -1))
    w["wg"] = np.ascontiguousarray(
        np.asarray(i["L0_attn_Wg"], np.float32).reshape(C, C))
    wo = np.asarray(i["L0_attn_Wo"], np.float32)
    w["wo"] = np.ascontiguousarray(
        wo.reshape(64, 128, 16, 128).transpose(1, 2, 0, 3).reshape(128, -1))
    wgu = np.asarray(i["L0_ffn_w_gate_up"], np.float32)
    t = wgu.reshape(32, 512, 16, 128).transpose(3, 0, 2, 1)
    w["wgu"] = np.ascontiguousarray(t[:, JORDER].reshape(128, -1))
    wd = np.asarray(i["L0_ffn_w_down"], np.float32)
    w["wd"] = np.ascontiguousarray(
        wd.reshape(16, 128, 64, 128).transpose(3, 0, 2, 1).reshape(128, -1))
    w["qnw"] = np.ascontiguousarray(
        np.asarray(i["L0_attn_q_norm_w"], np.float32).reshape(128, 1))
    w["knw"] = np.ascontiguousarray(
        np.asarray(i["L0_attn_k_norm_w"], np.float32).reshape(128, 1))
    w["sink"] = np.ascontiguousarray(
        np.asarray(i["L0_attn_sink"], np.float32).reshape(1, H))
    w["ident"] = np.eye(128, dtype=BF)
    w["identf"] = np.eye(128, dtype=np.float32)
    return w


def _host_core(x, core):
    b, half = core // 2, core % 2
    t0 = half * TL
    pr = LV0 - t0
    d = {}
    xtm = np.zeros((VIRT, C), np.float32)
    n = t0 + TL
    xtm[pr:pr + n] = x[b][0:n]
    d["x_tm"] = xtm
    d["x_fm"] = np.ascontiguousarray(xtm.T)
    d["pos"] = np.arange(t0 - 512, t0 - 512 + KVN,
                         dtype=np.float32).reshape(1, -1)
    msw = np.zeros((128, 2 * 8 * 512), BF)
    for ch in range(2):
        for e in range(8):
            s_g = (t0 - 512 + (4 * ch + e) * 128) + np.arange(128)[:, None]
            t_g = (t0 + 512 * ch) + np.arange(512)[None, :]
            ok = (s_g >= 0) & (s_g <= t_g) & (t_g - s_g < WIN)
            msw[:, (ch * 8 + e) * 512:(ch * 8 + e + 1) * 512] = ok.astype(BF)
    d["msw"] = msw
    mcm = np.empty((128, QT * NBV), np.float32)
    for j in range(QT):
        t_g = (t0 + 128 * j) + np.arange(128)[:, None]
        n = np.arange(NBV)[None, :] - (LV0 - t0) // M
        ok = (n >= 0) & (M * n + (M - 1) < t_g)
        mcm[:, j * NBV:(j + 1) * NBV] = np.where(ok, 0.0, NEG)
    d["mc"] = mcm
    return d


ENGS = ("sync", "tensor", "vector", "scalar", "gpsimd")


class Sem:
    __slots__ = ("h", "v", "name")

    def __init__(self, h, name):
        self.h, self.v, self.name = h, 0, name


class Bank:
    __slots__ = ("ps", "prod", "free")

    def __init__(self, ps):
        self.ps = ps
        self.prod = 0
        self.free = None


def _sb(es, nc, name, shape, dt=F32, side=None):
    return es.enter_context(nc.sbuf_tensor(name, shape, dt, side=side))


def build_nc(upto=9, taps=()):
    nc = bass.Bass("TRN2", target_bir_lowering=False, debug=False)
    dram = {}
    for name, shape, dt in [
        ("x_tm", [VIRT, C], F32), ("x_fm", [C, VIRT], F32),
        ("wqc", [C, QC], F32), ("wqup", [QC, C], F32),
        ("wk", [C, D], F32), ("wv", [C, D], F32),
        ("wcomp", [128, 256 * 128], F32), ("wg", [C, C], F32),
        ("wo", [128, 16 * 64 * 128], F32),
        ("wgu", [128, 32 * 16 * 512], F32), ("wd", [128, 16 * 64 * 128], F32),
        ("qnw", [128, 1], F32), ("knw", [128, 1], F32), ("sink", [1, H], F32),
        ("ident", [128, 128], BF16), ("identf", [128, 128], F32),
        ("pos", [1, KVN], F32),
        ("msw", [128, 2 * 8 * 512], BF16), ("mc", [128, QT * NBV], F32),
    ]:
        dram[name] = nc.dram_tensor(name, shape, dt, kind="ExternalInput").ap()
    out = nc.dram_tensor("out", [C, TL], F32, kind="ExternalOutput").ap()
    tap_map = {}
    for tname, tshape, tdt in taps:
        tap_map[tname] = nc.dram_tensor("tap_" + tname, tshape, tdt,
                                        kind="ExternalOutput").ap()
    scr = {
        "k1": nc.dram_tensor("k1_spill", [C, TL], BF16, kind="Internal").ap(),
        "att": nc.dram_tensor("att_spill", [C, TL], BF16, kind="Internal").ap(),
        "h": nc.dram_tensor("h_spill", [8 * 128, F2], F32, kind="Internal").ap(),
    }
    es = ExitStack()
    with es:
        steps = _program(nc, es, dram, out, scr, tap_map, upto)
        block = es.enter_context(nc.Block())
        for eng in ENGS:
            def body(e, _eng=eng):
                for fn in steps[_eng]:
                    fn(e)
            getattr(block, eng)(body)
    return nc


def _program(nc, es, dram, out, scr, tap_map, upto):
    steps = {e: [] for e in ENGS}
    bg = []

    SEMS = {}

    def mksem(name):
        sm = Sem(es.enter_context(nc.semaphore(name)), name)
        SEMS[name] = sm
        return sm

    s_ld = mksem("s_ld")
    s_x = mksem("s_x")
    s_v = mksem("s_v")
    s_a = mksem("s_a")
    s_t = mksem("s_t")
    s_gp = mksem("s_gp")
    s_pp = mksem("s_pp")
    s_pf = mksem("s_pf")
    s_sp = mksem("s_sp")
    s_out = mksem("s_out")
    SEMOF = {"vector": s_v, "scalar": s_a, "tensor": s_t, "gpsimd": s_gp}

    def add(eng, fn):
        steps[eng].append(fn)

    def wait(eng, sem, val):
        if val > 0:
            add(eng, lambda e, s=sem.h, vv=val: e.wait_ge(s, vv))

    def tok(eng, fn):
        sm = SEMOF[eng]
        sm.v += 1
        add(eng, lambda e, f=fn, s=sm.h: f(e).then_inc(s, 1))
        return sm.v

    def v(fn):
        tok("vector", fn)

    def a(fn):
        tok("scalar", fn)

    def t(fn):
        add("tensor", fn)

    def vtok(fn):
        return tok("vector", fn)

    def atok(fn):
        return tok("scalar", fn)

    def ttok(fn):
        return tok("tensor", fn)

    def vbar():
        wait("vector", s_v, s_v.v)

    def abar():
        wait("scalar", s_a, s_a.v)

    def gdma(dst, src, sem=s_ld):
        sem.v += 16
        add("gpsimd", lambda e, d=dst, s=src, h=sem.h:
            e.dma_start(out=d, in_=s).then_inc(h, 16))
        return sem.v

    def sdma(dst, src, sem=s_x):
        sem.v += 16
        add("sync", lambda e, d=dst, s=src, h=sem.h:
            e.dma_start(out=d, in_=s).then_inc(h, 16))
        return sem.v

    def edma(eng, dst, src, sem):
        sem.v += 16
        add(eng, lambda e, d=dst, s=src, h=sem.h:
            e.dma_start(out=d, in_=s).then_inc(h, 16))
        return sem.v

    def drain_bg(n=1):
        for _ in range(n):
            if bg:
                eng, fn, bump = bg.pop(0)
                add(eng, fn)
                if bump is not None:
                    bump[0].v += bump[1]

    def drain_all_bg():
        while bg:
            drain_bg(1)

    def pe_group(bank, emits):
        if bank.free is not None:
            wait("tensor", bank.free[0], bank.free[1])
            bank.free = None
        n = len(emits)
        for idx, f in enumerate(emits):
            st = idx == 0
            if idx == n - 1:
                bank.prod = ttok(lambda e, ff=f, st=st: ff(e, st, True))
            else:
                t(lambda e, ff=f, st=st: ff(e, st, False))
        return bank.prod

    def consume(bank, eng, fn):
        wait(eng, s_t, bank.prod)
        tk = tok(eng, fn)
        bank.free = (SEMOF[eng], tk)
        return tk

    def fence():
        for eng in ENGS:
            for sm in list(SEMS.values()):
                wait(eng, sm, sm.v)

    def tap(name, ap_fn, sem, tk):
        if name in tap_map:
            wait("sync", sem, tk)
            s_out.v += 16
            add("sync", lambda e, d=tap_map[name], f=ap_fn:
                e.dma_start(out=d, in_=f()).then_inc(s_out.h, 16))

    consts = es.enter_context(ExitStack())
    onesb = _sb(consts, nc, "onesb", [128, 1], BF16)
    onesf_r = _sb(consts, nc, "onesf_r", [1, 128], F32)
    onesf_c = _sb(consts, nc, "onesf_c", [128, 1], F32)
    identb = _sb(consts, nc, "identb", [128, 128], BF16)
    identf = _sb(consts, nc, "identf_s", [128, 128], F32)
    qnw = _sb(consts, nc, "qnw_in", [128, 1], F32)
    knw = _sb(consts, nc, "knw_in", [128, 1], F32)
    qnw_s = _sb(consts, nc, "qnw_s", [128, 1], F32)
    sinkrep = _sb(consts, nc, "sinkrep", [128, H], F32)
    expsink = _sb(consts, nc, "expsink", [1, H], F32)
    sx_g = _sb(consts, nc, "sx_g", [128, QT], F32)
    mxh_g = _sb(consts, nc, "mxh_g", [128, QT], F32)
    r1col = _sb(consts, nc, "r1col", [128, QT], F32)
    sx2col = _sb(consts, nc, "sx2col", [128, QT], F32)
    ppcol = _sb(consts, nc, "ppcol", [128, 192], F32)
    swv = _sb(consts, nc, "swv", [1, 8], F32)
    swcol = _sb(consts, nc, "swcol", [128, 8], F32)
    ppfin = _sb(consts, nc, "ppfin", [128, 8], F32)
    ppfin_i = _sb(consts, nc, "ppfin_i", [128, 2], I32)
    ppsums = _sb(consts, nc, "ppsums", [1, 8], F32)

    ropel = es.enter_context(ExitStack())
    cosq = _sb(ropel, nc, "cosq", [32, KVN], F32)
    sinq = _sb(ropel, nc, "sinq", [32, KVN], F32)
    posr_early = _sb(ropel, nc, "posr_e", [1, KVN], F32)
    glive = es.enter_context(ExitStack())
    xn = [_sb(glive, nc, f"xn{c}", [128, VIRT], BF16) for c in range(NCH)]

    pb = [Bank(es.enter_context(nc.psum_tensor(f"pb{k}", [128, 512], F32)))
          for k in range(7)]
    pbt = Bank(es.enter_context(nc.psum_tensor("pbt", [128, 1024], BF16)))

    s_aux = mksem("s_aux")
    gdma(identb[:], dram["ident"], sem=s_aux)
    gdma(identf[:], dram["identf"], sem=s_aux)
    gdma(qnw[:], dram["qnw"], sem=s_aux)
    gdma(knw[:], dram["knw"], sem=s_aux)
    gdma(sinkrep[0:1, 0:H], dram["sink"], sem=s_aux)
    tok_aux = gdma(posr_early[0:1, :], dram["pos"], sem=s_aux)

    wait("vector", s_aux, tok_aux)
    v(lambda e: e.memset(onesb[:], 1.0))
    v(lambda e: e.memset(onesf_r[:], 1.0))
    v(lambda e: e.memset(onesf_c[:], 1.0))
    tok_vc = vtok(lambda e: e.tensor_scalar(qnw_s[:], qnw[:], SCALE, None, ALU.mult))

    wait("tensor", s_v, tok_vc)
    pe_group(pb[6], [lambda e, st, sp: e.matmul(
        pb[6].ps[:, 0:H], onesf_r[:], sinkrep[0:1, 0:H], start=st, stop=sp)])
    tok_srep2 = consume(pb[6], "vector",
                        lambda e: e.tensor_copy(sinkrep[:], pb[6].ps[:, 0:H]))
    wait("scalar", s_v, tok_srep2)
    atok(lambda e: e.activation(expsink[:], sinkrep[0:1, 0:H], AF.Exp))

    ropes = es.enter_context(ExitStack())
    iot = _sb(ropes, nc, "iot", [32, 1], F32)
    posr = posr_early
    angt = _sb(ropes, nc, "angt", [32, KVN], F32)
    ang2 = _sb(ropes, nc, "ang2", [32, KVN], F32)
    angi = _sb(ropes, nc, "angi", [32, KVN], I32)
    angw = _sb(ropes, nc, "angw", [32, KVN], F32)
    invc = _sb(ropes, nc, "invc", [32, 1], F32)
    tok_iot = tok("gpsimd", lambda e: e.iota(
        iot[:], [[0, 1]], base=0, channel_multiplier=1,
        allow_small_or_imprecise_dtypes=True))
    wait("scalar", s_gp, tok_iot)
    LTH = float(np.log(THETA) / 32.0)
    tok_inv = atok(lambda e: e.activation(invc[:], iot[:], AF.Exp, scale=-LTH))
    wait("tensor", s_aux, tok_aux)
    wait("vector", s_a, tok_inv)
    for off in range(0, KVN, 512):
        pe_group(pb[6], [lambda e, st, sp, o=off: e.matmul(
            pb[6].ps[0:32, 0:512], onesf_r[0:1, 0:32], posr[0:1, o:o + 512],
            start=st, stop=sp)])
        consume(pb[6], "vector", lambda e, o=off: e.tensor_scalar(
            angt[:, o:o + 512], pb[6].ps[0:32, 0:512], invc[:, 0:1], None, ALU.mult))

    def range_reduce_sin(dst, shift):
        vbar()
        off = 0.25 if shift else 0.0
        v(lambda e, o=off: e.tensor_scalar(ang2[:], angt[:], 1.0 / TWO_PI, o,
                                           ALU.mult, ALU.add))
        vbar()
        v(lambda e: e.tensor_copy(angi[:], ang2[:]))
        vbar()
        v(lambda e: e.tensor_copy(angw[:], angi[:]))
        vbar()
        v(lambda e: e.tensor_tensor(ang2[:], ang2[:], angw[:], op=ALU.subtract))
        vbar()
        v(lambda e: e.tensor_scalar(angw[:], ang2[:], 0.0, 0.5,
                                    ALU.add, ALU.is_gt))
        vbar()
        v(lambda e: e.tensor_tensor(ang2[:], ang2[:], angw[:], op=ALU.subtract))
        vbar()
        v(lambda e: e.tensor_scalar(angw[:], ang2[:], -1.0, 0.5,
                                    ALU.mult, ALU.is_gt))
        vbar()
        tk = vtok(lambda e: e.tensor_tensor(ang2[:], ang2[:], angw[:],
                                            op=ALU.add))
        wait("scalar", s_v, tk)
        tk2 = atok(lambda e, d=dst: e.activation(d, ang2[:], AF.Sin,
                                                 scale=TWO_PI))
        wait("vector", s_a, tk2)

    range_reduce_sin(sinq[:], False)
    range_reduce_sin(cosq[:], True)
    ropes.close()
    fence()

    ppes = es.enter_context(ExitStack())
    s_ppv = [mksem("s_pp0"), mksem("s_pp1"), mksem("s_pp2")]
    pp_buf = [_sb(ppes, nc, f"pp{i}", [128, 2048], F32, side="right") for i in range(3)]
    pp_specs = [("wgu", k) for k in range(128)] + [("wd", k) for k in range(64)]
    pp_cnt = [0, 0, 0]
    for idx, (wname, k) in enumerate(pp_specs):
        bi = idx % 3
        if idx >= 3:
            bg.append(("gpsimd", lambda e, n=(idx - 2): e.wait_ge(s_pf.h, n),
                       None))
        pp_cnt[bi] += 16
        tk = pp_cnt[bi]
        bg.append(("gpsimd", lambda e, w=wname, kk=k, b=bi:
                   e.dma_start(out=pp_buf[b][:],
                               in_=dram[w][:, kk * 2048:(kk + 1) * 2048])
                   .then_inc(s_ppv[b].h, 16), (s_ppv[bi], 16)))
        bg.append(("vector", lambda e, tt_=tk, b=bi: e.wait_ge(s_ppv[b].h, tt_),
                   None))
        bg.append(("vector", lambda e, b=bi, cix=idx: e.reduce_sum(
            ppcol[:, cix:cix + 1], pp_buf[b][:], axis=AX.X,
            apply_absolute_value=True).then_inc(s_pf.h, 1), (s_pf, 1)))

    ph1 = es.enter_context(ExitStack())
    xt_buf = [_sb(ph1, nc, f"xt{i}", [128, C]) for i in range(3)]
    sum_g = _sb(ph1, nc, "sum_g", [128, NVT])
    sq_g = _sb(ph1, nc, "sq_g", [128, NVT])
    m_g = _sb(ph1, nc, "m_g", [128, NVT])
    nm_g = _sb(ph1, nc, "nm_g", [128, NVT])
    rs_g = _sb(ph1, nc, "rs_g", [128, NVT])
    var_g = _sb(ph1, nc, "var_g", [128, NVT])
    mx_g = _sb(ph1, nc, "mx_g", [128, QT])
    dump = _sb(ph1, nc, "dump", [128, C])
    dump_tok = [0]

    v(lambda e: e.memset(sum_g[:], 0.0))
    tok_z = vtok(lambda e: e.memset(sq_g[:], 0.0))
    wait("scalar", s_v, tok_z)
    s_xt = [mksem(f"s_xt{i}") for i in range(3)]
    buf_free = [0, 0, 0]
    for i in range(NVT):
        bi = i % 3
        wait("sync", s_a, buf_free[bi])
        sdma(xt_buf[bi][:], dram["x_tm"][128 * i:128 * (i + 1), :], sem=s_xt[bi])
        wait("scalar", s_xt[bi], s_xt[bi].v)
        wait("scalar", s_a, dump_tok[0])
        tok_sm = atok(lambda e, b=bi, i=i: e.activation(
            dump[:], xt_buf[b][:], AF.Copy, accum_out=sum_g[:, i:i + 1]))
        dump_tok[0] = tok_sm
        wait("vector", s_a, tok_sm)
        v(lambda e, i=i: e.tensor_scalar(m_g[:, i:i + 1], sum_g[:, i:i + 1],
                                         1.0 / C, None, ALU.mult))
        vbar()
        tok_nm = vtok(lambda e, i=i: e.tensor_scalar(
            nm_g[:, i:i + 1], m_g[:, i:i + 1], -1.0, None, ALU.mult))
        wait("scalar", s_v, tok_nm)
        wait("scalar", s_a, dump_tok[0])
        tok_stats = atok(lambda e, b=bi, i=i: e.activation(
            dump[:], xt_buf[b][:], AF.Square, bias=nm_g[:, i:i + 1], scale=1.0,
            accum_out=sq_g[:, i:i + 1]))
        dump_tok[0] = tok_stats
        buf_free[bi] = tok_stats
        drain_bg(2)

    wait("vector", s_a, tok_stats)
    v(lambda e: e.tensor_scalar(var_g[:], sq_g[:], 1.0 / C, None, ALU.mult))
    vbar()
    tok_var = vtok(lambda e: e.tensor_scalar(var_g[:], var_g[:], LN_EPS,
                                             None, ALU.add))
    wait("scalar", s_v, tok_var)
    tok_sqr = atok(lambda e: e.activation(rs_g[:], var_g[:], AF.Sqrt))
    wait("vector", s_a, tok_sqr)
    tok_rs = vtok(lambda e: e.reciprocal(rs_g[:], rs_g[:]))

    wait("scalar", s_v, tok_rs)
    for j in range(QT):
        i = NVT - QT + j
        bi = j % 3
        wait("sync", s_a, buf_free[bi])
        sdma(xt_buf[bi][:], dram["x_tm"][128 * i:128 * (i + 1), :], sem=s_xt[bi])
        wait("scalar", s_xt[bi], s_xt[bi].v)
        wait("scalar", s_a, dump_tok[0])
        tok_abs = atok(lambda e, b=bi, i=i: e.activation(
            dump[:], xt_buf[b][:], AF.Abs, bias=nm_g[:, i:i + 1], scale=1.0))
        dump_tok[0] = tok_abs
        buf_free[bi] = tok_abs
        wait("vector", s_a, tok_abs)
        tok_mx = vtok(lambda e, j=j: e.reduce_max(mx_g[:, j:j + 1], dump[:],
                                                  axis=AX.X))
        wait("scalar", s_v, tok_mx)
        drain_bg(2)
    vbar()
    v(lambda e: e.tensor_tensor(sx_g[:], mx_g[:], rs_g[:, NVT - QT:NVT],
                                op=ALU.mult))
    vbar()
    v(lambda e: e.tensor_scalar(sx_g[:], sx_g[:], QEPS, None, ALU.max))
    vbar()
    v(lambda e: e.reciprocal(sx_g[:], sx_g[:]))
    vbar()
    tok_sx = vtok(lambda e: e.tensor_scalar(sx_g[:], sx_g[:], 127.0, None,
                                            ALU.mult))

    sxl = es.enter_context(ExitStack())
    sxrow = _sb(sxl, nc, "sxrow", [1, TL], F32, side="right")
    reps = es.enter_context(ExitStack())
    m_rep = _sb(reps, nc, "m_rep", [128, VIRT], F32, side="right")
    rs_rep = _sb(reps, nc, "rs_rep", [128, VIRT], F32, side="right")
    packs = es.enter_context(ExitStack())
    pk_sb = _sb(packs, nc, "pk_sb", [NVT, 128], F32, side="right")
    mrow = _sb(packs, nc, "mrow", [1, VIRT], F32, side="right")
    rsrow = _sb(packs, nc, "rsrow", [1, VIRT], F32, side="right")

    s_pk = mksem("s_pk")
    pack_state = {"last_dma": 0}

    def pack_row(grid_ap, n, row_dst, gate_tok, gate_sem):
        wait("tensor", gate_sem, gate_tok)
        wait("scalar", s_pk, pack_state["last_dma"])
        pe_group(pb[6], [lambda e, st, sp, gap=grid_ap, n=n: e.transpose(
            pb[6].ps[0:n, 0:128], gap, identf[:])])
        tok_ev = consume(pb[6], "scalar",
                         lambda e, n=n: e.copy(pk_sb[0:n, :], pb[6].ps[0:n, 0:128]))
        wait("sync", s_a, tok_ev)
        pack_state["last_dma"] = sdma(row_dst, pk_sb[0:n, :], sem=s_pk)
        return pack_state["last_dma"]

    tap("sumg", lambda: sum_g[:], s_a, tok_stats)
    tap("sqg", lambda: sq_g[:], s_v, tok_rs)
    tap("mg", lambda: m_g[:], s_v, tok_rs)
    tap("rsg", lambda: rs_g[:], s_v, tok_rs)
    tok_mrow = pack_row(m_g[:], NVT, mrow[:], tok_rs, s_v)
    tok_rsrow = pack_row(rs_g[:], NVT, rsrow[:], tok_rs, s_v)
    tok_sxrow = pack_row(sx_g[:], QT, sxrow[:], tok_sx, s_v)
    ph1.close()

    tap("mrow", lambda: mrow[:], s_pk, tok_mrow)
    tap("rsrow", lambda: rsrow[:], s_pk, tok_rsrow)
    tap("sxrow", lambda: sxrow[:], s_pk, tok_sxrow)

    tok_rep_ev = 0
    for row_sb, rep_sb, gate in ((mrow, m_rep, tok_mrow), (rsrow, rs_rep, tok_rsrow)):
        wait("tensor", s_pk, gate)
        for off in range(0, VIRT, 512):
            pe_group(pb[6], [lambda e, st, sp, r=row_sb, o=off: e.matmul(
                pb[6].ps[:, 0:512], onesf_r[:], r[0:1, o:o + 512],
                start=st, stop=sp)])
            tok_rep_ev = consume(pb[6], "scalar", lambda e, d=rep_sb, o=off:
                                 e.copy(d[:, o:o + 512], pb[6].ps[:, 0:512]))

    packs.close()

    fence()
    ph2 = es.enter_context(ExitStack())
    xf_buf = [_sb(ph2, nc, f"xf{i}", [128, VIRT]) for i in range(3)]
    xn_tmp = _sb(ph2, nc, "xn_tmp", [128, VIRT])
    xnf_loc = _sb(ph2, nc, "xnf_loc", [128, TL])
    sx_rep = _sb(ph2, nc, "sx_rep", [128, TL])
    k1q = _sb(ph2, nc, "k1q", [128, TL], I32)
    k1 = [_sb(ph2, nc, f"k1_{c}", [128, TL], BF16) for c in range(2)]
    wait("tensor", s_pk, tok_sxrow)
    for off in range(0, TL, 512):
        pe_group(pb[6], [lambda e, st, sp, o=off: e.matmul(
            pb[6].ps[:, 0:512], onesf_r[:], sxrow[0:1, o:o + 512],
            start=st, stop=sp)])
        tok_sxr = consume(pb[6], "scalar", lambda e, o=off: e.copy(
            sx_rep[:, o:o + 512], pb[6].ps[:, 0:512]))
    wait("vector", s_a, tok_rep_ev)
    wait("vector", s_a, tok_sxr)
    s_xf = [mksem(f"s_xf{i}") for i in range(3)]
    s_sk = [mksem("s_sk0"), mksem("s_sk1")]
    tok_xn = [0] * NCH
    tok_k1 = [0] * NCH
    k1_spill_tok = [0] * NCH
    for cdx in range(NCH):
        bi = cdx % 3
        qdma, qeng = (sdma, "sync") if bi != 1 else (gdma, "gpsimd")
        if cdx >= 3:
            wait(qeng, s_v, tok_xn[cdx - 3])
        qdma(xf_buf[bi][:], dram["x_fm"][128 * cdx:128 * (cdx + 1), :], sem=s_xf[bi])
        wait("vector", s_xf[bi], s_xf[bi].v)
        v(lambda e, b=bi: e.tensor_tensor(xn_tmp[:], xf_buf[b][:], m_rep[:],
                                          op=ALU.subtract))
        vbar()
        tok_xn[cdx] = vtok(lambda e, c=cdx: e.tensor_tensor(
            xn[c][:], xn_tmp[:], rs_rep[:], op=ALU.mult))
        v(lambda e: e.tensor_tensor(xnf_loc[:], xn_tmp[:, LV0:VIRT],
                                    rs_rep[:, LV0:VIRT], op=ALU.mult))
        vbar()
        v(lambda e: e.tensor_tensor(k1q[:], xnf_loc[:], sx_rep[:], op=ALU.mult))
        vbar()
        sk = cdx % 2
        if cdx >= 2:
            wait("vector", s_sk[sk], k1_spill_tok[cdx - 2])
        tok_k1[cdx] = vtok(lambda e, c=sk: e.tensor_scalar(
            k1[c][:], k1q[:], -128.0, 127.0, ALU.max, ALU.min))
        wait("gpsimd", s_v, tok_k1[cdx])
        k1_spill_tok[cdx] = gdma(scr["k1"][128 * cdx:128 * (cdx + 1), :],
                                 k1[sk][:], sem=s_sk[sk])
        drain_bg(4)
    tok_xn_all = tok_xn[-1]
    tok_k1spill = (s_sk[0].v, s_sk[1].v)
    tap("xn0", lambda: xn[0][:], s_v, tok_xn_all)
    tap("k10", lambda: k1[0][:], s_v, tok_k1[0])

    ph2.close()
    reps.close()
    sxl.close()

    def finish():
        drain_all_bg()
        for sname, sm in list(SEMS.items()):
            if sm.v > 0 and sname.startswith(("s_ob", "s_sk", "s_sat", "s_sh",
                                              "s_out", "s_pp")):
                wait("sync", sm, sm.v)

    if upto <= 1:
        finish()
        return steps

    fence()
    qkv = es.enter_context(ExitStack())
    q_bf = [_sb(qkv, nc, f"q{h}", [128, TL], BF16, side="right") for h in range(H)]
    kvl = es.enter_context(ExitStack())
    k_bf = _sb(kvl, nc, "k_bf", [128, KVN], BF16, side="right")
    v_tm = [_sb(kvl, nc, f"vtm{k}", [128, 128], BF16, side="right") for k in range(NKV)]
    ckv = _sb(kvl, nc, "ckv", [128, NBV], BF16, side="right")
    ckt1 = _sb(kvl, nc, "ckt1", [128, 128], BF16, side="right")
    ckt2 = _sb(kvl, nc, "ckt2", [32, 128], BF16, side="right")

    ph4s = es.enter_context(ExitStack())
    qc_sb = [_sb(ph4s, nc, f"qcs{c}", [128, TL], BF16) for c in range(4)]
    pst = _sb(ph4s, nc, "pst", [128, 512])
    psq = _sb(ph4s, nc, "psq", [128, 512])
    rt = [_sb(ph4s, nc, f"rt{i}", [32, 512]) for i in range(5)]
    rrow_sb = _sb(ph4s, nc, "rrow_sb", [1, 512])

    def proj_span_norm(bk_mm, ems, dst_ap, nw_ap, cos_ap, sin_ap, do_rope):
        wait("scalar", s_v, s_v.v)
        pe_group(bk_mm, ems)
        tok_ev = consume(bk_mm, "scalar", lambda e: e.copy(pst[:], bk_mm.ps[:, 0:512]))
        wait("vector", s_a, tok_ev)
        if do_rope:
            v(lambda e: e.tensor_copy(rt[4][:], pst[32:64, :]))
            vbar()
            v(lambda e: e.tensor_tensor(rt[0][:], pst[0:32, :], cos_ap, op=ALU.mult))
            v(lambda e: e.tensor_tensor(rt[1][:], rt[4][:], sin_ap, op=ALU.mult))
            v(lambda e: e.tensor_tensor(rt[2][:], pst[0:32, :], sin_ap, op=ALU.mult))
            v(lambda e: e.tensor_tensor(rt[3][:], rt[4][:], cos_ap, op=ALU.mult))
            vbar()
            v(lambda e: e.tensor_tensor(pst[0:32, :], rt[0][:], rt[1][:],
                                        op=ALU.subtract))
            v(lambda e: e.tensor_tensor(rt[4][:], rt[2][:], rt[3][:],
                                        op=ALU.add))
            vbar()
            tok_pre = vtok(lambda e: e.tensor_copy(pst[32:64, :], rt[4][:]))
            wait("scalar", s_v, tok_pre)
        tok_sq = atok(lambda e: e.activation(psq[:], pst[:], AF.Square))
        wait("tensor", s_a, tok_sq)
        bkr = pb[4]
        if bkr.free is not None:
            wait("tensor", bkr.free[0], bkr.free[1])
            bkr.free = None
        pe_group(bkr, [lambda e, st, sp: e.matmul(
            bkr.ps[0:1, 0:512], onesf_c[:], psq[:], start=st, stop=sp)])
        tok_rr = consume(bkr, "vector", lambda e: e.tensor_scalar(
            rrow_sb[:], bkr.ps[0:1, 0:512], 1.0 / D, RMS_EPS, ALU.mult, ALU.add))
        wait("scalar", s_v, tok_rr)
        tok_rsq = atok(lambda e: e.activation(rrow_sb[:], rrow_sb[:], AF.Sqrt))
        wait("vector", s_a, tok_rsq)
        tok_rrc = vtok(lambda e: e.reciprocal(rrow_sb[:], rrow_sb[:]))
        wait("tensor", s_v, tok_rrc)
        bkp = pb[5]
        if bkp.free is not None:
            wait("tensor", bkp.free[0], bkp.free[1])
            bkp.free = None
        pe_group(bkp, [lambda e, st, sp: e.matmul(
            bkp.ps[:, 0:512], onesf_r[:], rrow_sb[:], start=st, stop=sp)])
        return consume(bkp, "vector", lambda e, d=dst_ap: e.scalar_tensor_tensor(
            d, pst[:], nw_ap, bkp.ps[:, 0:512], ALU.mult, ALU.mult))

    ph4a = es.enter_context(ExitStack())
    wqc_sb = [_sb(ph4a, nc, f"wqc{c}", [128, QC], BF16) for c in range(NCH)]
    s_wqc = mksem("s_wqc")
    for c in range(NCH):
        gdma(wqc_sb[c][:], dram["wqc"][128 * c:128 * (c + 1), :], sem=s_wqc)
    tok_wqc = s_wqc.v
    wait("tensor", s_wqc, tok_wqc)
    wait("tensor", s_v, tok_xn_all)
    for oc in range(4):
        for th in range(2):
            bk = pb[(2 * oc + th) % 2]
            ems = []
            for c in range(NCH):
                ems.append(lambda e, st, sp, bkk=bk, c=c, oc=oc, th=th: e.matmul(
                    bkk.ps[:, 0:512], wqc_sb[c][:, 128 * oc:128 * (oc + 1)],
                    xn[c][:, LV0 + 512 * th:LV0 + 512 * (th + 1)],
                    start=st, stop=sp))
            pe_group(bk, ems)
            consume(bk, "scalar", lambda e, bkk=bk, oc=oc, th=th: e.copy(
                qc_sb[oc][:, 512 * th:512 * (th + 1)], bkk.ps[:, 0:512]))
        drain_bg(1)
    ph4a.close()
    fence()

    ph4b = es.enter_context(ExitStack())
    wqup_sb = [_sb(ph4b, nc, f"wqup{c}", [128, C], BF16) for c in range(4)]
    s_wqup = mksem("s_wqup")
    for c in range(4):
        gdma(wqup_sb[c][:], dram["wqup"][128 * c:128 * (c + 1), :], sem=s_wqup)
    tok_wqup = s_wqup.v
    wait("tensor", s_wqup, tok_wqup)
    COSQ0 = 512
    tok_q = [0] * H
    for h in range(H):
        for th in range(2):
            bk = pb[th]
            ems = []
            for c4 in range(4):
                ems.append(lambda e, st, sp, bkk=bk, c4=c4, h=h, th=th: e.matmul(
                    bkk.ps[:, 0:512], wqup_sb[c4][:, 128 * h:128 * (h + 1)],
                    qc_sb[c4][:, 512 * th:512 * (th + 1)], start=st, stop=sp))
            o0 = COSQ0 + 512 * th
            tok_q[h] = proj_span_norm(
                bk, ems, q_bf[h][:, 512 * th:512 * (th + 1)], qnw_s[:, 0:1],
                cosq[:, o0:o0 + 512], sinq[:, o0:o0 + 512], True)
        drain_bg(1)
    tap("q0", lambda: q_bf[0][:], s_v, tok_q[0])
    ph4b.close()
    fence()

    ph4c = es.enter_context(ExitStack())
    wk_sb = [_sb(ph4c, nc, f"wks{c}", [128, D], BF16) for c in range(NCH)]
    s_wk = mksem("s_wk")
    for c in range(NCH):
        gdma(wk_sb[c][:], dram["wk"][128 * c:128 * (c + 1), :], sem=s_wk)
    tok_wk = s_wk.v
    wait("tensor", s_wk, tok_wk)
    tok_kbf = 0
    for th in range(3):
        bk = pb[th % 2]
        ems = []
        for c in range(NCH):
            ems.append(lambda e, st, sp, bkk=bk, c=c, th=th: e.matmul(
                bkk.ps[:, 0:512], wk_sb[c][:],
                xn[c][:, KV0 + 512 * th:KV0 + 512 * (th + 1)], start=st, stop=sp))
        o0 = 512 * th
        tok_kbf = proj_span_norm(
            bk, ems, k_bf[:, 512 * th:512 * (th + 1)], knw[:, 0:1],
            cosq[:, o0:o0 + 512], sinq[:, o0:o0 + 512], True)
        drain_bg(1)
    tap("kbf", lambda: k_bf[:], s_v, tok_kbf)
    ph4c.close()
    fence()

    ph4d = es.enter_context(ExitStack())
    wv_sb = [_sb(ph4d, nc, f"wvs{c}", [128, D], BF16) for c in range(NCH)]
    v_fm = _sb(ph4d, nc, "v_fm", [128, KVN], BF16)
    s_wv = mksem("s_wv")
    for c in range(NCH):
        gdma(wv_sb[c][:], dram["wv"][128 * c:128 * (c + 1), :], sem=s_wv)
    tok_wv = s_wv.v
    wait("tensor", s_wv, tok_wv)
    for th in range(3):
        bk = pb[th % 2]
        ems = []
        for c in range(NCH):
            ems.append(lambda e, st, sp, bkk=bk, c=c, th=th: e.matmul(
                bkk.ps[:, 0:512], wv_sb[c][:],
                xn[c][:, KV0 + 512 * th:KV0 + 512 * (th + 1)], start=st, stop=sp))
        pe_group(bk, ems)
        consume(bk, "scalar", lambda e, bkk=bk, th=th: e.copy(
            v_fm[:, 512 * th:512 * (th + 1)], bkk.ps[:, 0:512]))
    wait("tensor", s_a, s_a.v)
    tok_vtm = [0] * NKV
    for kt in range(NKV):
        half = (kt % 2) * 128
        if pbt.free is not None:
            wait("tensor", pbt.free[0], pbt.free[1])
            pbt.free = None
        pbt.prod = ttok(lambda e, kt=kt, hf=half: e.transpose(
            pbt.ps[:, hf:hf + 128], v_fm[:, 128 * kt:128 * (kt + 1)], identb[:]))
        tok_vtm[kt] = consume(pbt, "scalar", lambda e, kt=kt, hf=half: e.copy(
            v_tm[kt][:], pbt.ps[:, hf:hf + 128]))
        drain_bg(1)
    ph4d.close()
    fence()

    ph4e = es.enter_context(ExitStack())
    wcp_buf = [_sb(ph4e, nc, f"wcp{i}", [128, 4096], BF16) for i in range(2)]
    ckst = _sb(ph4e, nc, "ckst", [128, NBV])
    cksq = _sb(ph4e, nc, "cksq", [128, NBV])
    ckrow = _sb(ph4e, nc, "ckrow", [1, NBV])
    bk = pb[6]
    if bk.free is not None:
        wait("tensor", bk.free[0], bk.free[1])
        bk.free = None
    s_wcp = [mksem("s_wcp0"), mksem("s_wcp1")]
    tok_wcp = [0] * 8
    mm_wcp = [0] * 8
    tok_wcp[0] = gdma(wcp_buf[0][:], dram["wcomp"][:, 0:4096], sem=s_wcp[0])
    for kk in range(8):
        if kk < 7:
            if kk >= 1:
                wait("gpsimd", s_t, mm_wcp[kk - 1])
            tok_wcp[kk + 1] = gdma(wcp_buf[(kk + 1) % 2][:],
                                   dram["wcomp"][:, 4096 * (kk + 1):4096 * (kk + 2)],
                                   sem=s_wcp[(kk + 1) % 2])
        wait("tensor", s_wcp[kk % 2], tok_wcp[kk])
        for fl in range(32):
            f = 32 * kk + fl
            m_, cdx = f // 16, f % 16
            xn_str = xn[cdx][:, m_:VIRT:M]
            if fl == 31:
                mm_wcp[kk] = ttok(lambda e, b=kk % 2, fl=fl, xs=xn_str,
                                  st=(f == 0), sp=(f == 255): e.matmul(
                    bk.ps[:, 0:NBV], wcp_buf[b][:, 128 * fl:128 * (fl + 1)],
                    xs, start=st, stop=sp))
            else:
                t(lambda e, b=kk % 2, fl=fl, xs=xn_str, st=(f == 0): e.matmul(
                    bk.ps[:, 0:NBV], wcp_buf[b][:, 128 * fl:128 * (fl + 1)],
                    xs, start=st, stop=False))
        drain_bg(1)
    bk.prod = mm_wcp[7]
    tok_ck0 = consume(bk, "scalar", lambda e: e.copy(ckst[:], bk.ps[:, 0:NBV]))
    wait("scalar", s_a, tok_ck0)
    tok_cksq = atok(lambda e: e.activation(cksq[:], ckst[:], AF.Square))
    wait("tensor", s_a, tok_cksq)
    bkr = pb[4]
    if bkr.free is not None:
        wait("tensor", bkr.free[0], bkr.free[1])
        bkr.free = None
    pe_group(bkr, [lambda e, st, sp: e.matmul(
        bkr.ps[0:1, 0:NBV], onesf_c[:], cksq[:], start=st, stop=sp)])
    tok_rr = consume(bkr, "vector", lambda e: e.tensor_scalar(
        ckrow[:], bkr.ps[0:1, 0:NBV], 1.0 / D, RMS_EPS, ALU.mult, ALU.add))
    wait("scalar", s_v, tok_rr)
    tok_rsq = atok(lambda e: e.activation(ckrow[:], ckrow[:], AF.Sqrt))
    wait("vector", s_a, tok_rsq)
    tok_rrc = vtok(lambda e: e.reciprocal(ckrow[:], ckrow[:]))
    wait("tensor", s_v, tok_rrc)
    bkp = pb[5]
    if bkp.free is not None:
        wait("tensor", bkp.free[0], bkp.free[1])
        bkp.free = None
    pe_group(bkp, [lambda e, st, sp: e.matmul(
        bkp.ps[:, 0:NBV], onesf_r[:], ckrow[:], start=st, stop=sp)])
    tok_ckv = consume(bkp, "vector", lambda e: e.scalar_tensor_tensor(
        ckv[:], ckst[:], knw[:, 0:1], bkp.ps[:, 0:NBV], ALU.mult, ALU.mult))
    wait("tensor", s_v, tok_ckv)
    if pbt.free is not None:
        wait("tensor", pbt.free[0], pbt.free[1])
        pbt.free = None
    t(lambda e: e.transpose(pbt.ps[:, 0:128], ckv[:, 0:128], identb[:]))
    pbt.prod = ttok(lambda e: e.transpose(pbt.ps[0:32, 128:256], ckv[:, 128:160],
                                          identb[:]))
    wait("scalar", s_t, pbt.prod)
    a(lambda e: e.copy(ckt1[:], pbt.ps[:, 0:128]))
    tok_ckt2 = atok(lambda e: e.copy(ckt2[:], pbt.ps[0:32, 128:256]))
    pbt.free = (s_a, tok_ckt2)
    tap("ckv", lambda: ckv[:], s_v, tok_ckv)
    ph4e.close()
    ph4s.close()
    glive.close()
    ropel.close()
    if upto <= 2:
        finish()
        return steps

    fence()
    attcons = es.enter_context(ExitStack())
    s_msk = mksem("s_msk")
    msw = _sb(attcons, nc, "msw_s", [128, 2 * 8 * 512], BF16)
    mc = _sb(attcons, nc, "mc_s", [128, QT * NBV], F32)
    gdma(msw[:], dram["msw"], sem=s_msk)
    tok_masks = gdma(mc[:], dram["mc"], sem=s_msk)
    wait("vector", s_msk, tok_masks)

    ao = q_bf
    ph5 = es.enter_context(ExitStack())
    csc = [_sb(ph5, nc, f"csc{i}", [128, NBV]) for i in range(2)]
    pexp = [_sb(ph5, nc, f"pexp{i}", [128, NBV]) for i in range(2)]
    pn_sb = [_sb(ph5, nc, f"pn{i}", [128, NBV], BF16) for i in range(2)]
    ptA = [_sb(ph5, nc, f"ptA{i}", [128, 128], BF16) for i in range(2)]
    ptB = [_sb(ph5, nc, f"ptB{i}", [32, 128], BF16) for i in range(2)]
    mxc = _sb(ph5, nc, "mxc", [128, 4])
    u_sb = [_sb(ph5, nc, f"usb{i}", [128, 512]) for i in range(2)]
    ex_sb = [_sb(ph5, nc, f"exsb{i}", [128, 512], BF16) for i in range(2)]
    drow = _sb(ph5, nc, "drow", [1, 512])
    rrep = _sb(ph5, nc, "rrep", [128, 512])
    swtmp = _sb(ph5, nc, "swtmp", [128, 512])
    ao_c = [_sb(ph5, nc, f"aoc{i}", [128, 512]) for i in range(2)]

    wait("tensor", s_a, s_a.v)
    comp_gate = [0, 0]
    aoc_free = [0, 0]
    for h in range(H):
        for ch in range(2):
            for jj in range(4):
                j = 4 * ch + jj
                par = jj % 2
                bk2 = pb[2]
                pe_group(bk2, [lambda e, st, sp, h=h, j=j: e.matmul(
                    bk2.ps[:, 0:NBV], q_bf[h][:, 128 * j:128 * (j + 1)],
                    ckv[:], start=st, stop=sp)])
                wait("vector", s_t, comp_gate[par])
                tok_csc = consume(bk2, "vector", lambda e, p=par, j=j: e.tensor_tensor(
                    csc[p][:], bk2.ps[:, 0:NBV], mc[:, NBV * j:NBV * (j + 1)],
                    op=ALU.add))
                vbar()
                v(lambda e, p=par: e.reduce_max(mxc[:, 0:1], csc[p][:], axis=AX.X))
                vbar()
                v(lambda e, h=h: e.tensor_tensor(mxc[:, 0:1], mxc[:, 0:1],
                                                 sinkrep[:, h:h + 1], op=ALU.max))
                vbar()
                tok_mx = vtok(lambda e: e.tensor_scalar(mxc[:, 1:2], mxc[:, 0:1],
                                                        -1.0, None, ALU.mult))
                wait("scalar", s_v, tok_mx)
                tok_pexp = atok(lambda e, p=par: e.activation(
                    pexp[p][:], csc[p][:], AF.Exp, bias=mxc[:, 1:2]))
                a(lambda e, h=h: e.activation(mxc[:, 2:3], mxc[:, 0:1], AF.Exp,
                                              bias=sinkrep[:, h:h + 1], scale=-1.0))
                wait("vector", s_a, s_a.v)
                v(lambda e, p=par: e.reduce_sum(mxc[:, 3:4], pexp[p][:], axis=AX.X))
                vbar()
                v(lambda e: e.tensor_tensor(mxc[:, 3:4], mxc[:, 3:4], mxc[:, 2:3],
                                            op=ALU.add))
                vbar()
                v(lambda e: e.reciprocal(mxc[:, 3:4], mxc[:, 3:4]))
                vbar()
                tok_pn = vtok(lambda e, p=par: e.tensor_scalar(
                    pn_sb[p][:], pexp[p][:], mxc[:, 3:4], None, ALU.mult))
                wait("tensor", s_v, tok_pn)
                if pbt.free is not None:
                    wait("tensor", pbt.free[0], pbt.free[1])
                    pbt.free = None
                t(lambda e, p=par: e.transpose(pbt.ps[:, 0:128], pn_sb[p][:, 0:128],
                                               identb[:]))
                pbt.prod = ttok(lambda e, p=par: e.transpose(
                    pbt.ps[0:32, 128:256], pn_sb[p][:, 128:160], identb[:]))
                wait("scalar", s_t, pbt.prod)
                a(lambda e, p=par: e.copy(ptA[p][:], pbt.ps[:, 0:128]))
                tok_pt = atok(lambda e, p=par: e.copy(ptB[p][:], pbt.ps[0:32, 128:256]))
                pbt.free = (s_a, tok_pt)
                bk3 = pb[3]
                if jj == 0 and bk3.free is not None:
                    wait("tensor", bk3.free[0], bk3.free[1])
                    bk3.free = None
                wait("tensor", s_a, tok_pt)
                t(lambda e, p=par, jj=jj: e.matmul(
                    bk3.ps[:, 128 * jj:128 * (jj + 1)], ckt1[:], ptA[p][:],
                    start=True, stop=False))
                comp_gate[par] = ttok(lambda e, p=par, jj=jj: e.matmul(
                    bk3.ps[:, 128 * jj:128 * (jj + 1)], ckt2[:], ptB[p][:],
                    start=False, stop=True))
                bk3.prod = comp_gate[par]
            cpar = (2 * h + ch) % 2
            wait("scalar", s_t, bk3.prod)
            wait("scalar", s_v, aoc_free[cpar])
            tok_evc = atok(lambda e, cp=cpar: e.copy(ao_c[cp][:],
                                                     bk3.ps[:, 0:512]))
            bk3.free = (s_a, tok_evc)
            bk4, bk5 = pb[4], pb[5]
            if bk4.free is not None:
                wait("tensor", bk4.free[0], bk4.free[1])
                bk4.free = None
            if bk5.free is not None:
                wait("tensor", bk5.free[0], bk5.free[1])
                bk5.free = None
            tanh_tok = [0] * 8
            mask_tok = [0] * 8
            for e_i in range(8):
                kt = 4 * ch + e_i
                par = e_i % 2
                bks = pb[par]
                if e_i >= 2:
                    wait("tensor", s_a, tanh_tok[e_i - 2])
                elif bks.free is not None:
                    wait("tensor", bks.free[0], bks.free[1])
                    bks.free = None
                bks.prod = ttok(lambda e, h=h, ch=ch, kt=kt, p=par: e.matmul(
                    pb[p].ps[:, 0:512], k_bf[:, 128 * kt:128 * (kt + 1)],
                    q_bf[h][:, 512 * ch:512 * (ch + 1)], start=True, stop=True))
                wait("scalar", s_t, bks.prod)
                if e_i >= 2:
                    wait("scalar", s_t, s_t.v)
                tanh_tok[e_i] = atok(lambda e, p=par: e.activation(
                    u_sb[p][:], pb[p].ps[:, 0:512], AF.Tanh, scale=1.0 / CAP))
                abar()
                tok_ex = atok(lambda e, p=par: e.activation(
                    ex_sb[p][:], u_sb[p][:], AF.Exp, scale=CAP))
                wait("vector", s_a, tok_ex)
                mask_tok[e_i] = vtok(lambda e, p=par, ch=ch, e_i=e_i: e.tensor_tensor(
                    ex_sb[p][:], ex_sb[p][:],
                    msw[:, (ch * 8 + e_i) * 512:(ch * 8 + e_i + 1) * 512],
                    op=ALU.mult))
                wait("tensor", s_v, mask_tok[e_i])
                t(lambda e, p=par, e_i=e_i: e.matmul(
                    bk4.ps[0:1, 0:512], onesb[:], ex_sb[p][:],
                    start=(e_i == 0), stop=(e_i == 7)))
                last_pv = ttok(lambda e, p=par, e_i=e_i, kt=kt: e.matmul(
                    bk5.ps[:, 0:512], v_tm[kt][:], ex_sb[p][:],
                    start=(e_i == 0), stop=(e_i == 7)))
            bk4.prod = last_pv
            bk5.prod = last_pv
            pb[0].free = (s_a, tanh_tok[6])
            pb[1].free = (s_a, tanh_tok[7])
            wait("vector", s_t, last_pv)
            v(lambda e, h=h: e.tensor_scalar(drow[:], bk4.ps[0:1, 0:512],
                                             expsink[0:1, h:h + 1], None, ALU.add))
            vbar()
            tok_dr = vtok(lambda e: e.reciprocal(drow[:], drow[:]))
            bk4.free = (s_v, tok_dr)
            wait("tensor", s_v, tok_dr)
            bk6 = pb[6]
            if bk6.free is not None:
                wait("tensor", bk6.free[0], bk6.free[1])
                bk6.free = None
            pe_group(bk6, [lambda e, st, sp: e.matmul(
                bk6.ps[:, 0:512], onesf_r[:], drow[:], start=st, stop=sp)])
            tok_rrep = consume(bk6, "scalar", lambda e: e.copy(rrep[:],
                                                               bk6.ps[:, 0:512]))
            wait("vector", s_a, tok_rrep)
            tok_sw = vtok(lambda e: e.tensor_tensor(swtmp[:], bk5.ps[:, 0:512],
                                                    rrep[:], op=ALU.mult))
            bk5.free = (s_v, tok_sw)
            vbar()
            wait("vector", s_a, tok_evc)
            tok_ao = vtok(lambda e, h=h, ch=ch, cp=cpar: e.tensor_tensor(
                ao[h][:, 512 * ch:512 * (ch + 1)], swtmp[:], ao_c[cp][:],
                op=ALU.add))
            aoc_free[cpar] = tok_ao
            drain_bg(8)
    tok_ao_all = tok_ao
    tap("ao0", lambda: ao[0][:], s_v, tok_ao_all)

    ph5.close()
    attcons.close()
    kvl.close()

    if upto <= 3:
        finish()
        return steps

    fence()
    ph6l = es.enter_context(ExitStack())
    inter = [_sb(ph6l, nc, f"it{k}", [128, TL], BF16) for k in range(64)]
    ph6 = es.enter_context(ExitStack())
    s_wg = mksem("s_wg")
    wg_sb = [_sb(ph6, nc, f"wgsb{i}", [128, C], BF16) for i in range(4)]
    wg_mm_done = 0
    for g_ in range(4):
        wait("gpsimd", s_t, wg_mm_done)
        tok_wg = 0
        for ic in range(4):
            tok_wg = gdma(wg_sb[ic][:],
                          dram["wg"][128 * (4 * g_ + ic):128 * (4 * g_ + ic + 1), :],
                          sem=s_wg)
        wait("tensor", s_wg, tok_wg)
        for oc in range(16):
            for th in range(2):
                bk = pb[(2 * oc + th) % 2]
                ems = []
                for ic in range(4):
                    ems.append(lambda e, st, sp, ic=ic, g_=g_, oc=oc, th=th,
                               bkk=bk: e.matmul(
                        bkk.ps[:, 0:512], wg_sb[ic][:, 128 * oc:128 * (oc + 1)],
                        ao[4 * g_ + ic][:, 512 * th:512 * (th + 1)], start=st, stop=sp))
                pe_group(bk, ems)
                consume(bk, "scalar", lambda e, bkk=bk, g_=g_, oc=oc, th=th: e.copy(
                    inter[16 * g_ + oc][:, 512 * th:512 * (th + 1)], bkk.ps[:, 0:512]))
            drain_bg(4)
        wg_mm_done = s_t.v
    tap("inter0", lambda: inter[0][:], s_a, s_a.v)

    ph6.close()
    qkv.close()

    fence()
    ph7 = es.enter_context(ExitStack())
    wo_sb = [_sb(ph7, nc, f"wosb{i}", [128, 8192], BF16) for i in range(2)]
    att_st = [_sb(ph7, nc, f"attst{i}", [128, 512], BF16) for i in range(2)]
    s_wo = [mksem("s_wo0"), mksem("s_wo1")]
    s_sat = [mksem("s_sat0"), mksem("s_sat1")]
    tok_wo = [0] * 16
    tok_wo[0] = gdma(wo_sb[0][:], dram["wo"][:, 0:8192], sem=s_wo[0])
    tok_wo[1] = gdma(wo_sb[1][:], dram["wo"][:, 8192:16384], sem=s_wo[1])
    att_st_free = [0, 0]
    wo_mm_done = [0] * 16
    for cc in range(16):
        slot = cc % 2
        wait("tensor", s_wo[slot], tok_wo[cc])
        for th in range(2):
            bk = pb[2 + th]
            ems = []
            for k in range(64):
                ems.append(lambda e, st, sp, s_=slot, k=k, th=th, bkk=bk: e.matmul(
                    bkk.ps[:, 0:512], wo_sb[s_][:, 128 * k:128 * (k + 1)],
                    inter[k][:, 512 * th:512 * (th + 1)], start=st, stop=sp))
            pe_group(bk, ems)
            stp = (2 * cc + th) % 2
            wait("scalar", s_sat[stp], att_st_free[stp])
            tok_ev = consume(bk, "scalar", lambda e, bkk=bk, p=stp: e.copy(
                att_st[p][:], bkk.ps[:, 0:512]))
            wait("gpsimd", s_a, tok_ev)
            att_st_free[stp] = gdma(
                scr["att"][128 * cc:128 * (cc + 1), 512 * th:512 * (th + 1)],
                att_st[stp][:], sem=s_sat[stp])
        wo_mm_done[cc] = s_t.v
        if cc + 2 < 16:
            wait("gpsimd", s_t, wo_mm_done[cc])
            tok_wo[cc + 2] = gdma(wo_sb[slot][:],
                                  dram["wo"][:, 8192 * (cc + 2):8192 * (cc + 3)],
                                  sem=s_wo[slot])
        drain_bg(8)
    tok_att_sp = (s_sat[0].v, s_sat[1].v)
    if "att0" in tap_map:
        wait("sync", s_sat[0], tok_att_sp[0])
        wait("sync", s_sat[1], tok_att_sp[1])
        s_out.v += 16
        add("sync", lambda e: e.dma_start(out=tap_map["att0"],
                                          in_=scr["att"][0:128, :])
            .then_inc(s_out.h, 16))

    ph7.close()
    ph6l.close()

    if upto <= 4:
        finish()
        return steps

    drain_all_bg()
    wait("vector", s_pf, s_pf.v)
    v(lambda e: e.reduce_sum(swcol[:, 0:1], ppcol[:, 0:128], axis=AX.X))
    tok_ws = vtok(lambda e: e.reduce_sum(swcol[:, 1:2], ppcol[:, 128:192], axis=AX.X))
    vbar()
    wait("tensor", s_v, tok_ws)
    v(lambda e: e.reduce_sum(ppfin[:, 0:1], ppcol[:, 0:128], axis=AX.X))
    tok_cs = vtok(lambda e: e.reduce_sum(ppfin[:, 1:2], ppcol[:, 128:192],
                                         axis=AX.X))
    vbar()
    v(lambda e: e.tensor_scalar(ppfin_i[:, 0:2], ppfin[:, 0:2], 1.0 / 64.0,
                                None, ALU.mult))
    vbar()
    v(lambda e: e.tensor_copy(ppfin[:, 2:4], ppfin_i[:, 0:2]))
    vbar()
    tok_hilo = vtok(lambda e: e.scalar_tensor_tensor(
        ppfin[:, 4:6], ppfin[:, 2:4], -64.0, ppfin[:, 0:2], ALU.mult, ALU.add))
    wait("tensor", s_v, tok_hilo)
    bk6 = pb[6]
    pe_group(bk6, [
        lambda e, st, sp: e.matmul(bk6.ps[0:1, 0:2], onesf_c[:],
                                   ppfin[:, 2:4], start=True, stop=True),
        lambda e, st, sp: e.matmul(bk6.ps[0:1, 2:4], onesf_c[:],
                                   ppfin[:, 4:6], start=True, stop=True),
    ])
    tok_sums = consume(bk6, "vector", lambda e: e.tensor_copy(
        ppsums[0:1, 0:4], bk6.ps[0:1, 0:4]))
    vbar()
    v(lambda e: e.scalar_tensor_tensor(ppsums[0:1, 4:6], ppsums[0:1, 0:2],
                                       64.0, ppsums[0:1, 2:4], ALU.mult, ALU.add))
    vbar()
    v(lambda e: e.memset(swv[:], 0.0))
    vbar()
    v(lambda e: e.tensor_scalar(swv[0:1, 0:1], ppsums[0:1, 4:5],
                                1.0 / (2.0 * F2 * C), None, ALU.mult))
    v(lambda e: e.tensor_scalar(swv[0:1, 2:3], ppsums[0:1, 5:6],
                                1.0 / (C * F2), None, ALU.mult))
    vbar()
    v(lambda e: e.tensor_scalar(swv[0:1, 0:1], swv[0:1, 0:1], QEPS, None, ALU.max))
    v(lambda e: e.tensor_scalar(swv[0:1, 2:3], swv[0:1, 2:3], QEPS, None, ALU.max))
    vbar()
    v(lambda e: e.reciprocal(swv[0:1, 1:2], swv[0:1, 0:1]))
    v(lambda e: e.tensor_scalar(swv[0:1, 3:4], swv[0:1, 2:3], 0.5,
                                float(1.0 + 2.0 ** -24), ALU.mult, ALU.mult))
    vbar()
    tok_swv = vtok(lambda e: e.tensor_scalar(swv[0:1, 4:5], swv[0:1, 3:4],
                                             -1.0, None, ALU.mult))
    wait("tensor", s_v, tok_swv)
    pe_group(bk6, [lambda e, st, sp: e.matmul(
        bk6.ps[:, 0:8], onesf_r[:], swv[0:1, 0:8], start=st, stop=sp)])
    tok_swc = consume(bk6, "vector", lambda e: e.tensor_copy(swcol[:, 0:8],
                                                             bk6.ps[:, 0:8]))
    vbar()
    v(lambda e: e.reciprocal(r1col[:], sx_g[:]))
    vbar()
    tok_r1 = vtok(lambda e: e.tensor_scalar(r1col[:], r1col[:], swcol[:, 0:1],
                                            None, ALU.mult))
    tap("swv", lambda: swv[:], s_v, tok_swv)
    tap("ppfin", lambda: ppfin[:], s_v, tok_swv)
    ppes.close()

    fence()
    ph8 = es.enter_context(ExitStack())
    k1r = [_sb(ph8, nc, f"k1r{c}", [128, TL], BF16) for c in range(NCH)]
    wgu_f32 = [_sb(ph8, nc, f"wguf{i}", [128, 2048]) for i in range(4)]
    wgu_i = _sb(ph8, nc, "wgu_i", [128, 2048], I32)
    wgu_t = [_sb(ph8, nc, f"wgut{i}", [128, 8192], BF16) for i in range(2)]
    a_st = [_sb(ph8, nc, f"ast{i}", [128, 512], F32) for i in range(QT)]
    h_st = [_sb(ph8, nc, f"hst{i}", [128, 512], F32) for i in range(2)]
    hmx = _sb(ph8, nc, "hmx", [128, QT])

    s_k1r = mksem("s_k1r")
    s_wgf = [mksem(f"s_wgf{i}") for i in range(4)]
    s_sh = [mksem("s_sh0"), mksem("s_sh1")]
    wait("gpsimd", s_sk[0], tok_k1spill[0])
    wait("gpsimd", s_sk[1], tok_k1spill[1])
    tok_k1r = [0] * NCH
    for c in range(NCH):
        tok_k1r[c] = gdma(k1r[c][:], scr["k1"][128 * c:128 * (c + 1), :],
                          sem=s_k1r)
    v(lambda e: e.memset(mxh_g[:], 0.0))
    wait("tensor", s_k1r, tok_k1r[-1])
    wait("tensor", s_v, tok_r1)

    GROUPS = [([0, 1, 2, 3], [0, 1, 2, 3]),
              ([4, 5, 6], [4, 5, 6]),
              ([7], [0])]
    tern_tok = [0, 0]
    mm_done_tok = [0, 0]
    h_st_free = [0, 0]
    f32_free = [0, 0, 0, 0]
    silu_tok = [0] * QT
    a_st_free = [0] * QT
    for jj in range(32):
        par = jj % 2
        is_gate = jj % 2 == 0
        pair = jj // 2
        wait("vector", s_t, mm_done_tok[par])
        for q4 in range(4):
            fb = q4
            qdma, qeng = (sdma, "sync") if fb % 2 == 0 else (gdma, "gpsimd")
            wait(qeng, s_v, f32_free[fb])
            qdma(wgu_f32[fb][:], dram["wgu"][:, jj * 8192 + 2048 * q4:
                                             jj * 8192 + 2048 * (q4 + 1)],
                 sem=s_wgf[fb])
            wait("vector", s_wgf[fb], s_wgf[fb].v)
            if jj == 0 and q4 == 0:
                wait("vector", s_v, tok_swc)
            vbar()
            f32_free[fb] = vtok(lambda e, fb=fb: e.tensor_scalar(
                wgu_i[:], wgu_f32[fb][:], swcol[:, 1:2], None, ALU.mult))
            vbar()
            tern_tok[par] = vtok(lambda e, p=par, q4=q4: e.tensor_scalar(
                wgu_t[p][:, 2048 * q4:2048 * (q4 + 1)], wgu_i[:],
                -1.0, 1.0, ALU.max, ALU.min))
        wait("tensor", s_v, tern_tok[par])
        for tts, bankidx in GROUPS:
            bset = [pb[bx] for bx in bankidx]
            for bx in bset:
                if bx.free is not None:
                    wait("tensor", bx.free[0], bx.free[1])
                    bx.free = None
            for cc in range(NCH):
                for ti, tt in enumerate(tts):
                    bkx = bset[ti]
                    last = (cc == NCH - 1) and (ti == len(tts) - 1)
                    emitfn = (lambda e, bkk=bkx, c=cc, tt=tt, p=par,
                              st=(cc == 0), sp=(cc == NCH - 1): e.matmul(
                        bkk.ps[:, 0:512], k1r[c][:, 128 * tt:128 * (tt + 1)],
                        wgu_t[p][:, 512 * c:512 * (c + 1)], start=st, stop=sp))
                    if last:
                        mm_done_tok[par] = ttok(emitfn)
                        for bx in bset:
                            bx.prod = mm_done_tok[par]
                    else:
                        t(emitfn)
            for ti, tt in enumerate(tts):
                bkx = bset[ti]
                if is_gate:
                    wait("scalar", s_v, a_st_free[tt])
                    silu_tok[tt] = consume(bkx, "scalar", lambda e, bkk=bkx, tt=tt:
                                           e.activation(a_st[tt][:],
                                                        bkk.ps[:, 0:512], AF.Silu,
                                                        scale=r1col[:, tt:tt + 1]))
                else:
                    sp_ = tt % 2
                    wait("vector", s_a, silu_tok[tt])
                    wait("vector", s_sh[sp_], h_st_free[sp_])
                    a_st_free[tt] = consume(bkx, "vector",
                                            lambda e, bkk=bkx, tt=tt, p=sp_:
                                            e.scalar_tensor_tensor(
                        h_st[p][:], bkk.ps[:, 0:512], r1col[:, tt:tt + 1],
                        a_st[tt][:], ALU.mult, ALU.mult))
                    vbar()
                    v(lambda e, tt=tt, p=sp_: e.reduce_max(
                        hmx[:, tt:tt + 1], h_st[p][:], axis=AX.X,
                        apply_absolute_value=True))
                    vbar()
                    tok_hmx = vtok(lambda e, tt=tt: e.tensor_tensor(
                        mxh_g[:, tt:tt + 1], mxh_g[:, tt:tt + 1],
                        hmx[:, tt:tt + 1], op=ALU.max))
                    wait("scalar", s_v, tok_hmx)
                    h_st_free[sp_] = edma(
                        "scalar",
                        scr["h"][128 * tt:128 * (tt + 1),
                                 512 * pair:512 * (pair + 1)],
                        h_st[sp_][:], sem=s_sh[sp_])
    tok_h_sp = (s_sh[0].v, s_sh[1].v)

    ph8.close()

    if upto <= 5:
        finish()
        return steps

    fence()
    ph9l = es.enter_context(ExitStack())
    sx2row = _sb(ph9l, nc, "sx2row", [1, TL], F32, side="right")
    r2row = _sb(ph9l, nc, "r2row", [1, TL], F32, side="right")
    r2rep = _sb(ph9l, nc, "r2rep", [128, TL], F32, side="right")
    k2b = [_sb(ph9l, nc, f"k2b{f}", [128, 512], BF16, side="right") for f in range(NF)]

    vbar()
    v(lambda e: e.tensor_scalar(sx2col[:], mxh_g[:], QEPS, None, ALU.max))
    vbar()
    v(lambda e: e.reciprocal(sx2col[:], sx2col[:]))
    vbar()
    tok_sx2 = vtok(lambda e: e.tensor_scalar(sx2col[:], sx2col[:], 127.0,
                                             None, ALU.mult))
    pk2 = es.enter_context(ExitStack())
    pk2_sb = _sb(pk2, nc, "pk2_sb", [QT, 128], F32, side="right")
    wait("tensor", s_v, tok_sx2)
    bk6 = pb[6]
    if bk6.free is not None:
        wait("tensor", bk6.free[0], bk6.free[1])
        bk6.free = None
    pe_group(bk6, [lambda e, st, sp: e.transpose(bk6.ps[0:QT, 0:128], sx2col[:],
                                                 identf[:])])
    tok_pk2 = consume(bk6, "scalar", lambda e: e.copy(pk2_sb[:],
                                                      bk6.ps[0:QT, 0:128]))
    wait("sync", s_a, tok_pk2)
    tok_sx2row = sdma(sx2row[:], pk2_sb[:], sem=s_pk)
    wait("vector", s_pk, tok_sx2row)
    v(lambda e: e.reciprocal(r2row[:], sx2row[:]))
    vbar()
    tok_r2 = vtok(lambda e: e.tensor_scalar(r2row[:], r2row[:], swv[0:1, 2:3],
                                            None, ALU.mult))
    wait("tensor", s_v, tok_r2)
    tok_r2rep = 0
    for off in range(0, TL, 512):
        pe_group(bk6, [lambda e, st, sp, o=off: e.matmul(
            bk6.ps[:, 0:512], onesf_r[:], r2row[0:1, o:o + 512],
            start=st, stop=sp)])
        tok_r2rep = consume(bk6, "scalar", lambda e, o=off: e.copy(
            r2rep[:, o:o + 512], bk6.ps[:, 0:512]))
    pk2.close()

    k2a = [_sb(ph9l, nc, f"k2a{f}", [128, 512], BF16, side="right") for f in range(NF)]
    ph9q = es.enter_context(ExitStack())
    h_q = [_sb(ph9q, nc, f"h_q{i}", [128, 2048]) for i in range(2)]
    hq_i = _sb(ph9q, nc, "hq_i", [128, 2048], I32)
    k2tm = _sb(ph9q, nc, "k2tm", [128, 2048], BF16)

    s_hq = [mksem("s_hq0"), mksem("s_hq1")]
    wait("gpsimd", s_sh[0], tok_h_sp[0])
    wait("gpsimd", s_sh[1], tok_h_sp[1])
    hq_free = [0, 0]
    for tt in range(8):
        dst_list = k2a if tt < 4 else k2b
        dcol = 128 * (tt % 4)
        for q4 in range(4):
            hb = q4 % 2
            qdma, qeng = (gdma, "gpsimd") if hb == 0 else (sdma, "sync")
            wait(qeng, s_v, hq_free[hb])
            tok_hr = qdma(h_q[hb][:], scr["h"][128 * tt:128 * (tt + 1),
                                               2048 * q4:2048 * (q4 + 1)],
                          sem=s_hq[hb])
            wait("vector", s_hq[hb], tok_hr)
            hq_free[hb] = vtok(lambda e, hb=hb, tt=tt: e.tensor_scalar(
                hq_i[:], h_q[hb][:], sx2col[:, tt:tt + 1], None, ALU.mult))
            vbar()
            tok_q2 = vtok(lambda e: e.tensor_scalar(
                k2tm[:], hq_i[:], -128.0, 127.0, ALU.max, ALU.min))
            wait("tensor", s_v, tok_q2)
            for fl in range(16):
                f = 16 * q4 + fl
                hf = (fl % 2) * 128
                if pbt.free is not None:
                    wait("tensor", pbt.free[0], pbt.free[1])
                    pbt.free = None
                pbt.prod = ttok(lambda e, fl=fl, hf=hf: e.transpose(
                    pbt.ps[:, hf:hf + 128], k2tm[:, 128 * fl:128 * (fl + 1)],
                    identb[:]))
                consume(pbt, "scalar", lambda e, f=f, hf=hf, dl=dst_list,
                        dc=dcol: e.copy(dl[f][:, dc:dc + 128],
                                        pbt.ps[:, hf:hf + 128]))
            wait("vector", s_a, s_a.v)
    ph9q.close()

    fence()
    ph9d = es.enter_context(ExitStack())
    wd_f32 = [_sb(ph9d, nc, f"wdf{i}", [128, 2048]) for i in range(3)]
    wd_g1 = _sb(ph9d, nc, "wd_g1", [128, 2048])
    wd_g2 = _sb(ph9d, nc, "wd_g2", [128, 2048])
    wd_t = [_sb(ph9d, nc, f"wdt{i}", [128, 2048], BF16) for i in range(2)]
    xres = [_sb(ph9d, nc, f"xres{i}", [128, 512]) for i in range(2)]
    atres = [_sb(ph9d, nc, f"atres{i}", [128, 512], BF16) for i in range(2)]
    ores = [_sb(ph9d, nc, f"ores{i}", [128, 512]) for i in range(2)]
    otmp = _sb(ph9d, nc, "otmp", [128, 512])

    s_wdf = [mksem(f"s_wdf{i}") for i in range(3)]
    s_xr = [mksem("s_xr0"), mksem("s_xr1")]
    s_ar = [mksem("s_ar0"), mksem("s_ar1")]
    s_ob = [mksem("s_ob0"), mksem("s_ob1")]
    wait("tensor", s_a, s_a.v)
    wait("gpsimd", s_sat[0], tok_att_sp[0])
    wait("gpsimd", s_sat[1], tok_att_sp[1])
    wdf_free = [0, 0, 0]
    wdt_free = [0, 0]
    ores_free = [0, 0]
    res_free = [0, 0]
    for cc in range(16):
        xres_tok = [0, 0]
        atres_tok = [0, 0]
        for th in range(2):
            wait("sync", s_v, res_free[th])
            xres_tok[th] = sdma(
                xres[th][:], dram["x_fm"][128 * cc:128 * (cc + 1),
                                          LV0 + 512 * th:LV0 + 512 * (th + 1)],
                sem=s_xr[th])
            wait("gpsimd", s_v, res_free[th])
            atres_tok[th] = gdma(
                atres[th][:], scr["att"][128 * cc:128 * (cc + 1),
                                         512 * th:512 * (th + 1)], sem=s_ar[th])
        bkth = [pb[2], pb[3]]
        for bx in bkth:
            if bx.free is not None:
                wait("tensor", bx.free[0], bx.free[1])
                bx.free = None
        for q4 in range(4):
            fb = (cc * 4 + q4) % 3
            tb = q4 % 2
            qdma, qeng = (sdma, "sync") if fb % 2 == 0 else (gdma, "gpsimd")
            wait(qeng, s_v, wdf_free[fb])
            qdma(wd_f32[fb][:], dram["wd"][:, cc * 8192 + 2048 * q4:
                                           cc * 8192 + 2048 * (q4 + 1)],
                 sem=s_wdf[fb])
            wait("vector", s_wdf[fb], s_wdf[fb].v)
            wait("vector", s_t, wdt_free[tb])
            vbar()
            v(lambda e, fb=fb: e.tensor_scalar(
                wd_g1[:], wd_f32[fb][:], 0.0, swcol[:, 3:4], ALU.add, ALU.is_gt))
            wdf_free[fb] = vtok(lambda e, fb=fb: e.tensor_scalar(
                wd_g2[:], wd_f32[fb][:], -1.0, swcol[:, 3:4], ALU.mult, ALU.is_gt))
            vbar()
            tok_tern = vtok(lambda e, tb=tb: e.tensor_tensor(
                wd_t[tb][:], wd_g1[:], wd_g2[:], op=ALU.subtract))
            wait("tensor", s_v, tok_tern)
            for fl in range(16):
                f = 16 * q4 + fl
                for th in range(2):
                    src = k2a[f] if th == 0 else k2b[f]
                    st = f == 0
                    sp = f == NF - 1
                    emitfn = (lambda e, fb=tb, fl=fl, th=th, s=src, st=st, sp=sp:
                              e.matmul(bkth[th].ps[:, 0:512],
                                       wd_t[fb][:, 128 * fl:128 * (fl + 1)],
                                       s[:], start=st, stop=sp))
                    if fl == 15 and th == 1:
                        wdt_free[tb] = ttok(emitfn)
                        if sp:
                            bkth[1].prod = wdt_free[tb]
                    elif sp and th == 0:
                        bkth[0].prod = ttok(emitfn)
                    else:
                        t(emitfn)
        for th in range(2):
            wait("vector", s_xr[th], xres_tok[th])
            wait("vector", s_ar[th], atres_tok[th])
            wait("vector", s_ob[th], ores_free[th])
            wait("vector", s_a, tok_r2rep)
            wait("vector", s_t, bkth[th].prod)
            vbar()
            tok1 = vtok(lambda e, th=th: e.tensor_tensor(
                otmp[:], bkth[th].ps[:, 0:512],
                r2rep[:, 512 * th:512 * (th + 1)], op=ALU.mult))
            bkth[th].free = (s_v, tok1)
            vbar()
            v(lambda e, th=th: e.tensor_tensor(otmp[:], otmp[:], xres[th][:],
                                               op=ALU.add))
            vbar()
            tok2 = vtok(lambda e, th=th: e.tensor_tensor(
                ores[th][:], otmp[:], atres[th][:], op=ALU.add))
            res_free[th] = tok2
            wait("sync", s_v, tok2)
            ores_free[th] = sdma(
                out[128 * cc:128 * (cc + 1), 512 * th:512 * (th + 1)],
                ores[th][:], sem=s_ob[th])

    finish()
    return steps


def kernel(**inputs):
    w = _host_weights(inputs)
    x = np.asarray(inputs["L0_attn_h"], np.float32)
    nc = build_nc()
    in_maps = []
    for core in range(8):
        m = dict(w)
        m.update(_host_core(x, core))
        in_maps.append(m)
    res = run_bass_kernel_spmd(nc, in_maps, core_ids=list(range(8)))
    outp = np.empty((B, T, C), np.float32)
    for core in range(8):
        b, half = core // 2, core % 2
        outp[b, half * TL:(half + 1) * TL, :] = res.results[core]["out"].T
    return outp
